# version 18 (final)
# baseline (speedup 1.0000x reference)
import contextlib
import numpy as np
import ml_dtypes
import concourse.bass as bass
import concourse.mybir as mybir
from concourse.bass_utils import run_bass_kernel_spmd

F32 = mybir.dt.float32
BF16 = mybir.dt.bfloat16
AF = mybir.ActivationFunctionType
ALU = mybir.AluOpType
AX = mybir.AxisListType

MSTAB = 64.0
RG8 = [[0, 1, 2, 3, 4, 5, 6, 7]]


def _cfg(T, D, H, F1):
    c = dict(T=T, D=D, H=H, F1=F1)
    assert D == H * 64 and T % 512 == 0
    c["CH"] = T // 4
    c["OWNT"] = 2 * c["CH"]
    c["QW"] = min(512, c["CH"])
    c["TW"] = min(512, c["CH"])
    c["NDC"] = D // 128
    c["NF1C"] = F1 // 128
    c["NPAIR"] = H // 2
    c["chunks"] = {0: [0, 3], 1: [1, 2]}
    ntpg = c["QW"] // 128
    c["ntpg"] = ntpg
    slots = {}
    for p in (0, 1):
        sl = []
        for ch in c["chunks"][p]:
            t0 = ch * c["CH"] // 128
            for g in range((c["CH"] // 128) // ntpg):
                sl.append([t0 + g * ntpg + j for j in range(ntpg)])
        slots[p] = sl
    c["slots"] = slots
    c["NSLOT"] = len(slots[0])
    c["EPROF"] = [max(slots[0][s][-1], slots[1][s][-1]) + 1 for s in range(c["NSLOT"])]
    c["NPASS"] = sum(c["EPROF"])
    c["MINEF"] = [min(slots[0][s][-1], slots[1][s][-1]) + 1 - ntpg for s in range(c["NSLOT"])]
    c["MOFF"] = []
    acc = 0
    for s in range(c["NSLOT"]):
        c["MOFF"].append(acc)
        acc += c["EPROF"][s] - c["MINEF"][s]
    c["NMASK"] = acc
    return c


def _make_masks(c, p):
    QW = c["QW"]
    masks = np.zeros((c["NMASK"], 128, QW), dtype=np.float32)
    idx = 0
    for s in range(c["NSLOT"]):
        gtiles = c["slots"][p][s]
        for kt in range(c["MINEF"][s], c["EPROF"][s]):
            m = masks[idx]
            for j, qt in enumerate(gtiles):
                if kt < qt:
                    m[:, j * 128:(j + 1) * 128] = 1.0
                elif kt == qt:
                    kk = np.arange(128)[:, None]
                    qq = np.arange(128)[None, :]
                    m[:, j * 128:(j + 1) * 128] = (kk <= qq).astype(np.float32)
            idx += 1
    return masks


class Prog:
    ENGSEM = {"pe": "e_pe", "act": "e_act", "dve": "e_dve", "pool": "e_pool"}

    def __init__(self, nc, es):
        self.nc, self.es = nc, es
        self.h, self.cnt, self.sat = {}, {}, {}
        self.writers, self.users = {}, {}

    def sem(self, name):
        self.h[name] = self.es.enter_context(self.nc.semaphore(name))
        self.cnt[name] = 0

    def wait(self, eng, ename, sname, val):
        if val <= 0:
            return
        key = (ename, sname)
        if self.sat.get(key, 0) >= val:
            return
        eng.wait_ge(self.h[sname], val)
        self.sat[key] = val

    def op(self, eng, ename, reads, writes, emit_fn, sem=None, inc=1):
        for r in reads:
            for s, v in self.writers.get(r, {}).items():
                self.wait(eng, ename, s, v)
        for w in writes:
            for s, v in self.users.get(w, {}).items():
                self.wait(eng, ename, s, v)
        sname = sem or self.ENGSEM[ename]
        if sem is not None:
            self.wait(eng, ename, sname, self.cnt[sname])
        inst = emit_fn(eng)
        inst.then_inc(self.h[sname], inc)
        self.cnt[sname] += inc
        v = self.cnt[sname]
        for r in reads:
            self.users.setdefault(r, {})[sname] = v
        for w in writes:
            self.writers.setdefault(w, {})[sname] = v
            self.users.setdefault(w, {})[sname] = v
        return v

    def reset_sat(self):
        self.sat = {}


def build_nc(T=4096, D=1024, H=16, F1=4096):
    c = _cfg(T, D, H, F1)
    CH, OWNT, QW, TW = c["CH"], c["OWNT"], c["QW"], c["TW"]
    NDC, NF1C, NPAIR, NSLOT = c["NDC"], c["NF1C"], c["NPAIR"], c["NSLOT"]
    EPROF = c["EPROF"]
    NGT, NOT = T // 128, OWNT // 128

    nc = bass.Bass("TRN2", target_bir_lowering=False)

    xb = nc.dram_tensor("xb", [T, D], F32, kind="ExternalInput").ap()
    xbo = nc.dram_tensor("xbo", [OWNT, D], F32, kind="ExternalInput").ap()
    wattn = nc.dram_tensor("wattn", [3 * D, D], F32, kind="ExternalInput").ap()
    wproj = nc.dram_tensor("wproj", [D, D], F32, kind="ExternalInput").ap()
    w1 = nc.dram_tensor("w1", [F1, D], F32, kind="ExternalInput").ap()
    b1 = nc.dram_tensor("b1", [F1], F32, kind="ExternalInput").ap()
    w2 = nc.dram_tensor("w2", [D, F1], F32, kind="ExternalInput").ap()
    b2 = nc.dram_tensor("b2", [D], F32, kind="ExternalInput").ap()
    maskin = nc.dram_tensor("maskin", [max(1, c["NMASK"]), 128, QW], BF16, kind="ExternalInput").ap()
    out = nc.dram_tensor("out", [OWNT, D], F32, kind="ExternalOutput").ap()

    xnTg_hi = nc.dram_tensor("xnTg_hi", [D, T], BF16, kind="Internal").ap()
    xnTg_lo = nc.dram_tensor("xnTg_lo", [D, T], BF16, kind="Internal").ap()
    xnTo_hi = nc.dram_tensor("xnTo_hi", [D, OWNT], BF16, kind="Internal").ap()
    xnTo_lo = nc.dram_tensor("xnTo_lo", [D, OWNT], BF16, kind="Internal").ap()
    hscr = nc.dram_tensor("hscr", [F1, OWNT], F32, kind="Internal").ap()
    yscr_hi = nc.dram_tensor("yscr_hi", [D, OWNT], BF16, kind="Internal").ap()
    yscr_lo = nc.dram_tensor("yscr_lo", [D, OWNT], BF16, kind="Internal").ap()
    rbd = nc.dram_tensor("rbd", [H * NSLOT, QW], F32, kind="Internal").ap()
    w2qTd = nc.dram_tensor("w2qTd", [D // 128, 128, (F1 // 128) * 128], BF16, kind="Internal").ap()
    scd = nc.dram_tensor("scd", [8], F32, kind="Internal").ap()
    cc_in = nc.dram_tensor("cc_in", [1, 1], F32, kind="Internal").ap()
    cc_out = nc.dram_tensor("cc_out", [1, 1], F32, kind="Internal").ap()
    cc_in2 = nc.dram_tensor("cc_in2", [1, 1], F32, kind="Internal").ap()
    cc_out2 = nc.dram_tensor("cc_out2", [1, 1], F32, kind="Internal").ap()

    ges = contextlib.ExitStack()
    P = Prog(nc, ges)
    for s in ["qx0", "qx1", "qx2", "qx3", "qm0", "qm1", "qw0", "qw1", "qy0", "qy1",
              "qm2", "qm3", "qb", "gw_xnt", "gw_h", "gw_y", "gcc", "vr", "rr0", "oc",
              "e_pe", "e_act", "e_dve", "e_pool"]:
        P.sem(s)
    QX = ["qx0", "qx1", "qx2", "qx3"]
    QWSEM = ["qw0", "qw1"]

    idn32 = nc.alloc_sbuf_tensor("idn32", [128, 128], F32)
    idn16 = nc.alloc_sbuf_tensor("idn16", [128, 128], BF16)
    ones32 = nc.alloc_sbuf_tensor("ones32", [128, 128], F32)
    c_one = nc.alloc_sbuf_tensor("c_one", [128, 1], F32)
    c_lneps = nc.alloc_sbuf_tensor("c_lneps", [128, 1], F32)
    c_es = nc.alloc_sbuf_tensor("c_es", [128, 1], F32)
    c_eb = nc.alloc_sbuf_tensor("c_eb", [128, 1], F32)
    s1b = nc.alloc_sbuf_tensor("s1b", [128, 1], F32)
    rs1b = nc.alloc_sbuf_tensor("rs1b", [128, 1], F32)
    s2b = nc.alloc_sbuf_tensor("s2b", [128, 1], F32)
    rs2b = nc.alloc_sbuf_tensor("rs2b", [128, 1], F32)
    s1acc = nc.alloc_sbuf_tensor("s1acc", [128, 1], F32)
    s2acc = nc.alloc_sbuf_tensor("s2acc", [128, 1], F32)
    sscr = nc.alloc_sbuf_tensor("sscr", [128, 8], F32)

    def newton_recip(vector, dst, dstn, src, srcn, tmp, tmpn):
        P.op(vector, "dve", [srcn], [dstn],
             lambda v: v.reciprocal(out=dst, in_=src))
        P.op(vector, "dve", [srcn, dstn], [tmpn],
             lambda v: v.scalar_tensor_tensor(out=tmp, in0=src, scalar=-1.0,
                                              in1=dst, op0=ALU.mult, op1=ALU.mult))
        P.op(vector, "dve", [tmpn], [tmpn],
             lambda v: v.tensor_scalar_add(out=tmp, in0=tmp, scalar1=2.0))
        P.op(vector, "dve", [tmpn, dstn], [dstn],
             lambda v: v.tensor_mul(out=dst, in0=dst, in1=tmp))

    MAGIC = 12582912.0

    def round_chain(vector, buf, bufn, tmp, tmpn, lo, hi, outap, outn):
        P.op(vector, "dve", [bufn], [tmpn],
             lambda v: v.tensor_scalar(out=tmp, in0=buf, scalar1=MAGIC, scalar2=lo,
                                       op0=ALU.subtract, op1=ALU.max))
        P.op(vector, "dve", [tmpn], [outn],
             lambda v: v.tensor_single_scalar(out=outap, in_=tmp, scalar=hi, op=ALU.min))

    with nc.Block() as blk:
        def _(g):
            for t, v, n in [(ones32, 1.0, "ones32"), (c_one, 1.0, "c_one"),
                            (c_lneps, 1e-5, "c_lneps"), (c_es, 0.125, "c_es"),
                            (c_eb, -MSTAB, "c_eb"), (s1acc, -1e30, "s1acc"),
                            (s2acc, -1e30, "s2acc")]:
                P.op(g, "pool", [], [n], lambda gg, t=t, v=v: gg.memset(t[:], v))
            P.op(g, "pool", ["ones32"], ["idn32r"],
                 lambda gg: gg.affine_select(out=idn32[:, :], in_=ones32[:, :],
                                             pattern=[[-1, 128]], compare_op=ALU.is_equal,
                                             fill=0.0, base=0, channel_multiplier=1))
            P.op(g, "pool", ["idn32r"], ["idn16r"],
                 lambda gg: gg.tensor_copy(out=idn16[:, :], in_=idn32[:, :]))
        blk.gpsimd(_)
    P.reset_sat()

    with contextlib.ExitStack() as es, nc.Block() as blk:
        NLS = 3
        xrow = [es.enter_context(nc.sbuf_tensor(f"xrow{i}", [128, D], F32)) for i in range(NLS)]
        sq = [es.enter_context(nc.sbuf_tensor(f"sq{i}", [128, D], F32)) for i in range(NLS)]
        xn = [es.enter_context(nc.sbuf_tensor(f"xn{i}", [128, D], F32)) for i in range(NLS)]
        st_ = [es.enter_context(nc.sbuf_tensor(f"lnst{i}", [128, 8], F32)) for i in range(NLS)]
        hi_sb = [es.enter_context(nc.sbuf_tensor(f"hisb{i}", [128, 512], BF16)) for i in range(NLS)]
        lo_sb = [es.enter_context(nc.sbuf_tensor(f"losb{i}", [128, 512], BF16)) for i in range(NLS)]
        ps_t = [es.enter_context(nc.psum_tensor(f"lnps{i}", [128, 512], F32)) for i in range(3)]

        spr = min(4, NDC)
        tiles = [(xb, xnTg_hi, xnTg_lo, i) for i in range(NGT)] + \
                [(xbo, xnTo_hi, xnTo_lo, i) for i in range(NOT)]
        for idx, (src, dhi, dlo, i) in enumerate(tiles):
            sl = idx % NLS

            def _(sync, sl=sl, src=src, i=i):
                P.op(sync, "sync", [], [f"xrow{sl}"],
                     lambda q: q.dma_start(out=xrow[sl][:], in_=src[i * 128:(i + 1) * 128, :]),
                     sem=QX[sl], inc=16)
            blk.sync(_)

            def _(vector, sl=sl):
                P.op(vector, "dve", [f"xrow{sl}"], [f"msum{sl}"],
                     lambda v: v.reduce_sum(out=st_[sl][:, 0:1], in_=xrow[sl][:, :], axis=AX.X))
            blk.vector(_)

            def _(scalar, sl=sl):
                P.op(scalar, "act", [f"xrow{sl}"], [f"sq{sl}", f"sumsq{sl}"],
                     lambda a: a.activation(out=sq[sl][:, :], in_=xrow[sl][:, :], func=AF.Square,
                                            accum_out=st_[sl][:, 1:2]))
            blk.scalar(_)

            def _(vector, sl=sl):
                P.op(vector, "dve", [f"msum{sl}"], [f"m{sl}"],
                     lambda v: v.tensor_scalar_mul(out=st_[sl][:, 2:3], in0=st_[sl][:, 0:1],
                                                   scalar1=1.0 / D))
                P.op(vector, "dve", [f"sumsq{sl}"], [f"ex2{sl}"],
                     lambda v: v.tensor_scalar_mul(out=st_[sl][:, 3:4], in0=st_[sl][:, 1:2],
                                                   scalar1=1.0 / D))
                P.op(vector, "dve", [f"m{sl}"], [f"m2{sl}"],
                     lambda v: v.tensor_mul(out=st_[sl][:, 4:5], in0=st_[sl][:, 2:3],
                                            in1=st_[sl][:, 2:3]))
                P.op(vector, "dve", [f"ex2{sl}", f"m2{sl}"], [f"var{sl}"],
                     lambda v: v.tensor_sub(out=st_[sl][:, 5:6], in0=st_[sl][:, 3:4],
                                            in1=st_[sl][:, 4:5]))
            blk.vector(_)

            def _(scalar, sl=sl):
                P.op(scalar, "act", [f"var{sl}", "c_lneps", "c_one"], [f"sd{sl}"],
                     lambda a: a.activation(out=st_[sl][:, 6:7], in_=st_[sl][:, 5:6], func=AF.Sqrt,
                                            bias=c_lneps[:, :], scale=c_one[:, :]))
            blk.scalar(_)

            def _(vector, sl=sl):
                P.op(vector, "dve", [f"sd{sl}"], [f"rstd{sl}"],
                     lambda v: v.reciprocal(out=st_[sl][:, 6:7], in_=st_[sl][:, 6:7]))
                P.op(vector, "dve", [f"xrow{sl}", f"m{sl}", f"rstd{sl}"], [f"xn{sl}"],
                     lambda v: v.tensor_scalar(out=xn[sl][:, :], in0=xrow[sl][:, :],
                                               scalar1=st_[sl][:, 2:3], scalar2=st_[sl][:, 6:7],
                                               op0=ALU.subtract, op1=ALU.mult))
            blk.vector(_)

            def _(vector, sl=sl):
                P.op(vector, "dve", [f"xn{sl}"], [f"tmax{sl}"],
                     lambda v: v.reduce_max(out=sscr[:, sl:sl + 1], in_=xn[sl][:, :], axis=AX.X))
                P.op(vector, "dve", [f"tmax{sl}", "s1acc"], ["s1acc"],
                     lambda v: v.tensor_max(out=s1acc[:, :], in0=s1acc[:, :],
                                            in1=sscr[:, sl:sl + 1]))
            blk.vector(_)

            for r in range((NDC + spr - 1) // spr):
                nsub = min(spr, NDC - r * spr)

                def _(tensor, sl=sl, r=r, nsub=nsub):
                    def e(te):
                        ii = None
                        for j in range(nsub):
                            dch = r * spr + j
                            ii = te.transpose(ps_t[sl][:, j * 128:(j + 1) * 128],
                                              xn[sl][:, dch * 128:(dch + 1) * 128], idn32[:, :])
                        return ii
                    P.op(tensor, "pe", [f"xn{sl}", "idn32r"], [f"ps_t{sl}"], e)
                blk.tensor(_)

                def _(scalar, sl=sl, nsub=nsub):
                    P.op(scalar, "act", [f"ps_t{sl}", "c_one"], [f"hi_sb{sl}"],
                         lambda a: a.activation(out=hi_sb[sl][:, 0:nsub * 128],
                                                in_=ps_t[sl][:, 0:nsub * 128],
                                                func=AF.Copy, scale=c_one[:, :], bias=0.0))
                blk.scalar(_)

                def _(vector, sl=sl, nsub=nsub):
                    P.op(vector, "dve", [f"ps_t{sl}", f"hi_sb{sl}"], [f"lo_sb{sl}"],
                         lambda v: v.tensor_sub(out=lo_sb[sl][:, 0:nsub * 128],
                                                in0=ps_t[sl][:, 0:nsub * 128],
                                                in1=hi_sb[sl][:, 0:nsub * 128]))
                blk.vector(_)

                def _(g, sl=sl, r=r, i=i, nsub=nsub, dhi=dhi, dlo=dlo):
                    dview_hi = dhi[r * spr * 128:(r * spr + nsub) * 128,
                                   i * 128:(i + 1) * 128].rearrange("(j p) c -> p j c", p=128)
                    dview_lo = dlo[r * spr * 128:(r * spr + nsub) * 128,
                                   i * 128:(i + 1) * 128].rearrange("(j p) c -> p j c", p=128)
                    P.op(g, "pool", [f"hi_sb{sl}"], ["xnt_dram"],
                         lambda q: q.dma_start(
                             out=dview_hi,
                             in_=hi_sb[sl][:, 0:nsub * 128].rearrange("p (j c) -> p j c", c=128)),
                         sem="gw_xnt", inc=16)
                    P.op(g, "pool", [f"lo_sb{sl}"], ["xnt_dram"],
                         lambda q: q.dma_start(
                             out=dview_lo,
                             in_=lo_sb[sl][:, 0:nsub * 128].rearrange("p (j c) -> p j c", c=128)),
                         sem="gw_xnt", inc=16)
                blk.gpsimd(_)

        def _(g):
            P.op(g, "pool", ["s1acc"], ["s1f"],
                 lambda q: q.tensor_reduce(out=sscr[0:1, 2:3], in_=s1acc[:, :], axis=AX.C,
                                           op=ALU.max))
            P.op(g, "pool", ["s1f"], ["cc_in"],
                 lambda q: q.dma_start(out=cc_in[:, :], in_=sscr[0:1, 2:3]), sem="gcc", inc=16)
            P.op(g, "pool", ["cc_in"], ["cc_out"],
                 lambda q: q.collective_compute("AllReduce", ALU.max, replica_groups=RG8,
                                                ins=[cc_in[:, :]], outs=[cc_out[:, :]]))
            P.op(g, "pool", ["cc_out"], ["s1g"],
                 lambda q: q.dma_start(out=sscr[0:1, 3:4], in_=cc_out[:, :]), sem="gcc", inc=16)
        blk.gpsimd(_)

        def _(vector):
            P.op(vector, "dve", ["s1g"], ["s1v"],
                 lambda v: v.tensor_scalar(out=sscr[0:1, 4:5], in0=sscr[0:1, 3:4], scalar1=1e-8,
                                           scalar2=1.0 / 255.0, op0=ALU.max, op1=ALU.mult))
            newton_recip(vector, sscr[0:1, 5:6], "rs1v", sscr[0:1, 4:5], "s1v",
                         sscr[0:1, 6:7], "ntmp")
        blk.vector(_)

        def _(scalar):
            P.op(scalar, "act", ["s1v", "rs1v"], ["scd01"],
                 lambda a: a.dma_start(out=scd[4:6], in_=sscr[0:1, 4:6]), sem="vr", inc=16)
        blk.scalar(_)

        def _(sync):
            P.op(sync, "sync", ["scd01"], ["s1b"],
                 lambda q: q.dma_start(out=s1b[:, :], in_=scd[None, 4:5].broadcast_to([128, 1])),
                 sem="qb", inc=16)
            P.op(sync, "sync", ["scd01"], ["rs1b"],
                 lambda q: q.dma_start(out=rs1b[:, :], in_=scd[None, 5:6].broadcast_to([128, 1])),
                 sem="qb", inc=16)
            sync.wait_ge(P.h["qb"], P.cnt["qb"])
            sync.wait_ge(P.h["gw_xnt"], P.cnt["gw_xnt"])
        blk.sync(_)
    P.reset_sat()

    NOW = OWNT // TW
    with contextlib.ExitStack() as es, nc.Block() as blk:
        xqT = es.enter_context(nc.sbuf_tensor("xqT", [128, NDC * OWNT], BF16))
        xh = [es.enter_context(nc.sbuf_tensor(f"ffxh{i}", [128, TW], BF16)) for i in range(2)]
        xl = [es.enter_context(nc.sbuf_tensor(f"ffxl{i}", [128, TW], BF16)) for i in range(2)]
        tq = [es.enter_context(nc.sbuf_tensor(f"fftq{i}", [128, TW], F32)) for i in range(2)]
        tq2 = [es.enter_context(nc.sbuf_tensor(f"fftq2{i}", [128, TW], F32)) for i in range(2)]
        w1raw = [es.enter_context(nc.sbuf_tensor(f"w1raw{i}", [128, D], F32)) for i in range(2)]
        w1q = [es.enter_context(nc.sbuf_tensor(f"w1q{i}", [128, D], BF16)) for i in range(2)]
        w1qT = [es.enter_context(nc.sbuf_tensor(f"w1qT{i}", [128, D], BF16)) for i in range(2)]
        h_sb = [es.enter_context(nc.sbuf_tensor(f"hsb{i}", [128, TW], F32)) for i in range(4)]
        b1sb = es.enter_context(nc.sbuf_tensor("b1sb", [128, NF1C], F32))
        wst = [es.enter_context(nc.sbuf_tensor(f"ffwst{i}", [128, 10], F32)) for i in range(2)]
        hmx = es.enter_context(nc.sbuf_tensor("hmx", [128, 2], F32))
        ps_h = [es.enter_context(nc.psum_tensor(f"psh{i}", [128, TW], F32)) for i in range(4)]
        ps_w = es.enter_context(nc.psum_tensor("psw", [128, 512], BF16))

        def _(sync):
            b1r = b1.rearrange("(c p) -> p c", p=128)
            for j in range(NF1C):
                P.op(sync, "sync", [], ["b1sb"],
                     lambda q, j=j: q.dma_start(out=b1sb[:, j:j + 1], in_=b1r[:, j:j + 1]),
                     sem="qb", inc=16)
        blk.sync(_)

        for it, (dch, g) in enumerate([(d, g) for d in range(NDC) for g in range(NOW)]):
            sl = it % 2

            def _(sync, sl=sl, dch=dch, g=g):
                t0 = g * TW
                P.op(sync, "sync", ["xnt_dram"], [f"ffxh{sl}"],
                     lambda q: q.dma_start(out=xh[sl][:, :],
                                           in_=xnTo_hi[dch * 128:(dch + 1) * 128, t0:t0 + TW]),
                     sem=QX[sl], inc=16)
                P.op(sync, "sync", ["xnt_dram"], [f"ffxl{sl}"],
                     lambda q: q.dma_start(out=xl[sl][:, :],
                                           in_=xnTo_lo[dch * 128:(dch + 1) * 128, t0:t0 + TW]),
                     sem=QX[2 + sl], inc=16)
            blk.sync(_)

            def _(vector, sl=sl, dch=dch, g=g):
                P.op(vector, "dve", [f"ffxh{sl}", f"ffxl{sl}"], [f"fftq{sl}"],
                     lambda v: v.tensor_add(out=tq[sl][:, :], in0=xh[sl][:, :], in1=xl[sl][:, :]))
                P.op(vector, "dve", [f"fftq{sl}", "rs1b"], [f"fftq{sl}"],
                     lambda v: v.tensor_scalar(out=tq[sl][:, :], in0=tq[sl][:, :],
                                               scalar1=rs1b[:, :], scalar2=MAGIC,
                                               op0=ALU.mult, op1=ALU.add))
                cd = dch * OWNT + g * TW
                round_chain(vector, tq[sl][:, :], f"fftq{sl}", tq2[sl][:, :], f"fftq2{sl}",
                            0.0, 255.0, xqT[:, cd:cd + TW], "xqT")
            blk.vector(_)

        for fc in range(NF1C):
            sl = fc % 2

            def _(sync, sl=sl, fc=fc):
                P.op(sync, "sync", [], [f"w1raw{sl}"],
                     lambda q: q.dma_start(out=w1raw[sl][:, :], in_=w1[fc * 128:(fc + 1) * 128, :]),
                     sem=QWSEM[sl], inc=16)
            blk.sync(_)

            def _(vector, sl=sl, fc=fc):
                W = wst[sl]
                P.op(vector, "dve", [f"w1raw{sl}"], [f"aw{sl}"],
                     lambda v: v.tensor_reduce(out=W[:, 0:1], in_=w1raw[sl][:, :], axis=AX.X,
                                               op=ALU.max, apply_absolute_value=True))
                P.op(vector, "dve", [f"aw{sl}"], [f"s1w{sl}"],
                     lambda v: v.tensor_scalar(out=W[:, 1:2], in0=W[:, 0:1], scalar1=1e-8,
                                               scalar2=1.0 / 127.0, op0=ALU.max, op1=ALU.mult))
                newton_recip(vector, W[:, 2:3], f"rw{sl}", W[:, 1:2], f"s1w{sl}",
                             W[:, 3:4], f"wtmp{sl}")
                P.op(vector, "dve", [f"w1raw{sl}", f"rw{sl}"], [f"w1raw{sl}"],
                     lambda v: v.tensor_scalar(out=w1raw[sl][:, :], in0=w1raw[sl][:, :],
                                               scalar1=W[:, 2:3], scalar2=MAGIC,
                                               op0=ALU.mult, op1=ALU.add))
                P.op(vector, "dve", [f"w1raw{sl}"], [f"w1raw{sl}"],
                     lambda v: v.tensor_scalar(out=w1raw[sl][:, :], in0=w1raw[sl][:, :],
                                               scalar1=MAGIC, scalar2=-128.0,
                                               op0=ALU.subtract, op1=ALU.max))
                P.op(vector, "dve", [f"w1raw{sl}"], [f"w1q{sl}"],
                     lambda v: v.tensor_single_scalar(out=w1q[sl][:, :], in_=w1raw[sl][:, :],
                                                      scalar=127.0, op=ALU.min))
                P.op(vector, "dve", [f"s1w{sl}", "s1b"], [f"ssw{sl}"],
                     lambda v: v.tensor_mul(out=W[:, 4:5], in0=W[:, 1:2], in1=s1b[:, :]))
                newton_recip(vector, W[:, 5:6], f"rssw{sl}", W[:, 4:5], f"ssw{sl}",
                             W[:, 3:4], f"wtmp{sl}")
                P.op(vector, "dve", ["b1sb", f"rssw{sl}"], [f"bq{sl}"],
                     lambda v: v.scalar_tensor_tensor(out=W[:, 6:7], in0=b1sb[:, fc:fc + 1],
                                                      scalar=1.0, in1=W[:, 5:6],
                                                      op0=ALU.mult, op1=ALU.mult))
                P.op(vector, "dve", [f"bq{sl}"], [f"bq{sl}"],
                     lambda v: v.tensor_scalar_add(out=W[:, 6:7], in0=W[:, 6:7], scalar1=MAGIC))
                P.op(vector, "dve", [f"bq{sl}"], [f"bq{sl}"],
                     lambda v: v.tensor_scalar_sub(out=W[:, 6:7], in0=W[:, 6:7], scalar1=MAGIC))
                P.op(vector, "dve", [f"bq{sl}", f"ssw{sl}"], [f"bqf{sl}"],
                     lambda v: v.tensor_mul(out=W[:, 8:9], in0=W[:, 6:7], in1=W[:, 4:5]))
            blk.vector(_)

            spr2 = min(4, NDC)
            for r in range((NDC + spr2 - 1) // spr2):
                nsub = min(spr2, NDC - r * spr2)

                def _(tensor, sl=sl, r=r, nsub=nsub):
                    def e(te):
                        ii = None
                        for j in range(nsub):
                            dch = r * spr2 + j
                            ii = te.transpose(ps_w[:, j * 128:(j + 1) * 128],
                                              w1q[sl][:, dch * 128:(dch + 1) * 128], idn16[:, :])
                        return ii
                    P.op(tensor, "pe", [f"w1q{sl}", "idn16r"], ["ps_w"], e)
                blk.tensor(_)

                def _(scalar, sl=sl, r=r, nsub=nsub):
                    P.op(scalar, "act", ["ps_w", "c_one"], [f"w1qT{sl}"],
                         lambda a: a.activation(out=w1qT[sl][:, r * spr2 * 128:(r * spr2 + nsub) * 128],
                                                in_=ps_w[:, 0:nsub * 128], func=AF.Copy,
                                                scale=c_one[:, :], bias=0.0))
                blk.scalar(_)

            for g in range(NOW):
                psl = (fc * NOW + g) % 4

                def _(tensor, sl=sl, g=g, psl=psl):
                    def e(te):
                        ii = None
                        for j in range(NDC):
                            ii = te.matmul(ps_h[psl][:, :], w1qT[sl][:, j * 128:(j + 1) * 128],
                                           xqT[:, j * OWNT + g * TW: j * OWNT + g * TW + TW],
                                           start=(j == 0), stop=(j == NDC - 1))
                        return ii
                    P.op(tensor, "pe", [f"w1qT{sl}", "xqT"], [f"ps_h{psl}"], e)
                blk.tensor(_)

                def _(scalar, sl=sl, psl=psl):
                    P.op(scalar, "act", [f"ps_h{psl}", f"ssw{sl}", f"bqf{sl}"], [f"h_sb{psl}"],
                         lambda a: a.activation(out=h_sb[psl][:, :], in_=ps_h[psl][:, :],
                                                func=AF.Relu, scale=wst[sl][:, 4:5],
                                                bias=wst[sl][:, 8:9]))
                blk.scalar(_)

                def _(vector, psl=psl):
                    P.op(vector, "dve", [f"h_sb{psl}"], ["hm0"],
                         lambda v: v.reduce_max(out=hmx[:, 0:1], in_=h_sb[psl][:, :], axis=AX.X))
                    P.op(vector, "dve", ["hm0", "s2acc"], ["s2acc"],
                         lambda v: v.tensor_max(out=s2acc[:, :], in0=s2acc[:, :], in1=hmx[:, 0:1]))
                blk.vector(_)

                def _(g_, fc=fc, g=g, psl=psl):
                    P.op(g_, "pool", [f"h_sb{psl}"], ["hscr"],
                         lambda q: q.dma_start(out=hscr[fc * 128:(fc + 1) * 128,
                                                        g * TW:(g + 1) * TW],
                                               in_=h_sb[psl][:, :]),
                         sem="gw_h", inc=16)
                blk.gpsimd(_)

        def _(g):
            P.op(g, "pool", ["s2acc"], ["s2f"],
                 lambda q: q.tensor_reduce(out=sscr[0:1, 2:3], in_=s2acc[:, :], axis=AX.C,
                                           op=ALU.max))
            P.op(g, "pool", ["s2f"], ["cc_in2"],
                 lambda q: q.dma_start(out=cc_in2[:, :], in_=sscr[0:1, 2:3]), sem="gcc", inc=16)
            P.op(g, "pool", ["cc_in2"], ["cc_out2"],
                 lambda q: q.collective_compute("AllReduce", ALU.max, replica_groups=RG8,
                                                ins=[cc_in2[:, :]], outs=[cc_out2[:, :]]))
            P.op(g, "pool", ["cc_out2"], ["s2g"],
                 lambda q: q.dma_start(out=sscr[0:1, 3:4], in_=cc_out2[:, :]), sem="gcc", inc=16)
        blk.gpsimd(_)

        def _(vector):
            P.op(vector, "dve", ["s2g"], ["s2v"],
                 lambda v: v.tensor_scalar(out=sscr[0:1, 4:5], in0=sscr[0:1, 3:4], scalar1=1e-8,
                                           scalar2=1.0 / 255.0, op0=ALU.max, op1=ALU.mult))
            newton_recip(vector, sscr[0:1, 5:6], "rs2v", sscr[0:1, 4:5], "s2v",
                         sscr[0:1, 6:7], "ntmp2")
        blk.vector(_)

        def _(scalar):
            P.op(scalar, "act", ["s2v", "rs2v"], ["scd23"],
                 lambda a: a.dma_start(out=scd[6:8], in_=sscr[0:1, 4:6]), sem="vr", inc=16)
        blk.scalar(_)

        def _(sync):
            P.op(sync, "sync", ["scd23"], ["s2b"],
                 lambda q: q.dma_start(out=s2b[:, :], in_=scd[None, 6:7].broadcast_to([128, 1])),
                 sem="qb", inc=16)
            P.op(sync, "sync", ["scd23"], ["rs2b"],
                 lambda q: q.dma_start(out=rs2b[:, :], in_=scd[None, 7:8].broadcast_to([128, 1])),
                 sem="qb", inc=16)
            sync.wait_ge(P.h["qb"], P.cnt["qb"])
            sync.wait_ge(P.h["gw_h"], P.cnt["gw_h"])
        blk.sync(_)
    P.reset_sat()

    WN = 512
    NW = T // WN
    NOWq = OWNT // WN
    with contextlib.ExitStack() as es, nc.Block() as blk:
        qstk = [es.enter_context(nc.sbuf_tensor(f"qstk{h}", [128, OWNT], BF16)) for h in range(2)]
        qswp = [es.enter_context(nc.sbuf_tensor(f"qswp{h}", [128, OWNT], BF16)) for h in range(2)]
        kstk = [es.enter_context(nc.sbuf_tensor(f"kstk{h}", [128, T], BF16)) for h in range(2)]
        vstk = [es.enter_context(nc.sbuf_tensor(f"vstk{h}", [128, T], BF16)) for h in range(2)]
        wsl = [es.enter_context(nc.sbuf_tensor(f"wsl{i}", [128, D], F32)) for i in range(2)]
        whi = es.enter_context(nc.sbuf_tensor("whi", [128, D], BF16))
        wlo = es.enter_context(nc.sbuf_tensor("wlo", [128, D], BF16))
        wThi = [es.enter_context(nc.sbuf_tensor(f"wThi{o}", [128, D], BF16)) for o in range(3)]
        wTlo = [es.enter_context(nc.sbuf_tensor(f"wTlo{o}", [128, D], BF16)) for o in range(3)]
        NJH = max(1, NDC // 2)
        xwh = [es.enter_context(nc.sbuf_tensor(f"xwh{i}", [128, NJH * WN], BF16)) for i in range(2)]
        xwl = [es.enter_context(nc.sbuf_tensor(f"xwl{i}", [128, NJH * WN], BF16)) for i in range(2)]
        vf32 = es.enter_context(nc.sbuf_tensor("vf32", [128, WN], F32))
        em32 = [es.enter_context(nc.sbuf_tensor(f"em32_{i}", [128, QW], F32)) for i in range(8)]
        ebuf = [es.enter_context(nc.sbuf_tensor(f"ebuf{i}", [128, 2 * QW], BF16)) for i in range(8)]
        maskbuf = es.enter_context(nc.sbuf_tensor("maskbuf", [128, max(1, c["NMASK"]) * QW], BF16))
        onesk = es.enter_context(nc.sbuf_tensor("onesk", [128, 1], BF16))
        rinv = es.enter_context(nc.sbuf_tensor("rinv", [1, QW], F32))
        rrep = es.enter_context(nc.sbuf_tensor("rrep", [64, QW], F32))
        ytmp = es.enter_context(nc.sbuf_tensor("ytmp", [64, QW], F32))
        ylo2 = es.enter_context(nc.sbuf_tensor("ylo2", [64, QW], F32))
        ynrm = es.enter_context(nc.sbuf_tensor("ynrm", [64, QW], F32))
        yhi = es.enter_context(nc.sbuf_tensor("yhi", [64, QW], BF16))
        ylo = es.enter_context(nc.sbuf_tensor("ylo", [64, QW], BF16))
        ps_s = [es.enter_context(nc.psum_tensor(f"pss{i}", [128, QW], F32)) for i in range(2)]
        ps_y = es.enter_context(nc.psum_tensor("psy", [128, QW], F32))
        ps_q = es.enter_context(nc.psum_tensor("psq", [128, WN], F32))
        ps_k = es.enter_context(nc.psum_tensor("psk", [128, WN], F32))
        ps_v = es.enter_context(nc.psum_tensor("psv", [128, WN], F32))
        ps_x = es.enter_context(nc.psum_tensor("psx", [128, 512], F32))
        ps_xw = es.enter_context(nc.psum_tensor("psxw", [128, 512], BF16))

        def _(vector):
            P.op(vector, "dve", [], ["onesk"], lambda v: v.memset(onesk[:], 1.0))
        blk.vector(_)

        def _(sync):
            for mi in range(c["NMASK"]):
                P.op(sync, "sync", [], ["maskbuf"],
                     lambda q, mi=mi: q.dma_start(out=maskbuf[:, mi * QW:(mi + 1) * QW],
                                                  in_=maskin[mi, :, :]),
                     sem="qb", inc=16)
        blk.sync(_)

        MINEF, MOFF = c["MINEF"], c["MOFF"]
        qmsems = ["qm0", "qm1", "qm2", "qm3"]
        def emit_wprep(ph):
            for o in range(3):
                wi = o % 2
                rows = o * D + ph * 128

                def _(sync, wi=wi, rows=rows):
                    P.op(sync, "sync", [], [f"wsl{wi}"],
                         lambda q: q.dma_start(out=wsl[wi][:, :], in_=wattn[rows:rows + 128, :]),
                         sem=QWSEM[wi], inc=16)
                blk.sync(_)

                def _(vector, wi=wi):
                    P.op(vector, "dve", [f"wsl{wi}"], ["whi"],
                         lambda v: v.tensor_copy(out=whi[:, :], in_=wsl[wi][:, :]))
                    P.op(vector, "dve", [f"wsl{wi}", "whi"], ["wlo"],
                         lambda v: v.tensor_sub(out=wlo[:, :], in0=wsl[wi][:, :], in1=whi[:, :]))
                blk.vector(_)

                spr3 = min(4, NDC)
                for srcb, srcn, dstb, dstn in [(whi, "whi", wThi[o], f"wThi{o}"),
                                               (wlo, "wlo", wTlo[o], f"wTlo{o}")]:
                    for r in range((NDC + spr3 - 1) // spr3):
                        nsub = min(spr3, NDC - r * spr3)

                        def _(tensor, srcb=srcb, srcn=srcn, r=r, nsub=nsub):
                            def e(te):
                                ii = None
                                for j in range(nsub):
                                    dch = r * spr3 + j
                                    ii = te.transpose(ps_xw[:, j * 128:(j + 1) * 128],
                                                      srcb[:, dch * 128:(dch + 1) * 128],
                                                      idn16[:, :])
                                return ii
                            P.op(tensor, "pe", [srcn, "idn16r"], ["ps_xw"], e)
                        blk.tensor(_)

                        def _(scalar, dstb=dstb, dstn=dstn, r=r, nsub=nsub):
                            P.op(scalar, "act", ["ps_xw", "c_one"], [dstn],
                                 lambda a: a.activation(
                                     out=dstb[:, r * spr3 * 128:(r * spr3 + nsub) * 128],
                                     in_=ps_xw[:, 0:nsub * 128], func=AF.Copy,
                                     scale=c_one[:, :], bias=0.0))
                        blk.scalar(_)

        NHALF = NDC // NJH
        for ph in range(NPAIR):
            if ph == 0:
                emit_wprep(0)
            for tt in range(NW):
                for hf in range(NHALF):
                    xsl = (tt * NHALF + hf) % 2

                    def _(sync, tt=tt, hf=hf, xsl=xsl):
                        dv_h = xnTg_hi[hf * NJH * 128:(hf + 1) * NJH * 128,
                                       tt * WN:(tt + 1) * WN].rearrange("(j p) c -> p j c", p=128)
                        dv_l = xnTg_lo[hf * NJH * 128:(hf + 1) * NJH * 128,
                                       tt * WN:(tt + 1) * WN].rearrange("(j p) c -> p j c", p=128)
                        P.op(sync, "sync", ["xnt_dram"], [f"xwh{xsl}"],
                             lambda q: q.dma_start(
                                 out=xwh[xsl][:, :].rearrange("p (j c) -> p j c", c=WN), in_=dv_h),
                             sem=QX[xsl], inc=16)
                        P.op(sync, "sync", ["xnt_dram"], [f"xwl{xsl}"],
                             lambda q: q.dma_start(
                                 out=xwl[xsl][:, :].rearrange("p (j c) -> p j c", c=WN), in_=dv_l),
                             sem=QX[2 + xsl], inc=16)
                    blk.sync(_)

                    def _(tensor, hf=hf, xsl=xsl):
                        for jj in range(NJH):
                            j = hf * NJH + jj
                            for o, psd, psn in [(1, ps_k, "ps_k"), (2, ps_v, "ps_v")]:
                                def e(te, o=o, psd=psd, j=j, jj=jj):
                                    te.matmul(psd[:, :], wThi[o][:, j * 128:(j + 1) * 128],
                                              xwh[xsl][:, jj * WN:(jj + 1) * WN],
                                              start=(j == 0), stop=False)
                                    te.matmul(psd[:, :], wThi[o][:, j * 128:(j + 1) * 128],
                                              xwl[xsl][:, jj * WN:(jj + 1) * WN],
                                              start=False, stop=False)
                                    return te.matmul(psd[:, :], wTlo[o][:, j * 128:(j + 1) * 128],
                                                     xwh[xsl][:, jj * WN:(jj + 1) * WN],
                                                     start=False, stop=(j == NDC - 1))
                                P.op(tensor, "pe",
                                     [f"wThi{o}", f"wTlo{o}", f"xwh{xsl}", f"xwl{xsl}"], [psn], e)
                    blk.tensor(_)

                def _(scalar, tt=tt):
                    def e(a):
                        for h in range(2):
                            a.activation(out=kstk[h][0:64, tt * WN:(tt + 1) * WN],
                                         in_=ps_k[h * 64:(h + 1) * 64, :],
                                         func=AF.Copy, scale=c_one[0:64, :], bias=0.0)
                        return a.activation(out=vf32[:, :], in_=ps_v[:, :],
                                            func=AF.Copy, scale=c_one[:, :], bias=0.0)
                    P.op(scalar, "act", ["ps_k", "ps_v", "c_one"], ["kstk0", "kstk1", "vf32"], e)
                blk.scalar(_)

                def _(vector, tt=tt):
                    for h in range(2):
                        P.op(vector, "dve", ["ps_k", f"kstk{h}"], [f"kstk{h}"],
                             lambda v, h=h: v.tensor_sub(
                                 out=kstk[h][64:128, tt * WN:(tt + 1) * WN],
                                 in0=ps_k[h * 64:(h + 1) * 64, :],
                                 in1=kstk[h][0:64, tt * WN:(tt + 1) * WN]))
                blk.vector(_)

                nst = WN // 128

                def _(tensor, tt=tt, nst=nst):
                    def e(te):
                        ii = None
                        for stq in range(nst):
                            vps = ps_x if stq % 2 == 0 else ps_s[0]
                            ii = te.transpose(vps[:, (stq // 2) * 128:(stq // 2) * 128 + 128],
                                              vf32[:, stq * 128:(stq + 1) * 128], idn32[:, :])
                        return ii
                    P.op(tensor, "pe", ["vf32", "idn32r"], ["ps_x", "ps_s0"], e)
                blk.tensor(_)

                def _(scalar, tt=tt, nst=nst):
                    def e(a):
                        ii = None
                        for stq in range(nst):
                            vps = ps_x if stq % 2 == 0 else ps_s[0]
                            kt = tt * nst + stq
                            for h in range(2):
                                ii = a.activation(out=vstk[h][:, kt * 128:kt * 128 + 64],
                                                  in_=vps[:, (stq // 2) * 128 + h * 64:(stq // 2) * 128 + (h + 1) * 64],
                                                  func=AF.Copy, scale=c_one[:, :], bias=0.0)
                        return ii
                    P.op(scalar, "act", ["ps_x", "ps_s0", "c_one"], ["vstk0", "vstk1"], e)
                blk.scalar(_)

                def _(vector, tt=tt, nst=nst):
                    for stq in range(nst):
                        vps = ps_x if stq % 2 == 0 else ps_s[0]
                        vpsn = "ps_x" if stq % 2 == 0 else "ps_s0"
                        kt = tt * nst + stq
                        for h in range(2):
                            P.op(vector, "dve", [vpsn, f"vstk{h}"], [f"vstk{h}"],
                                 lambda v, h=h, vps=vps, stq=stq, kt=kt: v.tensor_sub(
                                     out=vstk[h][:, kt * 128 + 64:kt * 128 + 128],
                                     in0=vps[:, (stq // 2) * 128 + h * 64:(stq // 2) * 128 + (h + 1) * 64],
                                     in1=vstk[h][:, kt * 128:kt * 128 + 64]))
                blk.vector(_)

            for tt in range(NOWq):
                for hf in range(NHALF):
                    xsl = (tt * NHALF + hf) % 2

                    def _(sync, tt=tt, hf=hf, xsl=xsl):
                        dv_h = xnTo_hi[hf * NJH * 128:(hf + 1) * NJH * 128,
                                       tt * WN:(tt + 1) * WN].rearrange("(j p) c -> p j c", p=128)
                        dv_l = xnTo_lo[hf * NJH * 128:(hf + 1) * NJH * 128,
                                       tt * WN:(tt + 1) * WN].rearrange("(j p) c -> p j c", p=128)
                        P.op(sync, "sync", ["xnt_dram"], [f"xwh{xsl}"],
                             lambda q: q.dma_start(
                                 out=xwh[xsl][:, :].rearrange("p (j c) -> p j c", c=WN), in_=dv_h),
                             sem=QX[xsl], inc=16)
                        P.op(sync, "sync", ["xnt_dram"], [f"xwl{xsl}"],
                             lambda q: q.dma_start(
                                 out=xwl[xsl][:, :].rearrange("p (j c) -> p j c", c=WN), in_=dv_l),
                             sem=QX[2 + xsl], inc=16)
                    blk.sync(_)

                    def _(tensor, hf=hf, xsl=xsl):
                        for jj in range(NJH):
                            j = hf * NJH + jj

                            def e(te, j=j, jj=jj):
                                te.matmul(ps_q[:, :], wThi[0][:, j * 128:(j + 1) * 128],
                                          xwh[xsl][:, jj * WN:(jj + 1) * WN],
                                          start=(j == 0), stop=False)
                                te.matmul(ps_q[:, :], wThi[0][:, j * 128:(j + 1) * 128],
                                          xwl[xsl][:, jj * WN:(jj + 1) * WN],
                                          start=False, stop=False)
                                return te.matmul(ps_q[:, :], wTlo[0][:, j * 128:(j + 1) * 128],
                                                 xwh[xsl][:, jj * WN:(jj + 1) * WN],
                                                 start=False, stop=(j == NDC - 1))
                            P.op(tensor, "pe", ["wThi0", "wTlo0", f"xwh{xsl}", f"xwl{xsl}"],
                                 ["ps_q"], e)
                    blk.tensor(_)

                def _(scalar, tt=tt):
                    def e(a):
                        ii = None
                        for h in range(2):
                            ii = a.activation(out=qstk[h][0:64, tt * WN:(tt + 1) * WN],
                                              in_=ps_q[h * 64:(h + 1) * 64, :],
                                              func=AF.Copy, scale=c_one[0:64, :], bias=0.0)
                        return ii
                    P.op(scalar, "act", ["ps_q", "c_one"], ["qstk0", "qstk1"], e)
                blk.scalar(_)

                def _(vector, tt=tt):
                    for h in range(2):
                        P.op(vector, "dve", ["ps_q", f"qstk{h}"], [f"qstk{h}"],
                             lambda v, h=h: v.tensor_sub(
                                 out=qstk[h][64:128, tt * WN:(tt + 1) * WN],
                                 in0=ps_q[h * 64:(h + 1) * 64, :],
                                 in1=qstk[h][0:64, tt * WN:(tt + 1) * WN]))
                blk.vector(_)

                def _(vector, tt=tt):
                    for h in range(2):
                        P.op(vector, "dve", [f"qstk{h}"], [f"qswp{h}"],
                             lambda v, h=h: v.tensor_copy(
                                 out=qswp[h][0:64, tt * WN:(tt + 1) * WN],
                                 in_=qstk[h][64:128, tt * WN:(tt + 1) * WN]))
                        P.op(vector, "dve", [f"qstk{h}"], [f"qswp{h}"],
                             lambda v, h=h: v.tensor_copy(
                                 out=qswp[h][64:128, tt * WN:(tt + 1) * WN],
                                 in_=qstk[h][0:64, tt * WN:(tt + 1) * WN]))
                blk.vector(_)

            if ph + 1 < NPAIR:
                emit_wprep(ph + 1)

            SPS = [ps_s[0], ps_s[1], ps_q, ps_k]
            SPSN = ["ps_s0", "ps_s1", "ps_q", "ps_k"]
            for h in range(2):
                head = ph * 2 + h
                for s in range(NSLOT):
                    E = EPROF[s]
                    qc = s * QW
                    mef, mof = MINEF[s], MOFF[s]

                    def emit_av(kts, h=h, E=E):
                        def _(tensor, kts=kts):
                            for kt in kts:
                                i = kt % 8

                                def e(te, kt=kt, i=i):
                                    te.matmul(ps_y[:, :], vstk[h][:, kt * 128:(kt + 1) * 128],
                                              ebuf[i][:, 0:QW], start=(kt == 0), stop=False)
                                    te.matmul(ps_y[:, :], vstk[h][:, kt * 128:(kt + 1) * 128],
                                              ebuf[i][:, QW:2 * QW], start=False, stop=(kt == E - 1))
                                    te.matmul(ps_v[0:1, 0:QW], onesk[:, :], ebuf[i][:, 0:QW],
                                              start=(kt == 0), stop=False)
                                    return te.matmul(ps_v[0:1, 0:QW], onesk[:, :],
                                                     ebuf[i][:, QW:2 * QW],
                                                     start=False, stop=(kt == E - 1))
                                P.op(tensor, "pe", [f"vstk{h}", f"ebuf{i}", "onesk"],
                                     ["ps_y", "ps_v"], e)
                        blk.tensor(_)

                    pending = None
                    for k0 in range(0, E, 4):
                        kts = list(range(k0, min(k0 + 4, E)))

                        def _(tensor, h=h, kts=kts, qc=qc):
                            for kt in kts:
                                i = kt % 4

                                def e(te, kt=kt, i=i):
                                    te.matmul(SPS[i][:, 0:QW], kstk[h][:, kt * 128:(kt + 1) * 128],
                                              qstk[h][:, qc:qc + QW], start=True, stop=False)
                                    return te.matmul(SPS[i][:, 0:QW], kstk[h][:, kt * 128:(kt + 1) * 128],
                                                     qswp[h][:, qc:qc + QW], start=False, stop=True)
                                P.op(tensor, "pe", [f"kstk{h}", f"qstk{h}", f"qswp{h}"], [SPSN[i]], e)
                        blk.tensor(_)

                        def _(scalar, kts=kts):
                            for kt in kts:
                                i = kt % 4
                                i8 = kt % 8
                                P.op(scalar, "act", [SPSN[i], "c_es", "c_eb"], [f"em32_{i8}"],
                                     lambda a, i=i, i8=i8: a.activation(out=em32[i8][:, :],
                                                                        in_=SPS[i][:, 0:QW],
                                                                        func=AF.Exp, scale=c_es[:, :],
                                                                        bias=c_eb[:, :]))
                        blk.scalar(_)

                        if any(kt >= mef for kt in kts):
                            def _(vector, kts=kts, mef=mef, mof=mof):
                                for kt in kts:
                                    if kt >= mef:
                                        i8 = kt % 8
                                        mi = mof + kt - mef
                                        P.op(vector, "dve", ["maskbuf", f"em32_{i8}"], [f"em32_{i8}"],
                                             lambda v, i8=i8, mi=mi: v.scalar_tensor_tensor(
                                                 out=em32[i8][:, :], in0=em32[i8][:, :], scalar=1e38,
                                                 in1=maskbuf[:, mi * QW:(mi + 1) * QW],
                                                 op0=ALU.min, op1=ALU.mult))
                            blk.vector(_)

                        def _(vector, kts=kts):
                            for kt in kts:
                                i8 = kt % 8
                                P.op(vector, "dve", [f"em32_{i8}"], [f"ebuf{i8}"],
                                     lambda v, i8=i8: v.tensor_copy(out=ebuf[i8][:, 0:QW],
                                                                    in_=em32[i8][:, :]))
                                P.op(vector, "dve", [f"em32_{i8}", f"ebuf{i8}"], [f"ebuf{i8}"],
                                     lambda v, i8=i8: v.tensor_sub(out=ebuf[i8][:, QW:2 * QW],
                                                                   in0=em32[i8][:, :],
                                                                   in1=ebuf[i8][:, 0:QW]))
                        blk.vector(_)

                        if pending is not None:
                            emit_av(pending)
                        pending = kts
                    emit_av(pending)

                    def _(vector):
                        P.op(vector, "dve", ["ps_v"], ["rinv"],
                             lambda v: v.reciprocal(out=rinv[:, :], in_=ps_v[0:1, 0:QW]))
                    blk.vector(_)

                    def _(scalar, head=head, s=s):
                        P.op(scalar, "act", ["rinv"], ["rbd"],
                             lambda a: a.dma_start(out=rbd[head * NSLOT + s, :], in_=rinv[0:1, :]),
                             sem="vr", inc=16)
                        P.op(scalar, "act", ["rbd"], ["rrep"],
                             lambda q: q.dma_start(
                                 out=rrep[:, :],
                                 in_=rbd[head * NSLOT + s:head * NSLOT + s + 1, :]
                                 .broadcast_to([64, QW])),
                             sem="rr0", inc=16)
                    blk.scalar(_)

                    def _(vector):
                        P.op(vector, "dve", ["ps_y"], ["ylo2"],
                             lambda v: v.tensor_copy(out=ylo2[:, :], in_=ps_y[64:128, :]))
                        P.op(vector, "dve", ["ps_y", "ylo2"], ["ytmp"],
                             lambda v: v.tensor_add(out=ytmp[:, :], in0=ps_y[0:64, :],
                                                    in1=ylo2[:, :]))
                        P.op(vector, "dve", ["ytmp", "rrep"], ["ynrm"],
                             lambda v: v.tensor_mul(out=ynrm[:, :], in0=ytmp[:, :], in1=rrep[:, :]))
                        P.op(vector, "dve", ["ynrm"], ["yhi"],
                             lambda v: v.tensor_copy(out=yhi[:, :], in_=ynrm[:, :]))
                        P.op(vector, "dve", ["ynrm", "yhi"], ["ylo"],
                             lambda v: v.tensor_sub(out=ylo[:, :], in0=ynrm[:, :], in1=yhi[:, :]))
                    blk.vector(_)

                    def _(g, head=head, s=s):
                        P.op(g, "pool", ["yhi"], ["yscr"],
                             lambda q: q.dma_start(
                                 out=yscr_hi[head * 64:(head + 1) * 64, s * QW:(s + 1) * QW],
                                 in_=yhi[:, :]), sem="gw_y", inc=16)
                        P.op(g, "pool", ["ylo"], ["yscr"],
                             lambda q: q.dma_start(
                                 out=yscr_lo[head * 64:(head + 1) * 64, s * QW:(s + 1) * QW],
                                 in_=ylo[:, :]), sem="gw_y", inc=16)
                    blk.gpsimd(_)

        def _(sync):
            sync.wait_ge(P.h["gw_y"], P.cnt["gw_y"])
        blk.sync(_)
    P.reset_sat()

    NST = TW // 128
    with contextlib.ExitStack() as es, nc.Block() as blk:
        wpr = [es.enter_context(nc.sbuf_tensor("wpr0", [128, D], F32))] * 2
        wph = es.enter_context(nc.sbuf_tensor("wph", [128, D], BF16))
        wpl = es.enter_context(nc.sbuf_tensor("wpl", [128, D], BF16))
        wpTh = es.enter_context(nc.sbuf_tensor("wpTh", [128, NDC * D], BF16))
        wpTl = es.enter_context(nc.sbuf_tensor("wpTl", [128, NDC * D], BF16))
        w2r = [es.enter_context(nc.sbuf_tensor("w2r0", [128, F1], F32))] * 2
        w2q = [es.enter_context(nc.sbuf_tensor(f"w2q{i}", [128, F1], BF16)) for i in range(2)]
        w2stg = es.enter_context(nc.sbuf_tensor("w2stg", [128, 512], BF16))
        w2row = [es.enter_context(nc.sbuf_tensor(f"w2row{i}", [128, NF1C * 128], BF16)) for i in range(2)]
        w2st = es.enter_context(nc.sbuf_tensor("w2st", [128, 8 * NDC], F32))
        b2sb = es.enter_context(nc.sbuf_tensor("b2sb", [128, NDC], F32))
        hq = es.enter_context(nc.sbuf_tensor("hq", [128, NF1C * TW], BF16))
        hwin = [es.enter_context(nc.sbuf_tensor(f"hwin{i}", [128, 2 * TW], F32)) for i in range(2)]
        tqa = [es.enter_context(nc.sbuf_tensor(f"tqa{i}", [128, 2 * TW], F32)) for i in range(2)]
        tqb = [es.enter_context(nc.sbuf_tensor(f"tqb{i}", [128, 2 * TW], F32)) for i in range(2)]
        ybh = es.enter_context(nc.sbuf_tensor("ybh", [128, NDC * TW], BF16))
        ybl = es.enter_context(nc.sbuf_tensor("ybl", [128, NDC * TW], BF16))
        sfm = es.enter_context(nc.sbuf_tensor("sfm", [128, TW], F32))
        prj = es.enter_context(nc.sbuf_tensor("prj", [128, TW], F32))
        outsb = es.enter_context(nc.sbuf_tensor("outsb", [128, NST * D], F32))
        xres = [es.enter_context(nc.sbuf_tensor(f"xres{i}", [128, D], F32)) for i in range(2)]
        outw = [es.enter_context(nc.sbuf_tensor(f"outw{i}", [128, D], F32)) for i in range(2)]
        ps_o1p = [es.enter_context(nc.psum_tensor(f"pso1_{i}", [128, TW], F32)) for i in range(2)]
        ps_o2p = [es.enter_context(nc.psum_tensor(f"pso2_{i}", [128, TW], F32)) for i in range(2)]
        ps_x2 = es.enter_context(nc.psum_tensor("psx2", [128, 512], BF16))
        ps_tt = es.enter_context(nc.psum_tensor("pstt", [128, 512], F32))

        def _(sync):
            b2r = b2.rearrange("(c p) -> p c", p=128)
            for j in range(NDC):
                P.op(sync, "sync", [], ["b2sb"],
                     lambda q, j=j: q.dma_start(out=b2sb[:, j:j + 1], in_=b2r[:, j:j + 1]),
                     sem="qb", inc=16)
        blk.sync(_)

        for dm in range(NDC):
            sl = dm % 2

            def _(sync, sl=sl, dm=dm):
                P.op(sync, "sync", [], ["wpr0"],
                     lambda q: q.dma_start(out=wpr[sl][:, :],
                                           in_=wproj[dm * 128:(dm + 1) * 128, :]),
                     sem=QWSEM[sl], inc=16)
            blk.sync(_)

            def _(vector, sl=sl):
                P.op(vector, "dve", ["wpr0"], ["wph"],
                     lambda v: v.tensor_copy(out=wph[:, :], in_=wpr[sl][:, :]))
                P.op(vector, "dve", ["wpr0", "wph"], ["wpl"],
                     lambda v: v.tensor_sub(out=wpl[:, :], in0=wpr[sl][:, :], in1=wph[:, :]))
            blk.vector(_)

            spr4 = min(4, NDC)
            for srcb, srcn, dstb, dstn in [(wph, "wph", wpTh, "wpTh"), (wpl, "wpl", wpTl, "wpTl")]:
                for r in range((NDC + spr4 - 1) // spr4):
                    nsub = min(spr4, NDC - r * spr4)

                    def _(tensor, srcb=srcb, srcn=srcn, r=r, nsub=nsub):
                        def e(te):
                            ii = None
                            for j in range(nsub):
                                fch = r * spr4 + j
                                ii = te.transpose(ps_x2[:, j * 128:(j + 1) * 128],
                                                  srcb[:, fch * 128:(fch + 1) * 128], idn16[:, :])
                            return ii
                        P.op(tensor, "pe", [srcn, "idn16r"], ["ps_x2"], e)
                    blk.tensor(_)

                    def _(scalar, dstb=dstb, dstn=dstn, dm=dm, r=r, nsub=nsub):
                        def e(a):
                            ii = None
                            for j in range(nsub):
                                fch = r * spr4 + j
                                cd = (fch * NDC + dm) * 128
                                ii = a.activation(out=dstb[:, cd:cd + 128],
                                                  in_=ps_x2[:, j * 128:(j + 1) * 128],
                                                  func=AF.Copy, scale=c_one[:, :], bias=0.0)
                            return ii
                        P.op(scalar, "act", ["ps_x2", "c_one"], [dstn], e)
                    blk.scalar(_)

        for dm in range(NDC):
            sl = dm % 2

            def _(sync, sl=sl, dm=dm):
                P.op(sync, "sync", [], ["w2r0"],
                     lambda q: q.dma_start(out=w2r[sl][:, :], in_=w2[dm * 128:(dm + 1) * 128, :]),
                     sem=QWSEM[sl], inc=16)
            blk.sync(_)

            def _(vector, sl=sl, dm=dm):
                W = w2st
                c0 = dm * 8
                P.op(vector, "dve", ["w2r0"], [f"w2aw{dm}"],
                     lambda v: v.tensor_reduce(out=W[:, c0:c0 + 1], in_=w2r[sl][:, :], axis=AX.X,
                                               op=ALU.max, apply_absolute_value=True))
                P.op(vector, "dve", [f"w2aw{dm}"], [f"w2sw{dm}"],
                     lambda v: v.tensor_scalar(out=W[:, c0 + 1:c0 + 2], in0=W[:, c0:c0 + 1],
                                               scalar1=1e-8, scalar2=1.0 / 127.0,
                                               op0=ALU.max, op1=ALU.mult))
                newton_recip(vector, W[:, c0 + 2:c0 + 3], f"w2rw{dm}", W[:, c0 + 1:c0 + 2],
                             f"w2sw{dm}", W[:, c0 + 3:c0 + 4], f"w2tmp{dm}")
                P.op(vector, "dve", ["w2r0", f"w2rw{dm}"], ["w2r0"],
                     lambda v: v.tensor_scalar(out=w2r[sl][:, :], in0=w2r[sl][:, :],
                                               scalar1=W[:, c0 + 2:c0 + 3], scalar2=MAGIC,
                                               op0=ALU.mult, op1=ALU.add))
                P.op(vector, "dve", ["w2r0"], ["w2r0"],
                     lambda v: v.tensor_scalar(out=w2r[sl][:, :], in0=w2r[sl][:, :],
                                               scalar1=MAGIC, scalar2=-128.0,
                                               op0=ALU.subtract, op1=ALU.max))
                P.op(vector, "dve", ["w2r0"], [f"w2q{sl}"],
                     lambda v: v.tensor_single_scalar(out=w2q[sl][:, :], in_=w2r[sl][:, :],
                                                      scalar=127.0, op=ALU.min))
                P.op(vector, "dve", [f"w2sw{dm}", "s2b"], [f"w2ssw{dm}"],
                     lambda v: v.tensor_mul(out=W[:, c0 + 4:c0 + 5], in0=W[:, c0 + 1:c0 + 2],
                                            in1=s2b[:, :]))
                newton_recip(vector, W[:, c0 + 5:c0 + 6], f"w2rssw{dm}", W[:, c0 + 4:c0 + 5],
                             f"w2ssw{dm}", W[:, c0 + 3:c0 + 4], f"w2tmp{dm}")
                P.op(vector, "dve", ["b2sb", f"w2rssw{dm}"], [f"w2bq{dm}"],
                     lambda v: v.scalar_tensor_tensor(out=W[:, c0 + 6:c0 + 7],
                                                      in0=b2sb[:, dm:dm + 1], scalar=1.0,
                                                      in1=W[:, c0 + 5:c0 + 6],
                                                      op0=ALU.mult, op1=ALU.mult))
                P.op(vector, "dve", [f"w2bq{dm}"], [f"w2bq{dm}"],
                     lambda v: v.tensor_scalar_add(out=W[:, c0 + 6:c0 + 7],
                                                   in0=W[:, c0 + 6:c0 + 7], scalar1=MAGIC))
                P.op(vector, "dve", [f"w2bq{dm}"], [f"w2bq{dm}"],
                     lambda v: v.tensor_scalar_sub(out=W[:, c0 + 6:c0 + 7],
                                                   in0=W[:, c0 + 6:c0 + 7], scalar1=MAGIC))
                P.op(vector, "dve", [f"w2bq{dm}", f"w2ssw{dm}"], [f"w2bqf{dm}"],
                     lambda v: v.tensor_mul(out=W[:, c0 + 6:c0 + 7], in0=W[:, c0 + 6:c0 + 7],
                                            in1=W[:, c0 + 4:c0 + 5]))
            blk.vector(_)

            spr5 = 4
            for r in range((NF1C + spr5 - 1) // spr5):
                nsub = min(spr5, NF1C - r * spr5)

                def _(tensor, sl=sl, r=r, nsub=nsub):
                    def e(te):
                        ii = None
                        for j in range(nsub):
                            fch = r * spr5 + j
                            ii = te.transpose(ps_x2[:, j * 128:(j + 1) * 128],
                                              w2q[sl][:, fch * 128:(fch + 1) * 128], idn16[:, :])
                        return ii
                    P.op(tensor, "pe", [f"w2q{sl}", "idn16r"], ["ps_x2"], e)
                blk.tensor(_)

                def _(scalar, r=r, nsub=nsub):
                    P.op(scalar, "act", ["ps_x2", "c_one"], ["w2stg"],
                         lambda a: a.activation(out=w2stg[:, 0:nsub * 128],
                                                in_=ps_x2[:, 0:nsub * 128],
                                                func=AF.Copy, scale=c_one[:, :], bias=0.0))
                blk.scalar(_)

                def _(g_, dm=dm, r=r, nsub=nsub):
                    for j in range(nsub):
                        fch = r * spr5 + j
                        P.op(g_, "pool", ["w2stg"], ["w2qTd"],
                             lambda q, j=j, fch=fch: q.dma_start(
                                 out=w2qTd[dm, :, fch * 128:(fch + 1) * 128],
                                 in_=w2stg[:, j * 128:(j + 1) * 128]),
                             sem="gw_y", inc=16)
                blk.gpsimd(_)

        for g in range(NOW):
            for fc0 in range(0, NF1C, 2):
                sl = (fc0 // 2) % 2

                def _(sync, sl=sl, fc0=fc0, g=g):
                    dv = hscr[fc0 * 128:(fc0 + 2) * 128,
                              g * TW:(g + 1) * TW].rearrange("(j p) c -> p j c", p=128)
                    P.op(sync, "sync", ["hscr"], [f"hwin{sl}"],
                         lambda q: q.dma_start(
                             out=hwin[sl][:, :].rearrange("p (j c) -> p j c", c=TW), in_=dv),
                         sem=QX[sl], inc=16)
                blk.sync(_)

                def _(vector, sl=sl, fc0=fc0):
                    P.op(vector, "dve", [f"hwin{sl}", "rs2b"], [f"tqa{sl}"],
                         lambda v: v.tensor_scalar(out=tqa[sl][:, :], in0=hwin[sl][:, :],
                                                   scalar1=rs2b[:, :], scalar2=MAGIC,
                                                   op0=ALU.mult, op1=ALU.add))
                    round_chain(vector, tqa[sl][:, :], f"tqa{sl}", tqb[sl][:, :], f"tqb{sl}",
                                0.0, 255.0, hq[:, fc0 * TW:(fc0 + 2) * TW], "hq")
                blk.vector(_)

            def _(sync, g=g):
                for fch in range(NDC):
                    P.op(sync, "sync", ["yscr"], ["ybh"],
                         lambda q, fch=fch: q.dma_start(
                             out=ybh[:, fch * TW:(fch + 1) * TW],
                             in_=yscr_hi[fch * 128:(fch + 1) * 128, g * TW:(g + 1) * TW]),
                         sem="qy0", inc=16)
                    P.op(sync, "sync", ["yscr"], ["ybl"],
                         lambda q, fch=fch: q.dma_start(
                             out=ybl[:, fch * TW:(fch + 1) * TW],
                             in_=yscr_lo[fch * 128:(fch + 1) * 128, g * TW:(g + 1) * TW]),
                         sem="qy1", inc=16)
            blk.sync(_)

            for dm in range(NDC):
                wsl2 = dm % 2
                po1, po1n = ps_o1p[dm % 2], f"ps_o1_{dm % 2}"
                po2, po2n = ps_o2p[dm % 2], f"ps_o2_{dm % 2}"

                def _(sync, g=g, dm=dm, wsl2=wsl2):
                    P.op(sync, "sync", ["w2qTd"], [f"w2row{wsl2}"],
                         lambda q: q.dma_start(out=w2row[wsl2][:, :], in_=w2qTd[dm, :, :]),
                         sem="qy0" if wsl2 == 0 else "qy1", inc=16)
                blk.sync(_)

                def _(tensor, g=g, dm=dm, wsl2=wsl2, po1=po1, po2=po2, po1n=po1n, po2n=po2n):
                    def e(te):
                        for fch in range(NDC):
                            cd = (fch * NDC + dm) * 128
                            te.matmul(po1[:, :], wpTh[:, cd:cd + 128],
                                      ybh[:, fch * TW:(fch + 1) * TW],
                                      start=(fch == 0), stop=False)
                            te.matmul(po1[:, :], wpTh[:, cd:cd + 128],
                                      ybl[:, fch * TW:(fch + 1) * TW], start=False, stop=False)
                            te.matmul(po1[:, :], wpTl[:, cd:cd + 128],
                                      ybh[:, fch * TW:(fch + 1) * TW],
                                      start=False, stop=(fch == NDC - 1))
                        ii = None
                        for fch in range(NF1C):
                            ii = te.matmul(po2[:, :], w2row[wsl2][:, fch * 128:(fch + 1) * 128],
                                           hq[:, fch * TW:(fch + 1) * TW],
                                           start=(fch == 0), stop=(fch == NF1C - 1))
                        return ii
                    P.op(tensor, "pe", ["wpTh", "wpTl", "ybh", "ybl", f"w2row{wsl2}", "hq"],
                         [po1n, po2n], e)
                blk.tensor(_)

                def _(scalar, dm=dm, po1=po1, po1n=po1n):
                    P.op(scalar, "act", [po1n, "c_one"], ["prj"],
                         lambda a: a.activation(out=prj[:, :], in_=po1[:, :],
                                                func=AF.Copy, scale=c_one[:, :], bias=0.0))
                blk.scalar(_)

                def _(vector, dm=dm, po2=po2, po2n=po2n):
                    c0 = dm * 8
                    P.op(vector, "dve", ["prj", po2n, f"w2ssw{dm}"], ["sfm"],
                         lambda v: v.scalar_tensor_tensor(out=sfm[:, :], in0=po2[:, :],
                                                          scalar=w2st[:, c0 + 4:c0 + 5],
                                                          in1=prj[:, :],
                                                          op0=ALU.mult, op1=ALU.add))
                    P.op(vector, "dve", ["sfm", f"w2bqf{dm}"], ["sfm"],
                         lambda v: v.tensor_scalar_add(out=sfm[:, :], in0=sfm[:, :],
                                                       scalar1=w2st[:, c0 + 6:c0 + 7]))
                blk.vector(_)

                def _(tensor, dm=dm):
                    def e(te):
                        ii = None
                        for st in range(NST):
                            ii = te.transpose(ps_tt[:, st * 128:(st + 1) * 128],
                                              sfm[:, st * 128:(st + 1) * 128], idn32[:, :])
                        return ii
                    P.op(tensor, "pe", ["sfm", "idn32r"], ["ps_tt"], e)
                blk.tensor(_)

                def _(scalar, dm=dm):
                    def e(a):
                        ii = None
                        for st in range(NST):
                            ii = a.activation(out=outsb[:, st * D + dm * 128: st * D + dm * 128 + 128],
                                              in_=ps_tt[:, st * 128:(st + 1) * 128],
                                              func=AF.Copy, scale=c_one[:, :], bias=0.0)
                        return ii
                    P.op(scalar, "act", ["ps_tt", "c_one"], ["outsb"], e)
                blk.scalar(_)

            for st in range(NST):
                rsl = (g * NST + st) % 2

                def _(sync, g=g, st=st, rsl=rsl):
                    row = g * TW + st * 128
                    P.op(sync, "sync", [], [f"xres{rsl}"],
                         lambda q: q.dma_start(out=xres[rsl][:, :], in_=xbo[row:row + 128, :]),
                         sem=QWSEM[rsl], inc=16)
                blk.sync(_)

                def _(vector, st=st, rsl=rsl):
                    P.op(vector, "dve", ["outsb", f"xres{rsl}"], [f"outw{rsl}"],
                         lambda v: v.tensor_add(out=outw[rsl][:, :],
                                                in0=outsb[:, st * D:(st + 1) * D],
                                                in1=xres[rsl][:, :]))
                blk.vector(_)

                def _(scalar, g=g, st=st, rsl=rsl):
                    row = g * TW + st * 128
                    P.op(scalar, "act", [f"outw{rsl}"], ["outd"],
                         lambda q: q.dma_start(out=out[row:row + 128, :], in_=outw[rsl][:, :]),
                         sem="oc", inc=16)
                blk.scalar(_)

        def _(sync):
            sync.wait_ge(P.h["oc"], P.cnt["oc"])
        blk.sync(_)

    return nc, c


_NC_CACHE = {}


def kernel(L0_attn_x, L0_attn_w_attn, L0_attn_w_proj, L0_ffn_w1, L0_ffn_b1,
           L0_ffn_w2, L0_ffn_b2):
    x = np.ascontiguousarray(L0_attn_x, dtype=np.float32)
    B, T, D = x.shape
    H = 16 if D == 1024 else D // 64
    F1 = L0_ffn_w1.shape[0]
    key = (T, D, H, F1)
    if key not in _NC_CACHE:
        _NC_CACHE[key] = build_nc(T, D, H, F1)
    nc, c = _NC_CACHE[key]
    CH, OWNT = c["CH"], c["OWNT"]

    wattn = np.ascontiguousarray(L0_attn_w_attn, dtype=np.float32)
    wproj = np.ascontiguousarray(L0_attn_w_proj, dtype=np.float32)
    w1 = np.ascontiguousarray(L0_ffn_w1, dtype=np.float32)
    b1 = np.ascontiguousarray(L0_ffn_b1, dtype=np.float32)
    w2 = np.ascontiguousarray(L0_ffn_w2, dtype=np.float32)
    b2 = np.ascontiguousarray(L0_ffn_b2, dtype=np.float32)
    masks = {p: _make_masks(c, p).astype(ml_dtypes.bfloat16) for p in (0, 1)}

    in_maps = []
    for core in range(8):
        b, p = core // 2, core % 2
        xbc = np.ascontiguousarray(x[b])
        own_rows = np.concatenate([np.arange(ch * CH, (ch + 1) * CH) for ch in c["chunks"][p]])
        xbo = np.ascontiguousarray(xbc[own_rows])
        in_maps.append({
            "xb": xbc, "xbo": xbo, "wattn": wattn, "wproj": wproj,
            "w1": w1, "b1": b1, "w2": w2, "b2": b2,
            "maskin": masks[p],
        })

    res = run_bass_kernel_spmd(nc, in_maps, core_ids=list(range(8)))
    outf = np.empty((B, T, D), dtype=np.float32)
    for core in range(8):
        b, p = core // 2, core % 2
        o = np.asarray(res.results[core]["out"])
        for k, ch in enumerate(c["chunks"][p]):
            outf[b, ch * CH:(ch + 1) * CH] = o[k * CH:(k + 1) * CH]
    return outf


# version 19
# speedup vs baseline: 1.0963x; 1.0963x over previous
import contextlib
import numpy as np
import ml_dtypes
import concourse.bass as bass
import concourse.mybir as mybir
from concourse.bass_utils import run_bass_kernel_spmd

F32 = mybir.dt.float32
BF16 = mybir.dt.bfloat16
AF = mybir.ActivationFunctionType
ALU = mybir.AluOpType
AX = mybir.AxisListType

MSTAB = 64.0
RG8 = [[0, 1, 2, 3, 4, 5, 6, 7]]


def _cfg(T, D, H, F1):
    c = dict(T=T, D=D, H=H, F1=F1)
    assert D == H * 64 and T % 512 == 0
    c["CH"] = T // 4
    c["OWNT"] = 2 * c["CH"]
    c["QW"] = min(512, c["CH"])
    c["TW"] = min(512, c["CH"])
    c["NDC"] = D // 128
    c["NF1C"] = F1 // 128
    c["NPAIR"] = H // 2
    c["chunks"] = {0: [0, 3], 1: [1, 2]}
    ntpg = c["QW"] // 128
    c["ntpg"] = ntpg
    slots = {}
    for p in (0, 1):
        sl = []
        for ch in c["chunks"][p]:
            t0 = ch * c["CH"] // 128
            for g in range((c["CH"] // 128) // ntpg):
                sl.append([t0 + g * ntpg + j for j in range(ntpg)])
        slots[p] = sl
    c["slots"] = slots
    c["NSLOT"] = len(slots[0])
    c["EPROF"] = [max(slots[0][s][-1], slots[1][s][-1]) + 1 for s in range(c["NSLOT"])]
    c["NPASS"] = sum(c["EPROF"])
    c["MINEF"] = [min(slots[0][s][-1], slots[1][s][-1]) + 1 - ntpg for s in range(c["NSLOT"])]
    c["MOFF"] = []
    acc = 0
    for s in range(c["NSLOT"]):
        c["MOFF"].append(acc)
        acc += c["EPROF"][s] - c["MINEF"][s]
    c["NMASK"] = acc
    return c


def _make_masks(c, p):
    QW = c["QW"]
    masks = np.zeros((c["NMASK"], 128, QW), dtype=np.float32)
    idx = 0
    for s in range(c["NSLOT"]):
        gtiles = c["slots"][p][s]
        for kt in range(c["MINEF"][s], c["EPROF"][s]):
            m = masks[idx]
            for j, qt in enumerate(gtiles):
                if kt < qt:
                    m[:, j * 128:(j + 1) * 128] = 1.0
                elif kt == qt:
                    kk = np.arange(128)[:, None]
                    qq = np.arange(128)[None, :]
                    m[:, j * 128:(j + 1) * 128] = (kk <= qq).astype(np.float32)
            idx += 1
    return masks


class Prog:
    ENGSEM = {"pe": "e_pe", "act": "e_act", "dve": "e_dve", "pool": "e_pool"}

    def __init__(self, nc, es):
        self.nc, self.es = nc, es
        self.h, self.cnt, self.sat = {}, {}, {}
        self.writers, self.users = {}, {}

    def sem(self, name):
        self.h[name] = self.es.enter_context(self.nc.semaphore(name))
        self.cnt[name] = 0

    def wait(self, eng, ename, sname, val):
        if val <= 0:
            return
        key = (ename, sname)
        if self.sat.get(key, 0) >= val:
            return
        eng.wait_ge(self.h[sname], val)
        self.sat[key] = val

    def op(self, eng, ename, reads, writes, emit_fn, sem=None, inc=1):
        for r in reads:
            for s, v in self.writers.get(r, {}).items():
                self.wait(eng, ename, s, v)
        for w in writes:
            for s, v in self.users.get(w, {}).items():
                self.wait(eng, ename, s, v)
        sname = sem or self.ENGSEM[ename]
        if sem is not None:
            self.wait(eng, ename, sname, self.cnt[sname])
        inst = emit_fn(eng)
        inst.then_inc(self.h[sname], inc)
        self.cnt[sname] += inc
        v = self.cnt[sname]
        for r in reads:
            self.users.setdefault(r, {})[sname] = v
        for w in writes:
            self.writers.setdefault(w, {})[sname] = v
            self.users.setdefault(w, {})[sname] = v
        return v

    def reset_sat(self):
        self.sat = {}


def build_nc(T=4096, D=1024, H=16, F1=4096):
    c = _cfg(T, D, H, F1)
    CH, OWNT, QW, TW = c["CH"], c["OWNT"], c["QW"], c["TW"]
    NDC, NF1C, NPAIR, NSLOT = c["NDC"], c["NF1C"], c["NPAIR"], c["NSLOT"]
    EPROF = c["EPROF"]
    NGT, NOT = T // 128, OWNT // 128

    nc = bass.Bass("TRN2", target_bir_lowering=False)

    xb = nc.dram_tensor("xb", [T, D], F32, kind="ExternalInput").ap()
    xbo = nc.dram_tensor("xbo", [OWNT, D], F32, kind="ExternalInput").ap()
    wattn = nc.dram_tensor("wattn", [3 * D, D], F32, kind="ExternalInput").ap()
    wproj = nc.dram_tensor("wproj", [D, D], F32, kind="ExternalInput").ap()
    w1 = nc.dram_tensor("w1", [F1, D], F32, kind="ExternalInput").ap()
    b1 = nc.dram_tensor("b1", [F1], F32, kind="ExternalInput").ap()
    w2 = nc.dram_tensor("w2", [D, F1], F32, kind="ExternalInput").ap()
    b2 = nc.dram_tensor("b2", [D], F32, kind="ExternalInput").ap()
    maskin = nc.dram_tensor("maskin", [max(1, c["NMASK"]), 128, QW], BF16, kind="ExternalInput").ap()
    out = nc.dram_tensor("out", [OWNT, D], F32, kind="ExternalOutput").ap()

    xnTg_hi = nc.dram_tensor("xnTg_hi", [D, T], BF16, kind="Internal").ap()
    xnTg_lo = nc.dram_tensor("xnTg_lo", [D, T], BF16, kind="Internal").ap()
    xnTo_hi = nc.dram_tensor("xnTo_hi", [D, OWNT], BF16, kind="Internal").ap()
    xnTo_lo = nc.dram_tensor("xnTo_lo", [D, OWNT], BF16, kind="Internal").ap()
    hscr = nc.dram_tensor("hscr", [F1, OWNT], F32, kind="Internal").ap()
    yscr_hi = nc.dram_tensor("yscr_hi", [D, OWNT], BF16, kind="Internal").ap()
    yscr_lo = nc.dram_tensor("yscr_lo", [D, OWNT], BF16, kind="Internal").ap()
    rbd = nc.dram_tensor("rbd", [H * NSLOT, QW], F32, kind="Internal").ap()
    w2qTd = nc.dram_tensor("w2qTd", [D // 128, 128, (F1 // 128) * 128], BF16, kind="Internal").ap()
    scd = nc.dram_tensor("scd", [8], F32, kind="Internal").ap()
    cc_in = nc.dram_tensor("cc_in", [1, 1], F32, kind="Internal").ap()
    cc_out = nc.dram_tensor("cc_out", [1, 1], F32, kind="Internal").ap()
    cc_in2 = nc.dram_tensor("cc_in2", [1, 1], F32, kind="Internal").ap()
    cc_out2 = nc.dram_tensor("cc_out2", [1, 1], F32, kind="Internal").ap()

    ges = contextlib.ExitStack()
    P = Prog(nc, ges)
    for s in ["qx0", "qx1", "qx2", "qx3", "qm0", "qm1", "qw0", "qw1", "qy0", "qy1",
              "qm2", "qm3", "qb", "gw_xnt", "gw_h", "gw_y", "gcc", "vr", "rr0", "oc",
              "e_pe", "e_act", "e_dve", "e_pool"]:
        P.sem(s)
    QX = ["qx0", "qx1", "qx2", "qx3"]
    QWSEM = ["qw0", "qw1"]

    idn32 = nc.alloc_sbuf_tensor("idn32", [128, 128], F32)
    idn16 = nc.alloc_sbuf_tensor("idn16", [128, 128], BF16)
    ones32 = nc.alloc_sbuf_tensor("ones32", [128, 128], F32)
    c_one = nc.alloc_sbuf_tensor("c_one", [128, 1], F32)
    c_lneps = nc.alloc_sbuf_tensor("c_lneps", [128, 1], F32)
    c_es = nc.alloc_sbuf_tensor("c_es", [128, 1], F32)
    c_eb = nc.alloc_sbuf_tensor("c_eb", [128, 1], F32)
    s1b = nc.alloc_sbuf_tensor("s1b", [128, 1], F32)
    rs1b = nc.alloc_sbuf_tensor("rs1b", [128, 1], F32)
    s2b = nc.alloc_sbuf_tensor("s2b", [128, 1], F32)
    rs2b = nc.alloc_sbuf_tensor("rs2b", [128, 1], F32)
    s1acc = nc.alloc_sbuf_tensor("s1acc", [128, 1], F32)
    s2acc = nc.alloc_sbuf_tensor("s2acc", [128, 1], F32)
    sscr = nc.alloc_sbuf_tensor("sscr", [128, 8], F32)

    def newton_recip(vector, dst, dstn, src, srcn, tmp, tmpn):
        P.op(vector, "dve", [srcn], [dstn],
             lambda v: v.reciprocal(out=dst, in_=src))
        P.op(vector, "dve", [srcn, dstn], [tmpn],
             lambda v: v.scalar_tensor_tensor(out=tmp, in0=src, scalar=-1.0,
                                              in1=dst, op0=ALU.mult, op1=ALU.mult))
        P.op(vector, "dve", [tmpn], [tmpn],
             lambda v: v.tensor_scalar_add(out=tmp, in0=tmp, scalar1=2.0))
        P.op(vector, "dve", [tmpn, dstn], [dstn],
             lambda v: v.tensor_mul(out=dst, in0=dst, in1=tmp))

    MAGIC = 12582912.0

    def round_chain(vector, buf, bufn, tmp, tmpn, lo, hi, outap, outn):
        P.op(vector, "dve", [bufn], [tmpn],
             lambda v: v.tensor_scalar(out=tmp, in0=buf, scalar1=MAGIC, scalar2=lo,
                                       op0=ALU.subtract, op1=ALU.max))
        P.op(vector, "dve", [tmpn], [outn],
             lambda v: v.tensor_single_scalar(out=outap, in_=tmp, scalar=hi, op=ALU.min))

    with nc.Block() as blk:
        def _(g):
            for t, v, n in [(ones32, 1.0, "ones32"), (c_one, 1.0, "c_one"),
                            (c_lneps, 1e-5, "c_lneps"), (c_es, 0.125, "c_es"),
                            (c_eb, -MSTAB, "c_eb"), (s1acc, -1e30, "s1acc"),
                            (s2acc, -1e30, "s2acc")]:
                P.op(g, "pool", [], [n], lambda gg, t=t, v=v: gg.memset(t[:], v))
            P.op(g, "pool", ["ones32"], ["idn32r"],
                 lambda gg: gg.affine_select(out=idn32[:, :], in_=ones32[:, :],
                                             pattern=[[-1, 128]], compare_op=ALU.is_equal,
                                             fill=0.0, base=0, channel_multiplier=1))
            P.op(g, "pool", ["idn32r"], ["idn16r"],
                 lambda gg: gg.tensor_copy(out=idn16[:, :], in_=idn32[:, :]))
        blk.gpsimd(_)
    P.reset_sat()

    with contextlib.ExitStack() as es, nc.Block() as blk:
        NLS = 4
        xrow = [es.enter_context(nc.sbuf_tensor(f"xrow{i}", [128, D], F32)) for i in range(NLS)]
        sq = [es.enter_context(nc.sbuf_tensor(f"sq{i}", [128, D], F32)) for i in range(NLS)]
        xn = [es.enter_context(nc.sbuf_tensor(f"xn{i}", [128, D], F32)) for i in range(NLS)]
        st_ = [es.enter_context(nc.sbuf_tensor(f"lnst{i}", [128, 8], F32)) for i in range(NLS)]
        hi_sb = [es.enter_context(nc.sbuf_tensor(f"hisb{i}", [128, 512], BF16)) for i in range(NLS)]
        lo_sb = [es.enter_context(nc.sbuf_tensor(f"losb{i}", [128, 512], BF16)) for i in range(NLS)]
        ps_t = [es.enter_context(nc.psum_tensor(f"lnps{i}", [128, 512], F32)) for i in range(NLS)]

        spr = min(4, NDC)
        tiles = [(xb, xnTg_hi, xnTg_lo, i) for i in range(NGT)] + \
                [(xbo, xnTo_hi, xnTo_lo, i) for i in range(NOT)]
        for idx, (src, dhi, dlo, i) in enumerate(tiles):
            sl = idx % NLS

            def _(sync, sl=sl, src=src, i=i):
                P.op(sync, "sync", [], [f"xrow{sl}"],
                     lambda q: q.dma_start(out=xrow[sl][:], in_=src[i * 128:(i + 1) * 128, :]),
                     sem=QX[sl], inc=16)
            blk.sync(_)

            def _(vector, sl=sl):
                P.op(vector, "dve", [f"xrow{sl}"], [f"msum{sl}"],
                     lambda v: v.reduce_sum(out=st_[sl][:, 0:1], in_=xrow[sl][:, :], axis=AX.X))
            blk.vector(_)

            def _(scalar, sl=sl):
                P.op(scalar, "act", [f"xrow{sl}"], [f"sq{sl}", f"sumsq{sl}"],
                     lambda a: a.activation(out=sq[sl][:, :], in_=xrow[sl][:, :], func=AF.Square,
                                            accum_out=st_[sl][:, 1:2]))
            blk.scalar(_)

            def _(vector, sl=sl):
                P.op(vector, "dve", [f"msum{sl}"], [f"m{sl}"],
                     lambda v: v.tensor_scalar_mul(out=st_[sl][:, 2:3], in0=st_[sl][:, 0:1],
                                                   scalar1=1.0 / D))
                P.op(vector, "dve", [f"sumsq{sl}"], [f"ex2{sl}"],
                     lambda v: v.tensor_scalar_mul(out=st_[sl][:, 3:4], in0=st_[sl][:, 1:2],
                                                   scalar1=1.0 / D))
                P.op(vector, "dve", [f"m{sl}"], [f"m2{sl}"],
                     lambda v: v.tensor_mul(out=st_[sl][:, 4:5], in0=st_[sl][:, 2:3],
                                            in1=st_[sl][:, 2:3]))
                P.op(vector, "dve", [f"ex2{sl}", f"m2{sl}"], [f"var{sl}"],
                     lambda v: v.tensor_sub(out=st_[sl][:, 5:6], in0=st_[sl][:, 3:4],
                                            in1=st_[sl][:, 4:5]))
            blk.vector(_)

            def _(scalar, sl=sl):
                P.op(scalar, "act", [f"var{sl}", "c_lneps", "c_one"], [f"sd{sl}"],
                     lambda a: a.activation(out=st_[sl][:, 6:7], in_=st_[sl][:, 5:6], func=AF.Sqrt,
                                            bias=c_lneps[:, :], scale=c_one[:, :]))
            blk.scalar(_)

            def _(vector, sl=sl):
                P.op(vector, "dve", [f"sd{sl}"], [f"rstd{sl}"],
                     lambda v: v.reciprocal(out=st_[sl][:, 6:7], in_=st_[sl][:, 6:7]))
                P.op(vector, "dve", [f"xrow{sl}", f"m{sl}", f"rstd{sl}"], [f"xn{sl}"],
                     lambda v: v.tensor_scalar(out=xn[sl][:, :], in0=xrow[sl][:, :],
                                               scalar1=st_[sl][:, 2:3], scalar2=st_[sl][:, 6:7],
                                               op0=ALU.subtract, op1=ALU.mult))
            blk.vector(_)

            def _(vector, sl=sl):
                P.op(vector, "dve", [f"xn{sl}"], [f"tmax{sl}"],
                     lambda v: v.reduce_max(out=sscr[:, sl:sl + 1], in_=xn[sl][:, :], axis=AX.X))
                P.op(vector, "dve", [f"tmax{sl}", "s1acc"], ["s1acc"],
                     lambda v: v.tensor_max(out=s1acc[:, :], in0=s1acc[:, :],
                                            in1=sscr[:, sl:sl + 1]))
            blk.vector(_)

            for r in range((NDC + spr - 1) // spr):
                nsub = min(spr, NDC - r * spr)

                def _(tensor, sl=sl, r=r, nsub=nsub):
                    def e(te):
                        ii = None
                        for j in range(nsub):
                            dch = r * spr + j
                            ii = te.transpose(ps_t[sl][:, j * 128:(j + 1) * 128],
                                              xn[sl][:, dch * 128:(dch + 1) * 128], idn32[:, :])
                        return ii
                    P.op(tensor, "pe", [f"xn{sl}", "idn32r"], [f"ps_t{sl}"], e)
                blk.tensor(_)

                def _(scalar, sl=sl, nsub=nsub):
                    P.op(scalar, "act", [f"ps_t{sl}", "c_one"], [f"hi_sb{sl}"],
                         lambda a: a.activation(out=hi_sb[sl][:, 0:nsub * 128],
                                                in_=ps_t[sl][:, 0:nsub * 128],
                                                func=AF.Copy, scale=c_one[:, :], bias=0.0))
                blk.scalar(_)

                def _(vector, sl=sl, nsub=nsub):
                    P.op(vector, "dve", [f"ps_t{sl}", f"hi_sb{sl}"], [f"lo_sb{sl}"],
                         lambda v: v.tensor_sub(out=lo_sb[sl][:, 0:nsub * 128],
                                                in0=ps_t[sl][:, 0:nsub * 128],
                                                in1=hi_sb[sl][:, 0:nsub * 128]))
                blk.vector(_)

                def _(g, sl=sl, r=r, i=i, nsub=nsub, dhi=dhi, dlo=dlo):
                    dview_hi = dhi[r * spr * 128:(r * spr + nsub) * 128,
                                   i * 128:(i + 1) * 128].rearrange("(j p) c -> p j c", p=128)
                    dview_lo = dlo[r * spr * 128:(r * spr + nsub) * 128,
                                   i * 128:(i + 1) * 128].rearrange("(j p) c -> p j c", p=128)
                    P.op(g, "pool", [f"hi_sb{sl}"], ["xnt_dram"],
                         lambda q: q.dma_start(
                             out=dview_hi,
                             in_=hi_sb[sl][:, 0:nsub * 128].rearrange("p (j c) -> p j c", c=128)),
                         sem="gw_xnt", inc=16)
                    P.op(g, "pool", [f"lo_sb{sl}"], ["xnt_dram"],
                         lambda q: q.dma_start(
                             out=dview_lo,
                             in_=lo_sb[sl][:, 0:nsub * 128].rearrange("p (j c) -> p j c", c=128)),
                         sem="gw_xnt", inc=16)
                blk.gpsimd(_)

        def _(g):
            P.op(g, "pool", ["s1acc"], ["s1f"],
                 lambda q: q.tensor_reduce(out=sscr[0:1, 2:3], in_=s1acc[:, :], axis=AX.C,
                                           op=ALU.max))
            P.op(g, "pool", ["s1f"], ["cc_in"],
                 lambda q: q.dma_start(out=cc_in[:, :], in_=sscr[0:1, 2:3]), sem="gcc", inc=16)
            P.op(g, "pool", ["cc_in"], ["cc_out"],
                 lambda q: q.collective_compute("AllReduce", ALU.max, replica_groups=RG8,
                                                ins=[cc_in[:, :]], outs=[cc_out[:, :]]))
            P.op(g, "pool", ["cc_out"], ["s1g"],
                 lambda q: q.dma_start(out=sscr[0:1, 3:4], in_=cc_out[:, :]), sem="gcc", inc=16)
        blk.gpsimd(_)

        def _(vector):
            P.op(vector, "dve", ["s1g"], ["s1v"],
                 lambda v: v.tensor_scalar(out=sscr[0:1, 4:5], in0=sscr[0:1, 3:4], scalar1=1e-8,
                                           scalar2=1.0 / 255.0, op0=ALU.max, op1=ALU.mult))
            newton_recip(vector, sscr[0:1, 5:6], "rs1v", sscr[0:1, 4:5], "s1v",
                         sscr[0:1, 6:7], "ntmp")
        blk.vector(_)

        def _(scalar):
            P.op(scalar, "act", ["s1v", "rs1v"], ["scd01"],
                 lambda a: a.dma_start(out=scd[4:6], in_=sscr[0:1, 4:6]), sem="vr", inc=16)
        blk.scalar(_)

        def _(sync):
            P.op(sync, "sync", ["scd01"], ["s1b"],
                 lambda q: q.dma_start(out=s1b[:, :], in_=scd[None, 4:5].broadcast_to([128, 1])),
                 sem="qb", inc=16)
            P.op(sync, "sync", ["scd01"], ["rs1b"],
                 lambda q: q.dma_start(out=rs1b[:, :], in_=scd[None, 5:6].broadcast_to([128, 1])),
                 sem="qb", inc=16)
            sync.wait_ge(P.h["qb"], P.cnt["qb"])
            sync.wait_ge(P.h["gw_xnt"], P.cnt["gw_xnt"])
        blk.sync(_)
    P.reset_sat()

    NOW = OWNT // TW
    with contextlib.ExitStack() as es, nc.Block() as blk:
        xqT = es.enter_context(nc.sbuf_tensor("xqT", [128, NDC * OWNT], BF16))
        xh = [es.enter_context(nc.sbuf_tensor(f"ffxh{i}", [128, TW], BF16)) for i in range(2)]
        xl = [es.enter_context(nc.sbuf_tensor(f"ffxl{i}", [128, TW], BF16)) for i in range(2)]
        tq = [es.enter_context(nc.sbuf_tensor(f"fftq{i}", [128, TW], F32)) for i in range(2)]
        tq2 = [es.enter_context(nc.sbuf_tensor(f"fftq2{i}", [128, TW], F32)) for i in range(2)]
        w1raw = [es.enter_context(nc.sbuf_tensor(f"w1raw{i}", [128, D], F32)) for i in range(2)]
        w1q = [es.enter_context(nc.sbuf_tensor(f"w1q{i}", [128, D], BF16)) for i in range(2)]
        w1qT = [es.enter_context(nc.sbuf_tensor(f"w1qT{i}", [128, D], BF16)) for i in range(2)]
        h_sb = [es.enter_context(nc.sbuf_tensor(f"hsb{i}", [128, TW], F32)) for i in range(4)]
        b1sb = es.enter_context(nc.sbuf_tensor("b1sb", [128, NF1C], F32))
        wst = [es.enter_context(nc.sbuf_tensor(f"ffwst{i}", [128, 10], F32)) for i in range(2)]
        hmx = es.enter_context(nc.sbuf_tensor("hmx", [128, 2], F32))
        ps_h = [es.enter_context(nc.psum_tensor(f"psh{i}", [128, TW], F32)) for i in range(4)]
        ps_w = es.enter_context(nc.psum_tensor("psw", [128, 512], BF16))

        def _(sync):
            b1r = b1.rearrange("(c p) -> p c", p=128)
            for j in range(NF1C):
                P.op(sync, "sync", [], ["b1sb"],
                     lambda q, j=j: q.dma_start(out=b1sb[:, j:j + 1], in_=b1r[:, j:j + 1]),
                     sem="qb", inc=16)
        blk.sync(_)

        for it, (dch, g) in enumerate([(d, g) for d in range(NDC) for g in range(NOW)]):
            sl = it % 2

            def _(sync, sl=sl, dch=dch, g=g):
                t0 = g * TW
                P.op(sync, "sync", ["xnt_dram"], [f"ffxh{sl}"],
                     lambda q: q.dma_start(out=xh[sl][:, :],
                                           in_=xnTo_hi[dch * 128:(dch + 1) * 128, t0:t0 + TW]),
                     sem=QX[sl], inc=16)
                P.op(sync, "sync", ["xnt_dram"], [f"ffxl{sl}"],
                     lambda q: q.dma_start(out=xl[sl][:, :],
                                           in_=xnTo_lo[dch * 128:(dch + 1) * 128, t0:t0 + TW]),
                     sem=QX[2 + sl], inc=16)
            blk.sync(_)

            def _(vector, sl=sl, dch=dch, g=g):
                P.op(vector, "dve", [f"ffxh{sl}", f"ffxl{sl}"], [f"fftq{sl}"],
                     lambda v: v.tensor_add(out=tq[sl][:, :], in0=xh[sl][:, :], in1=xl[sl][:, :]))
                P.op(vector, "dve", [f"fftq{sl}", "rs1b"], [f"fftq{sl}"],
                     lambda v: v.tensor_scalar(out=tq[sl][:, :], in0=tq[sl][:, :],
                                               scalar1=rs1b[:, :], scalar2=MAGIC,
                                               op0=ALU.mult, op1=ALU.add))
                cd = dch * OWNT + g * TW
                round_chain(vector, tq[sl][:, :], f"fftq{sl}", tq2[sl][:, :], f"fftq2{sl}",
                            0.0, 255.0, xqT[:, cd:cd + TW], "xqT")
            blk.vector(_)

        for fc in range(NF1C):
            sl = fc % 2

            def _(sync, sl=sl, fc=fc):
                P.op(sync, "sync", [], [f"w1raw{sl}"],
                     lambda q: q.dma_start(out=w1raw[sl][:, :], in_=w1[fc * 128:(fc + 1) * 128, :]),
                     sem=QWSEM[sl], inc=16)
            blk.sync(_)

            def _(vector, sl=sl, fc=fc):
                W = wst[sl]
                P.op(vector, "dve", [f"w1raw{sl}"], [f"aw{sl}"],
                     lambda v: v.tensor_reduce(out=W[:, 0:1], in_=w1raw[sl][:, :], axis=AX.X,
                                               op=ALU.max, apply_absolute_value=True))
                P.op(vector, "dve", [f"aw{sl}"], [f"s1w{sl}"],
                     lambda v: v.tensor_scalar(out=W[:, 1:2], in0=W[:, 0:1], scalar1=1e-8,
                                               scalar2=1.0 / 127.0, op0=ALU.max, op1=ALU.mult))
                newton_recip(vector, W[:, 2:3], f"rw{sl}", W[:, 1:2], f"s1w{sl}",
                             W[:, 3:4], f"wtmp{sl}")
                P.op(vector, "dve", [f"w1raw{sl}", f"rw{sl}"], [f"w1raw{sl}"],
                     lambda v: v.tensor_scalar(out=w1raw[sl][:, :], in0=w1raw[sl][:, :],
                                               scalar1=W[:, 2:3], scalar2=MAGIC,
                                               op0=ALU.mult, op1=ALU.add))
                P.op(vector, "dve", [f"w1raw{sl}"], [f"w1raw{sl}"],
                     lambda v: v.tensor_scalar(out=w1raw[sl][:, :], in0=w1raw[sl][:, :],
                                               scalar1=MAGIC, scalar2=-128.0,
                                               op0=ALU.subtract, op1=ALU.max))
                P.op(vector, "dve", [f"w1raw{sl}"], [f"w1q{sl}"],
                     lambda v: v.tensor_single_scalar(out=w1q[sl][:, :], in_=w1raw[sl][:, :],
                                                      scalar=127.0, op=ALU.min))
                P.op(vector, "dve", [f"s1w{sl}", "s1b"], [f"ssw{sl}"],
                     lambda v: v.tensor_mul(out=W[:, 4:5], in0=W[:, 1:2], in1=s1b[:, :]))
                newton_recip(vector, W[:, 5:6], f"rssw{sl}", W[:, 4:5], f"ssw{sl}",
                             W[:, 3:4], f"wtmp{sl}")
                P.op(vector, "dve", ["b1sb", f"rssw{sl}"], [f"bq{sl}"],
                     lambda v: v.scalar_tensor_tensor(out=W[:, 6:7], in0=b1sb[:, fc:fc + 1],
                                                      scalar=1.0, in1=W[:, 5:6],
                                                      op0=ALU.mult, op1=ALU.mult))
                P.op(vector, "dve", [f"bq{sl}"], [f"bq{sl}"],
                     lambda v: v.tensor_scalar_add(out=W[:, 6:7], in0=W[:, 6:7], scalar1=MAGIC))
                P.op(vector, "dve", [f"bq{sl}"], [f"bq{sl}"],
                     lambda v: v.tensor_scalar_sub(out=W[:, 6:7], in0=W[:, 6:7], scalar1=MAGIC))
                P.op(vector, "dve", [f"bq{sl}", f"ssw{sl}"], [f"bqf{sl}"],
                     lambda v: v.tensor_mul(out=W[:, 8:9], in0=W[:, 6:7], in1=W[:, 4:5]))
            blk.vector(_)

            spr2 = min(4, NDC)
            for r in range((NDC + spr2 - 1) // spr2):
                nsub = min(spr2, NDC - r * spr2)

                def _(tensor, sl=sl, r=r, nsub=nsub):
                    def e(te):
                        ii = None
                        for j in range(nsub):
                            dch = r * spr2 + j
                            ii = te.transpose(ps_w[:, j * 128:(j + 1) * 128],
                                              w1q[sl][:, dch * 128:(dch + 1) * 128], idn16[:, :])
                        return ii
                    P.op(tensor, "pe", [f"w1q{sl}", "idn16r"], ["ps_w"], e)
                blk.tensor(_)

                def _(scalar, sl=sl, r=r, nsub=nsub):
                    P.op(scalar, "act", ["ps_w", "c_one"], [f"w1qT{sl}"],
                         lambda a: a.activation(out=w1qT[sl][:, r * spr2 * 128:(r * spr2 + nsub) * 128],
                                                in_=ps_w[:, 0:nsub * 128], func=AF.Copy,
                                                scale=c_one[:, :], bias=0.0))
                blk.scalar(_)

            for g in range(NOW):
                psl = (fc * NOW + g) % 4

                def _(tensor, sl=sl, g=g, psl=psl):
                    def e(te):
                        ii = None
                        for j in range(NDC):
                            ii = te.matmul(ps_h[psl][:, :], w1qT[sl][:, j * 128:(j + 1) * 128],
                                           xqT[:, j * OWNT + g * TW: j * OWNT + g * TW + TW],
                                           start=(j == 0), stop=(j == NDC - 1))
                        return ii
                    P.op(tensor, "pe", [f"w1qT{sl}", "xqT"], [f"ps_h{psl}"], e)
                blk.tensor(_)

                def _(scalar, sl=sl, psl=psl):
                    P.op(scalar, "act", [f"ps_h{psl}", f"ssw{sl}", f"bqf{sl}"], [f"h_sb{psl}"],
                         lambda a: a.activation(out=h_sb[psl][:, :], in_=ps_h[psl][:, :],
                                                func=AF.Relu, scale=wst[sl][:, 4:5],
                                                bias=wst[sl][:, 8:9]))
                blk.scalar(_)

                def _(vector, psl=psl):
                    P.op(vector, "dve", [f"h_sb{psl}"], ["hm0"],
                         lambda v: v.reduce_max(out=hmx[:, 0:1], in_=h_sb[psl][:, :], axis=AX.X))
                    P.op(vector, "dve", ["hm0", "s2acc"], ["s2acc"],
                         lambda v: v.tensor_max(out=s2acc[:, :], in0=s2acc[:, :], in1=hmx[:, 0:1]))
                blk.vector(_)

                def _(g_, fc=fc, g=g, psl=psl):
                    P.op(g_, "pool", [f"h_sb{psl}"], ["hscr"],
                         lambda q: q.dma_start(out=hscr[fc * 128:(fc + 1) * 128,
                                                        g * TW:(g + 1) * TW],
                                               in_=h_sb[psl][:, :]),
                         sem="gw_h", inc=16)
                blk.gpsimd(_)

        def _(g):
            P.op(g, "pool", ["s2acc"], ["s2f"],
                 lambda q: q.tensor_reduce(out=sscr[0:1, 2:3], in_=s2acc[:, :], axis=AX.C,
                                           op=ALU.max))
            P.op(g, "pool", ["s2f"], ["cc_in2"],
                 lambda q: q.dma_start(out=cc_in2[:, :], in_=sscr[0:1, 2:3]), sem="gcc", inc=16)
            P.op(g, "pool", ["cc_in2"], ["cc_out2"],
                 lambda q: q.collective_compute("AllReduce", ALU.max, replica_groups=RG8,
                                                ins=[cc_in2[:, :]], outs=[cc_out2[:, :]]))
            P.op(g, "pool", ["cc_out2"], ["s2g"],
                 lambda q: q.dma_start(out=sscr[0:1, 3:4], in_=cc_out2[:, :]), sem="gcc", inc=16)
        blk.gpsimd(_)

        def _(vector):
            P.op(vector, "dve", ["s2g"], ["s2v"],
                 lambda v: v.tensor_scalar(out=sscr[0:1, 4:5], in0=sscr[0:1, 3:4], scalar1=1e-8,
                                           scalar2=1.0 / 255.0, op0=ALU.max, op1=ALU.mult))
            newton_recip(vector, sscr[0:1, 5:6], "rs2v", sscr[0:1, 4:5], "s2v",
                         sscr[0:1, 6:7], "ntmp2")
        blk.vector(_)

        def _(scalar):
            P.op(scalar, "act", ["s2v", "rs2v"], ["scd23"],
                 lambda a: a.dma_start(out=scd[6:8], in_=sscr[0:1, 4:6]), sem="vr", inc=16)
        blk.scalar(_)

        def _(sync):
            P.op(sync, "sync", ["scd23"], ["s2b"],
                 lambda q: q.dma_start(out=s2b[:, :], in_=scd[None, 6:7].broadcast_to([128, 1])),
                 sem="qb", inc=16)
            P.op(sync, "sync", ["scd23"], ["rs2b"],
                 lambda q: q.dma_start(out=rs2b[:, :], in_=scd[None, 7:8].broadcast_to([128, 1])),
                 sem="qb", inc=16)
            sync.wait_ge(P.h["qb"], P.cnt["qb"])
            sync.wait_ge(P.h["gw_h"], P.cnt["gw_h"])
        blk.sync(_)
    P.reset_sat()

    WN = 512
    NW = T // WN
    NOWq = OWNT // WN
    with contextlib.ExitStack() as es, nc.Block() as blk:
        qstk = [es.enter_context(nc.sbuf_tensor(f"qstk{h}", [128, OWNT], BF16)) for h in range(2)]
        qswp = [es.enter_context(nc.sbuf_tensor(f"qswp{h}", [128, OWNT], BF16)) for h in range(2)]
        kstk = [es.enter_context(nc.sbuf_tensor(f"kstk{h}", [128, T], BF16)) for h in range(2)]
        vstk = [es.enter_context(nc.sbuf_tensor(f"vstk{h}", [128, T], BF16)) for h in range(2)]
        wsl = [es.enter_context(nc.sbuf_tensor(f"wsl{i}", [128, D], F32)) for i in range(2)]
        whi = es.enter_context(nc.sbuf_tensor("whi", [128, D], BF16))
        wlo = es.enter_context(nc.sbuf_tensor("wlo", [128, D], BF16))
        wThi = [es.enter_context(nc.sbuf_tensor(f"wThi{o}", [128, D], BF16)) for o in range(3)]
        wTlo = [es.enter_context(nc.sbuf_tensor(f"wTlo{o}", [128, D], BF16)) for o in range(3)]
        NJH = max(1, NDC // 2)
        xwh = [es.enter_context(nc.sbuf_tensor(f"xwh{i}", [128, NJH * WN], BF16)) for i in range(2)]
        xwl = [es.enter_context(nc.sbuf_tensor(f"xwl{i}", [128, NJH * WN], BF16)) for i in range(2)]
        vf32 = es.enter_context(nc.sbuf_tensor("vf32", [128, WN], F32))
        em32 = [es.enter_context(nc.sbuf_tensor(f"em32_{i}", [128, QW], F32)) for i in range(8)]
        ebuf = [es.enter_context(nc.sbuf_tensor(f"ebuf{i}", [128, 2 * QW], BF16)) for i in range(8)]
        maskbuf = es.enter_context(nc.sbuf_tensor("maskbuf", [128, max(1, c["NMASK"]) * QW], BF16))
        onesk = es.enter_context(nc.sbuf_tensor("onesk", [128, 1], BF16))
        rinv = es.enter_context(nc.sbuf_tensor("rinv", [1, QW], F32))
        rrep = es.enter_context(nc.sbuf_tensor("rrep", [64, QW], F32))
        ytmp = es.enter_context(nc.sbuf_tensor("ytmp", [64, QW], F32))
        ylo2 = es.enter_context(nc.sbuf_tensor("ylo2", [64, QW], F32))
        ynrm = es.enter_context(nc.sbuf_tensor("ynrm", [64, QW], F32))
        yhi = es.enter_context(nc.sbuf_tensor("yhi", [64, QW], BF16))
        ylo = es.enter_context(nc.sbuf_tensor("ylo", [64, QW], BF16))
        ps_s = [es.enter_context(nc.psum_tensor(f"pss{i}", [128, QW], F32)) for i in range(2)]
        ps_y = es.enter_context(nc.psum_tensor("psy", [128, QW], F32))
        ps_q = es.enter_context(nc.psum_tensor("psq", [128, WN], F32))
        ps_k = es.enter_context(nc.psum_tensor("psk", [128, WN], F32))
        ps_v = es.enter_context(nc.psum_tensor("psv", [128, WN], F32))
        ps_x = es.enter_context(nc.psum_tensor("psx", [128, 512], F32))
        ps_xw = es.enter_context(nc.psum_tensor("psxw", [128, 512], BF16))

        def _(vector):
            P.op(vector, "dve", [], ["onesk"], lambda v: v.memset(onesk[:], 1.0))
        blk.vector(_)

        def _(sync):
            for mi in range(c["NMASK"]):
                P.op(sync, "sync", [], ["maskbuf"],
                     lambda q, mi=mi: q.dma_start(out=maskbuf[:, mi * QW:(mi + 1) * QW],
                                                  in_=maskin[mi, :, :]),
                     sem="qb", inc=16)
        blk.sync(_)

        MINEF, MOFF = c["MINEF"], c["MOFF"]
        qmsems = ["qm0", "qm1", "qm2", "qm3"]
        def emit_wprep(ph):
            for o in range(3):
                wi = o % 2
                rows = o * D + ph * 128

                def _(sync, wi=wi, rows=rows):
                    P.op(sync, "sync", [], [f"wsl{wi}"],
                         lambda q: q.dma_start(out=wsl[wi][:, :], in_=wattn[rows:rows + 128, :]),
                         sem=QWSEM[wi], inc=16)
                blk.sync(_)

                def _(vector, wi=wi):
                    P.op(vector, "dve", [f"wsl{wi}"], ["whi"],
                         lambda v: v.tensor_copy(out=whi[:, :], in_=wsl[wi][:, :]))
                    P.op(vector, "dve", [f"wsl{wi}", "whi"], ["wlo"],
                         lambda v: v.tensor_sub(out=wlo[:, :], in0=wsl[wi][:, :], in1=whi[:, :]))
                blk.vector(_)

                spr3 = min(4, NDC)
                for srcb, srcn, dstb, dstn in [(whi, "whi", wThi[o], f"wThi{o}"),
                                               (wlo, "wlo", wTlo[o], f"wTlo{o}")]:
                    for r in range((NDC + spr3 - 1) // spr3):
                        nsub = min(spr3, NDC - r * spr3)

                        def _(tensor, srcb=srcb, srcn=srcn, r=r, nsub=nsub):
                            def e(te):
                                ii = None
                                for j in range(nsub):
                                    dch = r * spr3 + j
                                    ii = te.transpose(ps_xw[:, j * 128:(j + 1) * 128],
                                                      srcb[:, dch * 128:(dch + 1) * 128],
                                                      idn16[:, :])
                                return ii
                            P.op(tensor, "pe", [srcn, "idn16r"], ["ps_xw"], e)
                        blk.tensor(_)

                        def _(scalar, dstb=dstb, dstn=dstn, r=r, nsub=nsub):
                            P.op(scalar, "act", ["ps_xw", "c_one"], [dstn],
                                 lambda a: a.activation(
                                     out=dstb[:, r * spr3 * 128:(r * spr3 + nsub) * 128],
                                     in_=ps_xw[:, 0:nsub * 128], func=AF.Copy,
                                     scale=c_one[:, :], bias=0.0))
                        blk.scalar(_)

        NHALF = NDC // NJH
        for ph in range(NPAIR):
            if ph == 0:
                emit_wprep(0)
            for tt in range(NW):
                for hf in range(NHALF):
                    xsl = (tt * NHALF + hf) % 2

                    def _(sync, tt=tt, hf=hf, xsl=xsl):
                        dv_h = xnTg_hi[hf * NJH * 128:(hf + 1) * NJH * 128,
                                       tt * WN:(tt + 1) * WN].rearrange("(j p) c -> p j c", p=128)
                        dv_l = xnTg_lo[hf * NJH * 128:(hf + 1) * NJH * 128,
                                       tt * WN:(tt + 1) * WN].rearrange("(j p) c -> p j c", p=128)
                        P.op(sync, "sync", ["xnt_dram"], [f"xwh{xsl}"],
                             lambda q: q.dma_start(
                                 out=xwh[xsl][:, :].rearrange("p (j c) -> p j c", c=WN), in_=dv_h),
                             sem=QX[xsl], inc=16)
                        P.op(sync, "sync", ["xnt_dram"], [f"xwl{xsl}"],
                             lambda q: q.dma_start(
                                 out=xwl[xsl][:, :].rearrange("p (j c) -> p j c", c=WN), in_=dv_l),
                             sem=QX[2 + xsl], inc=16)
                    blk.sync(_)

                    def _(tensor, hf=hf, xsl=xsl):
                        for jj in range(NJH):
                            j = hf * NJH + jj
                            for o, psd, psn in [(1, ps_k, "ps_k"), (2, ps_v, "ps_v")]:
                                def e(te, o=o, psd=psd, j=j, jj=jj):
                                    te.matmul(psd[:, :], wThi[o][:, j * 128:(j + 1) * 128],
                                              xwh[xsl][:, jj * WN:(jj + 1) * WN],
                                              start=(j == 0), stop=False)
                                    te.matmul(psd[:, :], wThi[o][:, j * 128:(j + 1) * 128],
                                              xwl[xsl][:, jj * WN:(jj + 1) * WN],
                                              start=False, stop=False)
                                    return te.matmul(psd[:, :], wTlo[o][:, j * 128:(j + 1) * 128],
                                                     xwh[xsl][:, jj * WN:(jj + 1) * WN],
                                                     start=False, stop=(j == NDC - 1))
                                P.op(tensor, "pe",
                                     [f"wThi{o}", f"wTlo{o}", f"xwh{xsl}", f"xwl{xsl}"], [psn], e)
                    blk.tensor(_)

                def _(scalar, tt=tt):
                    def e(a):
                        for h in range(2):
                            a.activation(out=kstk[h][0:64, tt * WN:(tt + 1) * WN],
                                         in_=ps_k[h * 64:(h + 1) * 64, :],
                                         func=AF.Copy, scale=c_one[0:64, :], bias=0.0)
                        return a.activation(out=vf32[:, :], in_=ps_v[:, :],
                                            func=AF.Copy, scale=c_one[:, :], bias=0.0)
                    P.op(scalar, "act", ["ps_k", "ps_v", "c_one"], ["kstk0", "kstk1", "vf32"], e)
                blk.scalar(_)

                def _(vector, tt=tt):
                    for h in range(2):
                        P.op(vector, "dve", ["ps_k", f"kstk{h}"], [f"kstk{h}"],
                             lambda v, h=h: v.tensor_sub(
                                 out=kstk[h][64:128, tt * WN:(tt + 1) * WN],
                                 in0=ps_k[h * 64:(h + 1) * 64, :],
                                 in1=kstk[h][0:64, tt * WN:(tt + 1) * WN]))
                blk.vector(_)

                nst = WN // 128

                def _(tensor, tt=tt, nst=nst):
                    def e(te):
                        ii = None
                        for stq in range(nst):
                            vps = ps_x if stq % 2 == 0 else ps_s[0]
                            ii = te.transpose(vps[:, (stq // 2) * 128:(stq // 2) * 128 + 128],
                                              vf32[:, stq * 128:(stq + 1) * 128], idn32[:, :])
                        return ii
                    P.op(tensor, "pe", ["vf32", "idn32r"], ["ps_x", "ps_s0"], e)
                blk.tensor(_)

                def _(scalar, tt=tt, nst=nst):
                    def e(a):
                        ii = None
                        for stq in range(nst):
                            vps = ps_x if stq % 2 == 0 else ps_s[0]
                            kt = tt * nst + stq
                            for h in range(2):
                                ii = a.activation(out=vstk[h][:, kt * 128:kt * 128 + 64],
                                                  in_=vps[:, (stq // 2) * 128 + h * 64:(stq // 2) * 128 + (h + 1) * 64],
                                                  func=AF.Copy, scale=c_one[:, :], bias=0.0)
                        return ii
                    P.op(scalar, "act", ["ps_x", "ps_s0", "c_one"], ["vstk0", "vstk1"], e)
                blk.scalar(_)

                def _(vector, tt=tt, nst=nst):
                    for stq in range(nst):
                        vps = ps_x if stq % 2 == 0 else ps_s[0]
                        vpsn = "ps_x" if stq % 2 == 0 else "ps_s0"
                        kt = tt * nst + stq
                        for h in range(2):
                            P.op(vector, "dve", [vpsn, f"vstk{h}"], [f"vstk{h}"],
                                 lambda v, h=h, vps=vps, stq=stq, kt=kt: v.tensor_sub(
                                     out=vstk[h][:, kt * 128 + 64:kt * 128 + 128],
                                     in0=vps[:, (stq // 2) * 128 + h * 64:(stq // 2) * 128 + (h + 1) * 64],
                                     in1=vstk[h][:, kt * 128:kt * 128 + 64]))
                blk.vector(_)

            for tt in range(NOWq):
                for hf in range(NHALF):
                    xsl = (tt * NHALF + hf) % 2

                    def _(sync, tt=tt, hf=hf, xsl=xsl):
                        dv_h = xnTo_hi[hf * NJH * 128:(hf + 1) * NJH * 128,
                                       tt * WN:(tt + 1) * WN].rearrange("(j p) c -> p j c", p=128)
                        dv_l = xnTo_lo[hf * NJH * 128:(hf + 1) * NJH * 128,
                                       tt * WN:(tt + 1) * WN].rearrange("(j p) c -> p j c", p=128)
                        P.op(sync, "sync", ["xnt_dram"], [f"xwh{xsl}"],
                             lambda q: q.dma_start(
                                 out=xwh[xsl][:, :].rearrange("p (j c) -> p j c", c=WN), in_=dv_h),
                             sem=QX[xsl], inc=16)
                        P.op(sync, "sync", ["xnt_dram"], [f"xwl{xsl}"],
                             lambda q: q.dma_start(
                                 out=xwl[xsl][:, :].rearrange("p (j c) -> p j c", c=WN), in_=dv_l),
                             sem=QX[2 + xsl], inc=16)
                    blk.sync(_)

                    def _(tensor, hf=hf, xsl=xsl):
                        for jj in range(NJH):
                            j = hf * NJH + jj

                            def e(te, j=j, jj=jj):
                                te.matmul(ps_q[:, :], wThi[0][:, j * 128:(j + 1) * 128],
                                          xwh[xsl][:, jj * WN:(jj + 1) * WN],
                                          start=(j == 0), stop=False)
                                te.matmul(ps_q[:, :], wThi[0][:, j * 128:(j + 1) * 128],
                                          xwl[xsl][:, jj * WN:(jj + 1) * WN],
                                          start=False, stop=False)
                                return te.matmul(ps_q[:, :], wTlo[0][:, j * 128:(j + 1) * 128],
                                                 xwh[xsl][:, jj * WN:(jj + 1) * WN],
                                                 start=False, stop=(j == NDC - 1))
                            P.op(tensor, "pe", ["wThi0", "wTlo0", f"xwh{xsl}", f"xwl{xsl}"],
                                 ["ps_q"], e)
                    blk.tensor(_)

                def _(scalar, tt=tt):
                    def e(a):
                        ii = None
                        for h in range(2):
                            ii = a.activation(out=qstk[h][0:64, tt * WN:(tt + 1) * WN],
                                              in_=ps_q[h * 64:(h + 1) * 64, :],
                                              func=AF.Copy, scale=c_one[0:64, :], bias=0.0)
                        return ii
                    P.op(scalar, "act", ["ps_q", "c_one"], ["qstk0", "qstk1"], e)
                blk.scalar(_)

                def _(vector, tt=tt):
                    for h in range(2):
                        P.op(vector, "dve", ["ps_q", f"qstk{h}"], [f"qstk{h}"],
                             lambda v, h=h: v.tensor_sub(
                                 out=qstk[h][64:128, tt * WN:(tt + 1) * WN],
                                 in0=ps_q[h * 64:(h + 1) * 64, :],
                                 in1=qstk[h][0:64, tt * WN:(tt + 1) * WN]))
                blk.vector(_)

                def _(vector, tt=tt):
                    for h in range(2):
                        P.op(vector, "dve", [f"qstk{h}"], [f"qswp{h}"],
                             lambda v, h=h: v.tensor_copy(
                                 out=qswp[h][0:64, tt * WN:(tt + 1) * WN],
                                 in_=qstk[h][64:128, tt * WN:(tt + 1) * WN]))
                        P.op(vector, "dve", [f"qstk{h}"], [f"qswp{h}"],
                             lambda v, h=h: v.tensor_copy(
                                 out=qswp[h][64:128, tt * WN:(tt + 1) * WN],
                                 in_=qstk[h][0:64, tt * WN:(tt + 1) * WN]))
                blk.vector(_)

            if ph + 1 < NPAIR:
                emit_wprep(ph + 1)

            SPS = [ps_s[0], ps_s[1], ps_q, ps_k]
            SPSN = ["ps_s0", "ps_s1", "ps_q", "ps_k"]
            for h in range(2):
                head = ph * 2 + h
                for s in range(NSLOT):
                    E = EPROF[s]
                    qc = s * QW
                    mef, mof = MINEF[s], MOFF[s]

                    def emit_av(kts, h=h, E=E):
                        def _(tensor, kts=kts):
                            for kt in kts:
                                i = kt % 8

                                def e(te, kt=kt, i=i):
                                    te.matmul(ps_y[:, :], vstk[h][:, kt * 128:(kt + 1) * 128],
                                              ebuf[i][:, 0:QW], start=(kt == 0), stop=False)
                                    te.matmul(ps_y[:, :], vstk[h][:, kt * 128:(kt + 1) * 128],
                                              ebuf[i][:, QW:2 * QW], start=False, stop=(kt == E - 1))
                                    te.matmul(ps_v[0:1, 0:QW], onesk[:, :], ebuf[i][:, 0:QW],
                                              start=(kt == 0), stop=False)
                                    return te.matmul(ps_v[0:1, 0:QW], onesk[:, :],
                                                     ebuf[i][:, QW:2 * QW],
                                                     start=False, stop=(kt == E - 1))
                                P.op(tensor, "pe", [f"vstk{h}", f"ebuf{i}", "onesk"],
                                     ["ps_y", "ps_v"], e)
                        blk.tensor(_)

                    pending = None
                    for k0 in range(0, E, 4):
                        kts = list(range(k0, min(k0 + 4, E)))

                        def _(tensor, h=h, kts=kts, qc=qc):
                            for kt in kts:
                                i = kt % 4

                                def e(te, kt=kt, i=i):
                                    te.matmul(SPS[i][:, 0:QW], kstk[h][:, kt * 128:(kt + 1) * 128],
                                              qstk[h][:, qc:qc + QW], start=True, stop=False)
                                    return te.matmul(SPS[i][:, 0:QW], kstk[h][:, kt * 128:(kt + 1) * 128],
                                                     qswp[h][:, qc:qc + QW], start=False, stop=True)
                                P.op(tensor, "pe", [f"kstk{h}", f"qstk{h}", f"qswp{h}"], [SPSN[i]], e)
                        blk.tensor(_)

                        def _(scalar, kts=kts):
                            for kt in kts:
                                i = kt % 4
                                i8 = kt % 8
                                P.op(scalar, "act", [SPSN[i], "c_es", "c_eb"], [f"em32_{i8}"],
                                     lambda a, i=i, i8=i8: a.activation(out=em32[i8][:, :],
                                                                        in_=SPS[i][:, 0:QW],
                                                                        func=AF.Exp, scale=c_es[:, :],
                                                                        bias=c_eb[:, :]))
                        blk.scalar(_)

                        if any(kt >= mef for kt in kts):
                            def _(vector, kts=kts, mef=mef, mof=mof):
                                for kt in kts:
                                    if kt >= mef:
                                        i8 = kt % 8
                                        mi = mof + kt - mef
                                        P.op(vector, "dve", ["maskbuf", f"em32_{i8}"], [f"em32_{i8}"],
                                             lambda v, i8=i8, mi=mi: v.scalar_tensor_tensor(
                                                 out=em32[i8][:, :], in0=em32[i8][:, :], scalar=1e38,
                                                 in1=maskbuf[:, mi * QW:(mi + 1) * QW],
                                                 op0=ALU.min, op1=ALU.mult))
                            blk.vector(_)

                        def _(vector, kts=kts):
                            for kt in kts:
                                i8 = kt % 8
                                P.op(vector, "dve", [f"em32_{i8}"], [f"ebuf{i8}"],
                                     lambda v, i8=i8: v.tensor_copy(out=ebuf[i8][:, 0:QW],
                                                                    in_=em32[i8][:, :]))
                                P.op(vector, "dve", [f"em32_{i8}", f"ebuf{i8}"], [f"ebuf{i8}"],
                                     lambda v, i8=i8: v.tensor_sub(out=ebuf[i8][:, QW:2 * QW],
                                                                   in0=em32[i8][:, :],
                                                                   in1=ebuf[i8][:, 0:QW]))
                        blk.vector(_)

                        if pending is not None:
                            emit_av(pending)
                        pending = kts
                    emit_av(pending)

                    def _(vector):
                        P.op(vector, "dve", ["ps_v"], ["rinv"],
                             lambda v: v.reciprocal(out=rinv[:, :], in_=ps_v[0:1, 0:QW]))
                    blk.vector(_)

                    def _(scalar, head=head, s=s):
                        P.op(scalar, "act", ["rinv"], ["rbd"],
                             lambda a: a.dma_start(out=rbd[head * NSLOT + s, :], in_=rinv[0:1, :]),
                             sem="vr", inc=16)
                        P.op(scalar, "act", ["rbd"], ["rrep"],
                             lambda q: q.dma_start(
                                 out=rrep[:, :],
                                 in_=rbd[head * NSLOT + s:head * NSLOT + s + 1, :]
                                 .broadcast_to([64, QW])),
                             sem="rr0", inc=16)
                    blk.scalar(_)

                    def _(vector):
                        P.op(vector, "dve", ["ps_y"], ["ylo2"],
                             lambda v: v.tensor_copy(out=ylo2[:, :], in_=ps_y[64:128, :]))
                        P.op(vector, "dve", ["ps_y", "ylo2"], ["ytmp"],
                             lambda v: v.tensor_add(out=ytmp[:, :], in0=ps_y[0:64, :],
                                                    in1=ylo2[:, :]))
                        P.op(vector, "dve", ["ytmp", "rrep"], ["ynrm"],
                             lambda v: v.tensor_mul(out=ynrm[:, :], in0=ytmp[:, :], in1=rrep[:, :]))
                        P.op(vector, "dve", ["ynrm"], ["yhi"],
                             lambda v: v.tensor_copy(out=yhi[:, :], in_=ynrm[:, :]))
                        P.op(vector, "dve", ["ynrm", "yhi"], ["ylo"],
                             lambda v: v.tensor_sub(out=ylo[:, :], in0=ynrm[:, :], in1=yhi[:, :]))
                    blk.vector(_)

                    def _(g, head=head, s=s):
                        P.op(g, "pool", ["yhi"], ["yscr"],
                             lambda q: q.dma_start(
                                 out=yscr_hi[head * 64:(head + 1) * 64, s * QW:(s + 1) * QW],
                                 in_=yhi[:, :]), sem="gw_y", inc=16)
                        P.op(g, "pool", ["ylo"], ["yscr"],
                             lambda q: q.dma_start(
                                 out=yscr_lo[head * 64:(head + 1) * 64, s * QW:(s + 1) * QW],
                                 in_=ylo[:, :]), sem="gw_y", inc=16)
                    blk.gpsimd(_)

        def _(sync):
            sync.wait_ge(P.h["gw_y"], P.cnt["gw_y"])
        blk.sync(_)
    P.reset_sat()

    NST = TW // 128
    with contextlib.ExitStack() as es, nc.Block() as blk:
        wpr = [es.enter_context(nc.sbuf_tensor("wpr0", [128, D], F32))] * 2
        wph = es.enter_context(nc.sbuf_tensor("wph", [128, D], BF16))
        wpl = es.enter_context(nc.sbuf_tensor("wpl", [128, D], BF16))
        wpTh = es.enter_context(nc.sbuf_tensor("wpTh", [128, NDC * D], BF16))
        wpTl = es.enter_context(nc.sbuf_tensor("wpTl", [128, NDC * D], BF16))
        w2r = [es.enter_context(nc.sbuf_tensor("w2r0", [128, F1], F32))] * 2
        w2q = [es.enter_context(nc.sbuf_tensor(f"w2q{i}", [128, F1], BF16)) for i in range(2)]
        w2stg = es.enter_context(nc.sbuf_tensor("w2stg", [128, 512], BF16))
        w2row = [es.enter_context(nc.sbuf_tensor(f"w2row{i}", [128, NF1C * 128], BF16)) for i in range(2)]
        w2st = es.enter_context(nc.sbuf_tensor("w2st", [128, 8 * NDC], F32))
        b2sb = es.enter_context(nc.sbuf_tensor("b2sb", [128, NDC], F32))
        hq = es.enter_context(nc.sbuf_tensor("hq", [128, NF1C * TW], BF16))
        hwin = [es.enter_context(nc.sbuf_tensor(f"hwin{i}", [128, 2 * TW], F32)) for i in range(2)]
        tqa = [es.enter_context(nc.sbuf_tensor(f"tqa{i}", [128, 2 * TW], F32)) for i in range(2)]
        tqb = [es.enter_context(nc.sbuf_tensor(f"tqb{i}", [128, 2 * TW], F32)) for i in range(2)]
        ybh = es.enter_context(nc.sbuf_tensor("ybh", [128, NDC * TW], BF16))
        ybl = es.enter_context(nc.sbuf_tensor("ybl", [128, NDC * TW], BF16))
        sfm = es.enter_context(nc.sbuf_tensor("sfm", [128, TW], F32))
        prj = es.enter_context(nc.sbuf_tensor("prj", [128, TW], F32))
        outsb = es.enter_context(nc.sbuf_tensor("outsb", [128, NST * D], F32))
        xres = [es.enter_context(nc.sbuf_tensor(f"xres{i}", [128, D], F32)) for i in range(2)]
        outw = [es.enter_context(nc.sbuf_tensor(f"outw{i}", [128, D], F32)) for i in range(2)]
        ps_o1p = [es.enter_context(nc.psum_tensor(f"pso1_{i}", [128, TW], F32)) for i in range(2)]
        ps_o2p = [es.enter_context(nc.psum_tensor(f"pso2_{i}", [128, TW], F32)) for i in range(2)]
        ps_x2 = es.enter_context(nc.psum_tensor("psx2", [128, 512], BF16))
        ps_tt = es.enter_context(nc.psum_tensor("pstt", [128, 512], F32))

        def _(sync):
            b2r = b2.rearrange("(c p) -> p c", p=128)
            for j in range(NDC):
                P.op(sync, "sync", [], ["b2sb"],
                     lambda q, j=j: q.dma_start(out=b2sb[:, j:j + 1], in_=b2r[:, j:j + 1]),
                     sem="qb", inc=16)
        blk.sync(_)

        for dm in range(NDC):
            sl = dm % 2

            def _(sync, sl=sl, dm=dm):
                P.op(sync, "sync", [], ["wpr0"],
                     lambda q: q.dma_start(out=wpr[sl][:, :],
                                           in_=wproj[dm * 128:(dm + 1) * 128, :]),
                     sem=QWSEM[sl], inc=16)
            blk.sync(_)

            def _(vector, sl=sl):
                P.op(vector, "dve", ["wpr0"], ["wph"],
                     lambda v: v.tensor_copy(out=wph[:, :], in_=wpr[sl][:, :]))
                P.op(vector, "dve", ["wpr0", "wph"], ["wpl"],
                     lambda v: v.tensor_sub(out=wpl[:, :], in0=wpr[sl][:, :], in1=wph[:, :]))
            blk.vector(_)

            spr4 = min(4, NDC)
            for srcb, srcn, dstb, dstn in [(wph, "wph", wpTh, "wpTh"), (wpl, "wpl", wpTl, "wpTl")]:
                for r in range((NDC + spr4 - 1) // spr4):
                    nsub = min(spr4, NDC - r * spr4)

                    def _(tensor, srcb=srcb, srcn=srcn, r=r, nsub=nsub):
                        def e(te):
                            ii = None
                            for j in range(nsub):
                                fch = r * spr4 + j
                                ii = te.transpose(ps_x2[:, j * 128:(j + 1) * 128],
                                                  srcb[:, fch * 128:(fch + 1) * 128], idn16[:, :])
                            return ii
                        P.op(tensor, "pe", [srcn, "idn16r"], ["ps_x2"], e)
                    blk.tensor(_)

                    def _(scalar, dstb=dstb, dstn=dstn, dm=dm, r=r, nsub=nsub):
                        def e(a):
                            ii = None
                            for j in range(nsub):
                                fch = r * spr4 + j
                                cd = (fch * NDC + dm) * 128
                                ii = a.activation(out=dstb[:, cd:cd + 128],
                                                  in_=ps_x2[:, j * 128:(j + 1) * 128],
                                                  func=AF.Copy, scale=c_one[:, :], bias=0.0)
                            return ii
                        P.op(scalar, "act", ["ps_x2", "c_one"], [dstn], e)
                    blk.scalar(_)

        for dm in range(NDC):
            sl = dm % 2

            def _(sync, sl=sl, dm=dm):
                P.op(sync, "sync", [], ["w2r0"],
                     lambda q: q.dma_start(out=w2r[sl][:, :], in_=w2[dm * 128:(dm + 1) * 128, :]),
                     sem=QWSEM[sl], inc=16)
            blk.sync(_)

            def _(vector, sl=sl, dm=dm):
                W = w2st
                c0 = dm * 8
                P.op(vector, "dve", ["w2r0"], [f"w2aw{dm}"],
                     lambda v: v.tensor_reduce(out=W[:, c0:c0 + 1], in_=w2r[sl][:, :], axis=AX.X,
                                               op=ALU.max, apply_absolute_value=True))
                P.op(vector, "dve", [f"w2aw{dm}"], [f"w2sw{dm}"],
                     lambda v: v.tensor_scalar(out=W[:, c0 + 1:c0 + 2], in0=W[:, c0:c0 + 1],
                                               scalar1=1e-8, scalar2=1.0 / 127.0,
                                               op0=ALU.max, op1=ALU.mult))
                newton_recip(vector, W[:, c0 + 2:c0 + 3], f"w2rw{dm}", W[:, c0 + 1:c0 + 2],
                             f"w2sw{dm}", W[:, c0 + 3:c0 + 4], f"w2tmp{dm}")
                P.op(vector, "dve", ["w2r0", f"w2rw{dm}"], ["w2r0"],
                     lambda v: v.tensor_scalar(out=w2r[sl][:, :], in0=w2r[sl][:, :],
                                               scalar1=W[:, c0 + 2:c0 + 3], scalar2=MAGIC,
                                               op0=ALU.mult, op1=ALU.add))
                P.op(vector, "dve", ["w2r0"], ["w2r0"],
                     lambda v: v.tensor_scalar(out=w2r[sl][:, :], in0=w2r[sl][:, :],
                                               scalar1=MAGIC, scalar2=-128.0,
                                               op0=ALU.subtract, op1=ALU.max))
                P.op(vector, "dve", ["w2r0"], [f"w2q{sl}"],
                     lambda v: v.tensor_single_scalar(out=w2q[sl][:, :], in_=w2r[sl][:, :],
                                                      scalar=127.0, op=ALU.min))
                P.op(vector, "dve", [f"w2sw{dm}", "s2b"], [f"w2ssw{dm}"],
                     lambda v: v.tensor_mul(out=W[:, c0 + 4:c0 + 5], in0=W[:, c0 + 1:c0 + 2],
                                            in1=s2b[:, :]))
                newton_recip(vector, W[:, c0 + 5:c0 + 6], f"w2rssw{dm}", W[:, c0 + 4:c0 + 5],
                             f"w2ssw{dm}", W[:, c0 + 3:c0 + 4], f"w2tmp{dm}")
                P.op(vector, "dve", ["b2sb", f"w2rssw{dm}"], [f"w2bq{dm}"],
                     lambda v: v.scalar_tensor_tensor(out=W[:, c0 + 6:c0 + 7],
                                                      in0=b2sb[:, dm:dm + 1], scalar=1.0,
                                                      in1=W[:, c0 + 5:c0 + 6],
                                                      op0=ALU.mult, op1=ALU.mult))
                P.op(vector, "dve", [f"w2bq{dm}"], [f"w2bq{dm}"],
                     lambda v: v.tensor_scalar_add(out=W[:, c0 + 6:c0 + 7],
                                                   in0=W[:, c0 + 6:c0 + 7], scalar1=MAGIC))
                P.op(vector, "dve", [f"w2bq{dm}"], [f"w2bq{dm}"],
                     lambda v: v.tensor_scalar_sub(out=W[:, c0 + 6:c0 + 7],
                                                   in0=W[:, c0 + 6:c0 + 7], scalar1=MAGIC))
                P.op(vector, "dve", [f"w2bq{dm}", f"w2ssw{dm}"], [f"w2bqf{dm}"],
                     lambda v: v.tensor_mul(out=W[:, c0 + 6:c0 + 7], in0=W[:, c0 + 6:c0 + 7],
                                            in1=W[:, c0 + 4:c0 + 5]))
            blk.vector(_)

            spr5 = 4
            for r in range((NF1C + spr5 - 1) // spr5):
                nsub = min(spr5, NF1C - r * spr5)

                def _(tensor, sl=sl, r=r, nsub=nsub):
                    def e(te):
                        ii = None
                        for j in range(nsub):
                            fch = r * spr5 + j
                            ii = te.transpose(ps_x2[:, j * 128:(j + 1) * 128],
                                              w2q[sl][:, fch * 128:(fch + 1) * 128], idn16[:, :])
                        return ii
                    P.op(tensor, "pe", [f"w2q{sl}", "idn16r"], ["ps_x2"], e)
                blk.tensor(_)

                def _(scalar, r=r, nsub=nsub):
                    P.op(scalar, "act", ["ps_x2", "c_one"], ["w2stg"],
                         lambda a: a.activation(out=w2stg[:, 0:nsub * 128],
                                                in_=ps_x2[:, 0:nsub * 128],
                                                func=AF.Copy, scale=c_one[:, :], bias=0.0))
                blk.scalar(_)

                def _(g_, dm=dm, r=r, nsub=nsub):
                    for j in range(nsub):
                        fch = r * spr5 + j
                        P.op(g_, "pool", ["w2stg"], ["w2qTd"],
                             lambda q, j=j, fch=fch: q.dma_start(
                                 out=w2qTd[dm, :, fch * 128:(fch + 1) * 128],
                                 in_=w2stg[:, j * 128:(j + 1) * 128]),
                             sem="gw_y", inc=16)
                blk.gpsimd(_)

        for g in range(NOW):
            for fc0 in range(0, NF1C, 2):
                sl = (fc0 // 2) % 2

                def _(sync, sl=sl, fc0=fc0, g=g):
                    dv = hscr[fc0 * 128:(fc0 + 2) * 128,
                              g * TW:(g + 1) * TW].rearrange("(j p) c -> p j c", p=128)
                    P.op(sync, "sync", ["hscr"], [f"hwin{sl}"],
                         lambda q: q.dma_start(
                             out=hwin[sl][:, :].rearrange("p (j c) -> p j c", c=TW), in_=dv),
                         sem=QX[sl], inc=16)
                blk.sync(_)

                def _(vector, sl=sl, fc0=fc0):
                    P.op(vector, "dve", [f"hwin{sl}", "rs2b"], [f"tqa{sl}"],
                         lambda v: v.tensor_scalar(out=tqa[sl][:, :], in0=hwin[sl][:, :],
                                                   scalar1=rs2b[:, :], scalar2=MAGIC,
                                                   op0=ALU.mult, op1=ALU.add))
                    round_chain(vector, tqa[sl][:, :], f"tqa{sl}", tqb[sl][:, :], f"tqb{sl}",
                                0.0, 255.0, hq[:, fc0 * TW:(fc0 + 2) * TW], "hq")
                blk.vector(_)

            def _(sync, g=g):
                for fch in range(NDC):
                    P.op(sync, "sync", ["yscr"], ["ybh"],
                         lambda q, fch=fch: q.dma_start(
                             out=ybh[:, fch * TW:(fch + 1) * TW],
                             in_=yscr_hi[fch * 128:(fch + 1) * 128, g * TW:(g + 1) * TW]),
                         sem="qy0", inc=16)
                    P.op(sync, "sync", ["yscr"], ["ybl"],
                         lambda q, fch=fch: q.dma_start(
                             out=ybl[:, fch * TW:(fch + 1) * TW],
                             in_=yscr_lo[fch * 128:(fch + 1) * 128, g * TW:(g + 1) * TW]),
                         sem="qy1", inc=16)
            blk.sync(_)

            for dm in range(NDC):
                wsl2 = dm % 2
                po1, po1n = ps_o1p[dm % 2], f"ps_o1_{dm % 2}"
                po2, po2n = ps_o2p[dm % 2], f"ps_o2_{dm % 2}"

                def _(sync, g=g, dm=dm, wsl2=wsl2):
                    P.op(sync, "sync", ["w2qTd"], [f"w2row{wsl2}"],
                         lambda q: q.dma_start(out=w2row[wsl2][:, :], in_=w2qTd[dm, :, :]),
                         sem="qy0" if wsl2 == 0 else "qy1", inc=16)
                blk.sync(_)

                def _(tensor, g=g, dm=dm, wsl2=wsl2, po1=po1, po2=po2, po1n=po1n, po2n=po2n):
                    def e(te):
                        for fch in range(NDC):
                            cd = (fch * NDC + dm) * 128
                            te.matmul(po1[:, :], wpTh[:, cd:cd + 128],
                                      ybh[:, fch * TW:(fch + 1) * TW],
                                      start=(fch == 0), stop=False)
                            te.matmul(po1[:, :], wpTh[:, cd:cd + 128],
                                      ybl[:, fch * TW:(fch + 1) * TW], start=False, stop=False)
                            te.matmul(po1[:, :], wpTl[:, cd:cd + 128],
                                      ybh[:, fch * TW:(fch + 1) * TW],
                                      start=False, stop=(fch == NDC - 1))
                        ii = None
                        for fch in range(NF1C):
                            ii = te.matmul(po2[:, :], w2row[wsl2][:, fch * 128:(fch + 1) * 128],
                                           hq[:, fch * TW:(fch + 1) * TW],
                                           start=(fch == 0), stop=(fch == NF1C - 1))
                        return ii
                    P.op(tensor, "pe", ["wpTh", "wpTl", "ybh", "ybl", f"w2row{wsl2}", "hq"],
                         [po1n, po2n], e)
                blk.tensor(_)

                def _(scalar, dm=dm, po1=po1, po1n=po1n):
                    P.op(scalar, "act", [po1n, "c_one"], ["prj"],
                         lambda a: a.activation(out=prj[:, :], in_=po1[:, :],
                                                func=AF.Copy, scale=c_one[:, :], bias=0.0))
                blk.scalar(_)

                def _(vector, dm=dm, po2=po2, po2n=po2n):
                    c0 = dm * 8
                    P.op(vector, "dve", ["prj", po2n, f"w2ssw{dm}"], ["sfm"],
                         lambda v: v.scalar_tensor_tensor(out=sfm[:, :], in0=po2[:, :],
                                                          scalar=w2st[:, c0 + 4:c0 + 5],
                                                          in1=prj[:, :],
                                                          op0=ALU.mult, op1=ALU.add))
                    P.op(vector, "dve", ["sfm", f"w2bqf{dm}"], ["sfm"],
                         lambda v: v.tensor_scalar_add(out=sfm[:, :], in0=sfm[:, :],
                                                       scalar1=w2st[:, c0 + 6:c0 + 7]))
                blk.vector(_)

                def _(tensor, dm=dm):
                    def e(te):
                        ii = None
                        for st in range(NST):
                            ii = te.transpose(ps_tt[:, st * 128:(st + 1) * 128],
                                              sfm[:, st * 128:(st + 1) * 128], idn32[:, :])
                        return ii
                    P.op(tensor, "pe", ["sfm", "idn32r"], ["ps_tt"], e)
                blk.tensor(_)

                def _(scalar, dm=dm):
                    def e(a):
                        ii = None
                        for st in range(NST):
                            ii = a.activation(out=outsb[:, st * D + dm * 128: st * D + dm * 128 + 128],
                                              in_=ps_tt[:, st * 128:(st + 1) * 128],
                                              func=AF.Copy, scale=c_one[:, :], bias=0.0)
                        return ii
                    P.op(scalar, "act", ["ps_tt", "c_one"], ["outsb"], e)
                blk.scalar(_)

            for st in range(NST):
                rsl = (g * NST + st) % 2

                def _(sync, g=g, st=st, rsl=rsl):
                    row = g * TW + st * 128
                    P.op(sync, "sync", [], [f"xres{rsl}"],
                         lambda q: q.dma_start(out=xres[rsl][:, :], in_=xbo[row:row + 128, :]),
                         sem=QWSEM[rsl], inc=16)
                blk.sync(_)

                def _(vector, st=st, rsl=rsl):
                    P.op(vector, "dve", ["outsb", f"xres{rsl}"], [f"outw{rsl}"],
                         lambda v: v.tensor_add(out=outw[rsl][:, :],
                                                in0=outsb[:, st * D:(st + 1) * D],
                                                in1=xres[rsl][:, :]))
                blk.vector(_)

                def _(scalar, g=g, st=st, rsl=rsl):
                    row = g * TW + st * 128
                    P.op(scalar, "act", [f"outw{rsl}"], ["outd"],
                         lambda q: q.dma_start(out=out[row:row + 128, :], in_=outw[rsl][:, :]),
                         sem="oc", inc=16)
                blk.scalar(_)

        def _(sync):
            sync.wait_ge(P.h["oc"], P.cnt["oc"])
        blk.sync(_)

    return nc, c


_NC_CACHE = {}


def kernel(L0_attn_x, L0_attn_w_attn, L0_attn_w_proj, L0_ffn_w1, L0_ffn_b1,
           L0_ffn_w2, L0_ffn_b2):
    x = np.ascontiguousarray(L0_attn_x, dtype=np.float32)
    B, T, D = x.shape
    H = 16 if D == 1024 else D // 64
    F1 = L0_ffn_w1.shape[0]
    key = (T, D, H, F1)
    if key not in _NC_CACHE:
        _NC_CACHE[key] = build_nc(T, D, H, F1)
    nc, c = _NC_CACHE[key]
    CH, OWNT = c["CH"], c["OWNT"]

    wattn = np.ascontiguousarray(L0_attn_w_attn, dtype=np.float32)
    wproj = np.ascontiguousarray(L0_attn_w_proj, dtype=np.float32)
    w1 = np.ascontiguousarray(L0_ffn_w1, dtype=np.float32)
    b1 = np.ascontiguousarray(L0_ffn_b1, dtype=np.float32)
    w2 = np.ascontiguousarray(L0_ffn_w2, dtype=np.float32)
    b2 = np.ascontiguousarray(L0_ffn_b2, dtype=np.float32)
    masks = {p: _make_masks(c, p).astype(ml_dtypes.bfloat16) for p in (0, 1)}

    in_maps = []
    for core in range(8):
        b, p = core // 2, core % 2
        xbc = np.ascontiguousarray(x[b])
        own_rows = np.concatenate([np.arange(ch * CH, (ch + 1) * CH) for ch in c["chunks"][p]])
        xbo = np.ascontiguousarray(xbc[own_rows])
        in_maps.append({
            "xb": xbc, "xbo": xbo, "wattn": wattn, "wproj": wproj,
            "w1": w1, "b1": b1, "w2": w2, "b2": b2,
            "maskin": masks[p],
        })

    res = run_bass_kernel_spmd(nc, in_maps, core_ids=list(range(8)))
    outf = np.empty((B, T, D), dtype=np.float32)
    for core in range(8):
        b, p = core // 2, core % 2
        o = np.asarray(res.results[core]["out"])
        for k, ch in enumerate(c["chunks"][p]):
            outf[b, ch * CH:(ch + 1) * CH] = o[k * CH:(k + 1) * CH]
    return outf


# version 20
# speedup vs baseline: 1.1202x; 1.0219x over previous
import contextlib
import numpy as np
import ml_dtypes
import concourse.bass as bass
import concourse.mybir as mybir
from concourse.bass_utils import run_bass_kernel_spmd

F32 = mybir.dt.float32
BF16 = mybir.dt.bfloat16
AF = mybir.ActivationFunctionType
ALU = mybir.AluOpType
AX = mybir.AxisListType

MSTAB = 64.0
RG8 = [[0, 1, 2, 3, 4, 5, 6, 7]]


def _cfg(T, D, H, F1):
    c = dict(T=T, D=D, H=H, F1=F1)
    assert D == H * 64 and T % 512 == 0
    c["CH"] = T // 4
    c["OWNT"] = 2 * c["CH"]
    c["QW"] = min(512, c["CH"])
    c["TW"] = min(512, c["CH"])
    c["NDC"] = D // 128
    c["NF1C"] = F1 // 128
    c["NPAIR"] = H // 2
    c["chunks"] = {0: [0, 3], 1: [1, 2]}
    ntpg = c["QW"] // 128
    c["ntpg"] = ntpg
    slots = {}
    for p in (0, 1):
        sl = []
        for ch in c["chunks"][p]:
            t0 = ch * c["CH"] // 128
            for g in range((c["CH"] // 128) // ntpg):
                sl.append([t0 + g * ntpg + j for j in range(ntpg)])
        slots[p] = sl
    c["slots"] = slots
    c["NSLOT"] = len(slots[0])
    c["EPROF"] = [max(slots[0][s][-1], slots[1][s][-1]) + 1 for s in range(c["NSLOT"])]
    c["NPASS"] = sum(c["EPROF"])
    c["MINEF"] = [min(slots[0][s][-1], slots[1][s][-1]) + 1 - ntpg for s in range(c["NSLOT"])]
    c["MOFF"] = []
    acc = 0
    for s in range(c["NSLOT"]):
        c["MOFF"].append(acc)
        acc += c["EPROF"][s] - c["MINEF"][s]
    c["NMASK"] = acc
    return c


def _make_masks(c, p):
    QW = c["QW"]
    masks = np.zeros((c["NMASK"], 128, QW), dtype=np.float32)
    idx = 0
    for s in range(c["NSLOT"]):
        gtiles = c["slots"][p][s]
        for kt in range(c["MINEF"][s], c["EPROF"][s]):
            m = masks[idx]
            for j, qt in enumerate(gtiles):
                if kt < qt:
                    m[:, j * 128:(j + 1) * 128] = 1.0
                elif kt == qt:
                    kk = np.arange(128)[:, None]
                    qq = np.arange(128)[None, :]
                    m[:, j * 128:(j + 1) * 128] = (kk <= qq).astype(np.float32)
            idx += 1
    return masks


class Prog:
    ENGSEM = {"pe": "e_pe", "act": "e_act", "dve": "e_dve", "pool": "e_pool"}

    def __init__(self, nc, es):
        self.nc, self.es = nc, es
        self.h, self.cnt, self.sat = {}, {}, {}
        self.writers, self.users = {}, {}

    def sem(self, name):
        self.h[name] = self.es.enter_context(self.nc.semaphore(name))
        self.cnt[name] = 0

    def wait(self, eng, ename, sname, val):
        if val <= 0:
            return
        key = (ename, sname)
        if self.sat.get(key, 0) >= val:
            return
        eng.wait_ge(self.h[sname], val)
        self.sat[key] = val

    def op(self, eng, ename, reads, writes, emit_fn, sem=None, inc=1):
        for r in reads:
            for s, v in self.writers.get(r, {}).items():
                self.wait(eng, ename, s, v)
        for w in writes:
            for s, v in self.users.get(w, {}).items():
                self.wait(eng, ename, s, v)
        sname = sem or self.ENGSEM[ename]
        if sem is not None:
            self.wait(eng, ename, sname, self.cnt[sname])
        inst = emit_fn(eng)
        inst.then_inc(self.h[sname], inc)
        self.cnt[sname] += inc
        v = self.cnt[sname]
        for r in reads:
            self.users.setdefault(r, {})[sname] = v
        for w in writes:
            self.writers.setdefault(w, {})[sname] = v
            self.users.setdefault(w, {})[sname] = v
        return v

    def reset_sat(self):
        self.sat = {}


def build_nc(T=4096, D=1024, H=16, F1=4096):
    c = _cfg(T, D, H, F1)
    CH, OWNT, QW, TW = c["CH"], c["OWNT"], c["QW"], c["TW"]
    NDC, NF1C, NPAIR, NSLOT = c["NDC"], c["NF1C"], c["NPAIR"], c["NSLOT"]
    EPROF = c["EPROF"]
    NGT, NOT = T // 128, OWNT // 128

    nc = bass.Bass("TRN2", target_bir_lowering=False)

    xb = nc.dram_tensor("xb", [T, D], F32, kind="ExternalInput").ap()
    xbo = nc.dram_tensor("xbo", [OWNT, D], F32, kind="ExternalInput").ap()
    wattn = nc.dram_tensor("wattn", [3 * D, D], F32, kind="ExternalInput").ap()
    wproj = nc.dram_tensor("wproj", [D, D], F32, kind="ExternalInput").ap()
    w1 = nc.dram_tensor("w1", [F1, D], F32, kind="ExternalInput").ap()
    b1 = nc.dram_tensor("b1", [F1], F32, kind="ExternalInput").ap()
    w2 = nc.dram_tensor("w2", [D, F1], F32, kind="ExternalInput").ap()
    b2 = nc.dram_tensor("b2", [D], F32, kind="ExternalInput").ap()
    maskin = nc.dram_tensor("maskin", [max(1, c["NMASK"]), 128, QW], BF16, kind="ExternalInput").ap()
    out = nc.dram_tensor("out", [OWNT, D], F32, kind="ExternalOutput").ap()

    xnTg_hi = nc.dram_tensor("xnTg_hi", [D, T], BF16, kind="Internal").ap()
    xnTg_lo = nc.dram_tensor("xnTg_lo", [D, T], BF16, kind="Internal").ap()
    xnTo_hi = nc.dram_tensor("xnTo_hi", [D, OWNT], BF16, kind="Internal").ap()
    xnTo_lo = nc.dram_tensor("xnTo_lo", [D, OWNT], BF16, kind="Internal").ap()
    hscr = nc.dram_tensor("hscr", [F1, OWNT], F32, kind="Internal").ap()
    yscr_hi = nc.dram_tensor("yscr_hi", [D, OWNT], BF16, kind="Internal").ap()
    yscr_lo = nc.dram_tensor("yscr_lo", [D, OWNT], BF16, kind="Internal").ap()
    rbd = nc.dram_tensor("rbd", [H * NSLOT, QW], F32, kind="Internal").ap()
    w2qTd = nc.dram_tensor("w2qTd", [D // 128, 128, (F1 // 128) * 128], BF16, kind="Internal").ap()
    scd = nc.dram_tensor("scd", [8], F32, kind="Internal").ap()
    cc_in = nc.dram_tensor("cc_in", [1, 1], F32, kind="Internal").ap()
    cc_out = nc.dram_tensor("cc_out", [1, 1], F32, kind="Internal").ap()
    cc_in2 = nc.dram_tensor("cc_in2", [1, 1], F32, kind="Internal").ap()
    cc_out2 = nc.dram_tensor("cc_out2", [1, 1], F32, kind="Internal").ap()

    ges = contextlib.ExitStack()
    P = Prog(nc, ges)
    for s in ["qx0", "qx1", "qx2", "qx3", "qm0", "qm1", "qw0", "qw1", "qy0", "qy1",
              "qm2", "qm3", "qb", "gw_xnt", "gw_h", "gw_y", "gcc", "vr", "rr0", "oc",
              "e_pe", "e_act", "e_dve", "e_pool"]:
        P.sem(s)
    QX = ["qx0", "qx1", "qx2", "qx3"]
    QWSEM = ["qw0", "qw1"]

    idn32 = nc.alloc_sbuf_tensor("idn32", [128, 128], F32)
    idn16 = nc.alloc_sbuf_tensor("idn16", [128, 128], BF16)
    ones32 = nc.alloc_sbuf_tensor("ones32", [128, 128], F32)
    c_one = nc.alloc_sbuf_tensor("c_one", [128, 1], F32)
    c_lneps = nc.alloc_sbuf_tensor("c_lneps", [128, 1], F32)
    c_es = nc.alloc_sbuf_tensor("c_es", [128, 1], F32)
    c_eb = nc.alloc_sbuf_tensor("c_eb", [128, 1], F32)
    s1b = nc.alloc_sbuf_tensor("s1b", [128, 1], F32)
    rs1b = nc.alloc_sbuf_tensor("rs1b", [128, 1], F32)
    s2b = nc.alloc_sbuf_tensor("s2b", [128, 1], F32)
    rs2b = nc.alloc_sbuf_tensor("rs2b", [128, 1], F32)
    s1acc = nc.alloc_sbuf_tensor("s1acc", [128, 1], F32)
    s2acc = nc.alloc_sbuf_tensor("s2acc", [128, 1], F32)
    sscr = nc.alloc_sbuf_tensor("sscr", [128, 8], F32)

    def newton_recip(vector, dst, dstn, src, srcn, tmp, tmpn):
        P.op(vector, "dve", [srcn], [dstn],
             lambda v: v.reciprocal(out=dst, in_=src))
        P.op(vector, "dve", [srcn, dstn], [tmpn],
             lambda v: v.scalar_tensor_tensor(out=tmp, in0=src, scalar=-1.0,
                                              in1=dst, op0=ALU.mult, op1=ALU.mult))
        P.op(vector, "dve", [tmpn], [tmpn],
             lambda v: v.tensor_scalar_add(out=tmp, in0=tmp, scalar1=2.0))
        P.op(vector, "dve", [tmpn, dstn], [dstn],
             lambda v: v.tensor_mul(out=dst, in0=dst, in1=tmp))

    MAGIC = 12582912.0

    def round_chain(vector, buf, bufn, tmp, tmpn, lo, hi, outap, outn):
        P.op(vector, "dve", [bufn], [tmpn],
             lambda v: v.tensor_scalar(out=tmp, in0=buf, scalar1=MAGIC, scalar2=lo,
                                       op0=ALU.subtract, op1=ALU.max))
        P.op(vector, "dve", [tmpn], [outn],
             lambda v: v.tensor_single_scalar(out=outap, in_=tmp, scalar=hi, op=ALU.min))

    with nc.Block() as blk:
        def _(g):
            for t, v, n in [(ones32, 1.0, "ones32"), (c_one, 1.0, "c_one"),
                            (c_lneps, 1e-5, "c_lneps"), (c_es, 0.125, "c_es"),
                            (c_eb, -MSTAB, "c_eb"), (s1acc, -1e30, "s1acc"),
                            (s2acc, -1e30, "s2acc")]:
                P.op(g, "pool", [], [n], lambda gg, t=t, v=v: gg.memset(t[:], v))
            P.op(g, "pool", ["ones32"], ["idn32r"],
                 lambda gg: gg.affine_select(out=idn32[:, :], in_=ones32[:, :],
                                             pattern=[[-1, 128]], compare_op=ALU.is_equal,
                                             fill=0.0, base=0, channel_multiplier=1))
            P.op(g, "pool", ["idn32r"], ["idn16r"],
                 lambda gg: gg.tensor_copy(out=idn16[:, :], in_=idn32[:, :]))
        blk.gpsimd(_)
    P.reset_sat()

    with contextlib.ExitStack() as es, nc.Block() as blk:
        NLS = 3
        xrow = [es.enter_context(nc.sbuf_tensor(f"xrow{i}", [128, D], F32)) for i in range(NLS)]
        sq = [es.enter_context(nc.sbuf_tensor(f"sq{i}", [128, D], F32)) for i in range(NLS)]
        xn = [es.enter_context(nc.sbuf_tensor(f"xn{i}", [128, D], F32)) for i in range(NLS)]
        st_ = [es.enter_context(nc.sbuf_tensor(f"lnst{i}", [128, 8], F32)) for i in range(NLS)]
        hi_sb = [es.enter_context(nc.sbuf_tensor(f"hisb{i}", [128, 512], BF16)) for i in range(NLS)]
        lo_sb = [es.enter_context(nc.sbuf_tensor(f"losb{i}", [128, 512], BF16)) for i in range(NLS)]
        ps_t = [es.enter_context(nc.psum_tensor(f"lnps{i}", [128, 512], F32)) for i in range(3)]

        spr = min(4, NDC)
        tiles = [(xb, xnTg_hi, xnTg_lo, i) for i in range(NGT)] + \
                [(xbo, xnTo_hi, xnTo_lo, i) for i in range(NOT)]
        for idx, (src, dhi, dlo, i) in enumerate(tiles):
            sl = idx % NLS

            def _(sync, sl=sl, src=src, i=i):
                P.op(sync, "sync", [], [f"xrow{sl}"],
                     lambda q: q.dma_start(out=xrow[sl][:], in_=src[i * 128:(i + 1) * 128, :]),
                     sem=QX[sl], inc=16)
            blk.sync(_)

            def _(vector, sl=sl):
                P.op(vector, "dve", [f"xrow{sl}"], [f"msum{sl}"],
                     lambda v: v.reduce_sum(out=st_[sl][:, 0:1], in_=xrow[sl][:, :], axis=AX.X))
            blk.vector(_)

            def _(scalar, sl=sl):
                P.op(scalar, "act", [f"xrow{sl}"], [f"sq{sl}", f"sumsq{sl}"],
                     lambda a: a.activation(out=sq[sl][:, :], in_=xrow[sl][:, :], func=AF.Square,
                                            accum_out=st_[sl][:, 1:2]))
            blk.scalar(_)

            def _(vector, sl=sl):
                P.op(vector, "dve", [f"msum{sl}"], [f"m{sl}"],
                     lambda v: v.tensor_scalar_mul(out=st_[sl][:, 2:3], in0=st_[sl][:, 0:1],
                                                   scalar1=1.0 / D))
                P.op(vector, "dve", [f"sumsq{sl}"], [f"ex2{sl}"],
                     lambda v: v.tensor_scalar_mul(out=st_[sl][:, 3:4], in0=st_[sl][:, 1:2],
                                                   scalar1=1.0 / D))
                P.op(vector, "dve", [f"m{sl}"], [f"m2{sl}"],
                     lambda v: v.tensor_mul(out=st_[sl][:, 4:5], in0=st_[sl][:, 2:3],
                                            in1=st_[sl][:, 2:3]))
                P.op(vector, "dve", [f"ex2{sl}", f"m2{sl}"], [f"var{sl}"],
                     lambda v: v.tensor_sub(out=st_[sl][:, 5:6], in0=st_[sl][:, 3:4],
                                            in1=st_[sl][:, 4:5]))
            blk.vector(_)

            def _(scalar, sl=sl):
                P.op(scalar, "act", [f"var{sl}", "c_lneps", "c_one"], [f"sd{sl}"],
                     lambda a: a.activation(out=st_[sl][:, 6:7], in_=st_[sl][:, 5:6], func=AF.Sqrt,
                                            bias=c_lneps[:, :], scale=c_one[:, :]))
            blk.scalar(_)

            def _(vector, sl=sl):
                P.op(vector, "dve", [f"sd{sl}"], [f"rstd{sl}"],
                     lambda v: v.reciprocal(out=st_[sl][:, 6:7], in_=st_[sl][:, 6:7]))
                P.op(vector, "dve", [f"xrow{sl}", f"m{sl}", f"rstd{sl}"], [f"xn{sl}"],
                     lambda v: v.tensor_scalar(out=xn[sl][:, :], in0=xrow[sl][:, :],
                                               scalar1=st_[sl][:, 2:3], scalar2=st_[sl][:, 6:7],
                                               op0=ALU.subtract, op1=ALU.mult))
            blk.vector(_)

            def _(vector, sl=sl):
                P.op(vector, "dve", [f"xn{sl}"], [f"tmax{sl}"],
                     lambda v: v.reduce_max(out=sscr[:, sl:sl + 1], in_=xn[sl][:, :], axis=AX.X))
                P.op(vector, "dve", [f"tmax{sl}", "s1acc"], ["s1acc"],
                     lambda v: v.tensor_max(out=s1acc[:, :], in0=s1acc[:, :],
                                            in1=sscr[:, sl:sl + 1]))
            blk.vector(_)

            for r in range((NDC + spr - 1) // spr):
                nsub = min(spr, NDC - r * spr)

                def _(tensor, sl=sl, r=r, nsub=nsub):
                    def e(te):
                        ii = None
                        for j in range(nsub):
                            dch = r * spr + j
                            ii = te.transpose(ps_t[sl][:, j * 128:(j + 1) * 128],
                                              xn[sl][:, dch * 128:(dch + 1) * 128], idn32[:, :])
                        return ii
                    P.op(tensor, "pe", [f"xn{sl}", "idn32r"], [f"ps_t{sl}"], e)
                blk.tensor(_)

                def _(scalar, sl=sl, nsub=nsub):
                    P.op(scalar, "act", [f"ps_t{sl}", "c_one"], [f"hi_sb{sl}"],
                         lambda a: a.activation(out=hi_sb[sl][:, 0:nsub * 128],
                                                in_=ps_t[sl][:, 0:nsub * 128],
                                                func=AF.Copy, scale=c_one[:, :], bias=0.0))
                blk.scalar(_)

                def _(vector, sl=sl, nsub=nsub):
                    P.op(vector, "dve", [f"ps_t{sl}", f"hi_sb{sl}"], [f"lo_sb{sl}"],
                         lambda v: v.tensor_sub(out=lo_sb[sl][:, 0:nsub * 128],
                                                in0=ps_t[sl][:, 0:nsub * 128],
                                                in1=hi_sb[sl][:, 0:nsub * 128]))
                blk.vector(_)

                def _(g, sl=sl, r=r, i=i, nsub=nsub, dhi=dhi, dlo=dlo):
                    dview_hi = dhi[r * spr * 128:(r * spr + nsub) * 128,
                                   i * 128:(i + 1) * 128].rearrange("(j p) c -> p j c", p=128)
                    dview_lo = dlo[r * spr * 128:(r * spr + nsub) * 128,
                                   i * 128:(i + 1) * 128].rearrange("(j p) c -> p j c", p=128)
                    P.op(g, "pool", [f"hi_sb{sl}"], ["xnt_dram"],
                         lambda q: q.dma_start(
                             out=dview_hi,
                             in_=hi_sb[sl][:, 0:nsub * 128].rearrange("p (j c) -> p j c", c=128)),
                         sem="gw_xnt", inc=16)
                    P.op(g, "pool", [f"lo_sb{sl}"], ["xnt_dram"],
                         lambda q: q.dma_start(
                             out=dview_lo,
                             in_=lo_sb[sl][:, 0:nsub * 128].rearrange("p (j c) -> p j c", c=128)),
                         sem="gw_xnt", inc=16)
                blk.gpsimd(_)

        def _(g):
            P.op(g, "pool", ["s1acc"], ["s1f"],
                 lambda q: q.tensor_reduce(out=sscr[0:1, 2:3], in_=s1acc[:, :], axis=AX.C,
                                           op=ALU.max))
            P.op(g, "pool", ["s1f"], ["cc_in"],
                 lambda q: q.dma_start(out=cc_in[:, :], in_=sscr[0:1, 2:3]), sem="gcc", inc=16)
            P.op(g, "pool", ["cc_in"], ["cc_out"],
                 lambda q: q.collective_compute("AllReduce", ALU.max, replica_groups=RG8,
                                                ins=[cc_in[:, :]], outs=[cc_out[:, :]]))
            P.op(g, "pool", ["cc_out"], ["s1g"],
                 lambda q: q.dma_start(out=sscr[0:1, 3:4], in_=cc_out[:, :]), sem="gcc", inc=16)
        blk.gpsimd(_)

        def _(vector):
            P.op(vector, "dve", ["s1g"], ["s1v"],
                 lambda v: v.tensor_scalar(out=sscr[0:1, 4:5], in0=sscr[0:1, 3:4], scalar1=1e-8,
                                           scalar2=1.0 / 255.0, op0=ALU.max, op1=ALU.mult))
            newton_recip(vector, sscr[0:1, 5:6], "rs1v", sscr[0:1, 4:5], "s1v",
                         sscr[0:1, 6:7], "ntmp")
        blk.vector(_)

        def _(scalar):
            P.op(scalar, "act", ["s1v", "rs1v"], ["scd01"],
                 lambda a: a.dma_start(out=scd[4:6], in_=sscr[0:1, 4:6]), sem="vr", inc=16)
        blk.scalar(_)

        def _(sync):
            P.op(sync, "sync", ["scd01"], ["s1b"],
                 lambda q: q.dma_start(out=s1b[:, :], in_=scd[None, 4:5].broadcast_to([128, 1])),
                 sem="qb", inc=16)
            P.op(sync, "sync", ["scd01"], ["rs1b"],
                 lambda q: q.dma_start(out=rs1b[:, :], in_=scd[None, 5:6].broadcast_to([128, 1])),
                 sem="qb", inc=16)
            sync.wait_ge(P.h["qb"], P.cnt["qb"])
            sync.wait_ge(P.h["gw_xnt"], P.cnt["gw_xnt"])
        blk.sync(_)
    P.reset_sat()

    NOW = OWNT // TW
    with contextlib.ExitStack() as es, nc.Block() as blk:
        xqT = es.enter_context(nc.sbuf_tensor("xqT", [128, NDC * OWNT], BF16))
        xh = [es.enter_context(nc.sbuf_tensor(f"ffxh{i}", [128, TW], BF16)) for i in range(2)]
        xl = [es.enter_context(nc.sbuf_tensor(f"ffxl{i}", [128, TW], BF16)) for i in range(2)]
        tq = [es.enter_context(nc.sbuf_tensor(f"fftq{i}", [128, TW], F32)) for i in range(2)]
        tq2 = [es.enter_context(nc.sbuf_tensor(f"fftq2{i}", [128, TW], F32)) for i in range(2)]
        w1raw = [es.enter_context(nc.sbuf_tensor(f"w1raw{i}", [128, D], F32)) for i in range(2)]
        w1q = [es.enter_context(nc.sbuf_tensor(f"w1q{i}", [128, D], BF16)) for i in range(2)]
        w1qT = [es.enter_context(nc.sbuf_tensor(f"w1qT{i}", [128, D], BF16)) for i in range(2)]
        h_sb = [es.enter_context(nc.sbuf_tensor(f"hsb{i}", [128, TW], F32)) for i in range(3)]
        b1sb = es.enter_context(nc.sbuf_tensor("b1sb", [128, NF1C], F32))
        wst = [es.enter_context(nc.sbuf_tensor(f"ffwst{i}", [128, 10], F32)) for i in range(2)]
        hmx = es.enter_context(nc.sbuf_tensor("hmx", [128, 2], F32))
        ps_h = [es.enter_context(nc.psum_tensor(f"psh{i}", [128, TW], F32)) for i in range(3)]
        ps_w = es.enter_context(nc.psum_tensor("psw", [128, 512], BF16))

        def _(sync):
            b1r = b1.rearrange("(c p) -> p c", p=128)
            for j in range(NF1C):
                P.op(sync, "sync", [], ["b1sb"],
                     lambda q, j=j: q.dma_start(out=b1sb[:, j:j + 1], in_=b1r[:, j:j + 1]),
                     sem="qb", inc=16)
        blk.sync(_)

        for it, (dch, g) in enumerate([(d, g) for d in range(NDC) for g in range(NOW)]):
            sl = it % 2

            def _(sync, sl=sl, dch=dch, g=g):
                t0 = g * TW
                P.op(sync, "sync", ["xnt_dram"], [f"ffxh{sl}"],
                     lambda q: q.dma_start(out=xh[sl][:, :],
                                           in_=xnTo_hi[dch * 128:(dch + 1) * 128, t0:t0 + TW]),
                     sem=QX[sl], inc=16)
                P.op(sync, "sync", ["xnt_dram"], [f"ffxl{sl}"],
                     lambda q: q.dma_start(out=xl[sl][:, :],
                                           in_=xnTo_lo[dch * 128:(dch + 1) * 128, t0:t0 + TW]),
                     sem=QX[2 + sl], inc=16)
            blk.sync(_)

            def _(vector, sl=sl, dch=dch, g=g):
                P.op(vector, "dve", [f"ffxh{sl}", f"ffxl{sl}"], [f"fftq{sl}"],
                     lambda v: v.tensor_add(out=tq[sl][:, :], in0=xh[sl][:, :], in1=xl[sl][:, :]))
                P.op(vector, "dve", [f"fftq{sl}", "rs1b"], [f"fftq{sl}"],
                     lambda v: v.tensor_scalar(out=tq[sl][:, :], in0=tq[sl][:, :],
                                               scalar1=rs1b[:, :], scalar2=MAGIC,
                                               op0=ALU.mult, op1=ALU.add))
                cd = dch * OWNT + g * TW
                round_chain(vector, tq[sl][:, :], f"fftq{sl}", tq2[sl][:, :], f"fftq2{sl}",
                            0.0, 255.0, xqT[:, cd:cd + TW], "xqT")
            blk.vector(_)

        for fc in range(NF1C):
            sl = fc % 2

            def _(sync, sl=sl, fc=fc):
                P.op(sync, "sync", [], [f"w1raw{sl}"],
                     lambda q: q.dma_start(out=w1raw[sl][:, :], in_=w1[fc * 128:(fc + 1) * 128, :]),
                     sem=QWSEM[sl], inc=16)
            blk.sync(_)

            def _(vector, sl=sl, fc=fc):
                W = wst[sl]
                P.op(vector, "dve", [f"w1raw{sl}"], [f"aw{sl}"],
                     lambda v: v.tensor_reduce(out=W[:, 0:1], in_=w1raw[sl][:, :], axis=AX.X,
                                               op=ALU.max, apply_absolute_value=True))
                P.op(vector, "dve", [f"aw{sl}"], [f"s1w{sl}"],
                     lambda v: v.tensor_scalar(out=W[:, 1:2], in0=W[:, 0:1], scalar1=1e-8,
                                               scalar2=1.0 / 127.0, op0=ALU.max, op1=ALU.mult))
                newton_recip(vector, W[:, 2:3], f"rw{sl}", W[:, 1:2], f"s1w{sl}",
                             W[:, 3:4], f"wtmp{sl}")
                P.op(vector, "dve", [f"w1raw{sl}", f"rw{sl}"], [f"w1raw{sl}"],
                     lambda v: v.tensor_scalar(out=w1raw[sl][:, :], in0=w1raw[sl][:, :],
                                               scalar1=W[:, 2:3], scalar2=MAGIC,
                                               op0=ALU.mult, op1=ALU.add))
                P.op(vector, "dve", [f"w1raw{sl}"], [f"w1raw{sl}"],
                     lambda v: v.tensor_scalar(out=w1raw[sl][:, :], in0=w1raw[sl][:, :],
                                               scalar1=MAGIC, scalar2=-128.0,
                                               op0=ALU.subtract, op1=ALU.max))
                P.op(vector, "dve", [f"w1raw{sl}"], [f"w1q{sl}"],
                     lambda v: v.tensor_single_scalar(out=w1q[sl][:, :], in_=w1raw[sl][:, :],
                                                      scalar=127.0, op=ALU.min))
                P.op(vector, "dve", [f"s1w{sl}", "s1b"], [f"ssw{sl}"],
                     lambda v: v.tensor_mul(out=W[:, 4:5], in0=W[:, 1:2], in1=s1b[:, :]))
                newton_recip(vector, W[:, 5:6], f"rssw{sl}", W[:, 4:5], f"ssw{sl}",
                             W[:, 3:4], f"wtmp{sl}")
                P.op(vector, "dve", ["b1sb", f"rssw{sl}"], [f"bq{sl}"],
                     lambda v: v.scalar_tensor_tensor(out=W[:, 6:7], in0=b1sb[:, fc:fc + 1],
                                                      scalar=1.0, in1=W[:, 5:6],
                                                      op0=ALU.mult, op1=ALU.mult))
                P.op(vector, "dve", [f"bq{sl}"], [f"bq{sl}"],
                     lambda v: v.tensor_scalar_add(out=W[:, 6:7], in0=W[:, 6:7], scalar1=MAGIC))
                P.op(vector, "dve", [f"bq{sl}"], [f"bq{sl}"],
                     lambda v: v.tensor_scalar_sub(out=W[:, 6:7], in0=W[:, 6:7], scalar1=MAGIC))
                P.op(vector, "dve", [f"bq{sl}", f"ssw{sl}"], [f"bqf{sl}"],
                     lambda v: v.tensor_mul(out=W[:, 8:9], in0=W[:, 6:7], in1=W[:, 4:5]))
            blk.vector(_)

            spr2 = min(4, NDC)
            for r in range((NDC + spr2 - 1) // spr2):
                nsub = min(spr2, NDC - r * spr2)

                def _(tensor, sl=sl, r=r, nsub=nsub):
                    def e(te):
                        ii = None
                        for j in range(nsub):
                            dch = r * spr2 + j
                            ii = te.transpose(ps_w[:, j * 128:(j + 1) * 128],
                                              w1q[sl][:, dch * 128:(dch + 1) * 128], idn16[:, :])
                        return ii
                    P.op(tensor, "pe", [f"w1q{sl}", "idn16r"], ["ps_w"], e)
                blk.tensor(_)

                def _(scalar, sl=sl, r=r, nsub=nsub):
                    P.op(scalar, "act", ["ps_w", "c_one"], [f"w1qT{sl}"],
                         lambda a: a.activation(out=w1qT[sl][:, r * spr2 * 128:(r * spr2 + nsub) * 128],
                                                in_=ps_w[:, 0:nsub * 128], func=AF.Copy,
                                                scale=c_one[:, :], bias=0.0))
                blk.scalar(_)

            for g in range(NOW):
                psl = (fc * NOW + g) % 3

                def _(tensor, sl=sl, g=g, psl=psl):
                    def e(te):
                        ii = None
                        for j in range(NDC):
                            ii = te.matmul(ps_h[psl][:, :], w1qT[sl][:, j * 128:(j + 1) * 128],
                                           xqT[:, j * OWNT + g * TW: j * OWNT + g * TW + TW],
                                           start=(j == 0), stop=(j == NDC - 1))
                        return ii
                    P.op(tensor, "pe", [f"w1qT{sl}", "xqT"], [f"ps_h{psl}"], e)
                blk.tensor(_)

                def _(scalar, sl=sl, psl=psl):
                    P.op(scalar, "act", [f"ps_h{psl}", f"ssw{sl}", f"bqf{sl}"], [f"h_sb{psl}"],
                         lambda a: a.activation(out=h_sb[psl][:, :], in_=ps_h[psl][:, :],
                                                func=AF.Relu, scale=wst[sl][:, 4:5],
                                                bias=wst[sl][:, 8:9]))
                blk.scalar(_)

                def _(vector, psl=psl):
                    P.op(vector, "dve", [f"h_sb{psl}"], ["hm0"],
                         lambda v: v.reduce_max(out=hmx[:, 0:1], in_=h_sb[psl][:, :], axis=AX.X))
                    P.op(vector, "dve", ["hm0", "s2acc"], ["s2acc"],
                         lambda v: v.tensor_max(out=s2acc[:, :], in0=s2acc[:, :], in1=hmx[:, 0:1]))
                blk.vector(_)

                def _(g_, fc=fc, g=g, psl=psl):
                    P.op(g_, "pool", [f"h_sb{psl}"], ["hscr"],
                         lambda q: q.dma_start(out=hscr[fc * 128:(fc + 1) * 128,
                                                        g * TW:(g + 1) * TW],
                                               in_=h_sb[psl][:, :]),
                         sem="gw_h", inc=16)
                blk.gpsimd(_)

        def _(g):
            P.op(g, "pool", ["s2acc"], ["s2f"],
                 lambda q: q.tensor_reduce(out=sscr[0:1, 2:3], in_=s2acc[:, :], axis=AX.C,
                                           op=ALU.max))
            P.op(g, "pool", ["s2f"], ["cc_in2"],
                 lambda q: q.dma_start(out=cc_in2[:, :], in_=sscr[0:1, 2:3]), sem="gcc", inc=16)
            P.op(g, "pool", ["cc_in2"], ["cc_out2"],
                 lambda q: q.collective_compute("AllReduce", ALU.max, replica_groups=RG8,
                                                ins=[cc_in2[:, :]], outs=[cc_out2[:, :]]))
            P.op(g, "pool", ["cc_out2"], ["s2g"],
                 lambda q: q.dma_start(out=sscr[0:1, 3:4], in_=cc_out2[:, :]), sem="gcc", inc=16)
        blk.gpsimd(_)

        def _(vector):
            P.op(vector, "dve", ["s2g"], ["s2v"],
                 lambda v: v.tensor_scalar(out=sscr[0:1, 4:5], in0=sscr[0:1, 3:4], scalar1=1e-8,
                                           scalar2=1.0 / 255.0, op0=ALU.max, op1=ALU.mult))
            newton_recip(vector, sscr[0:1, 5:6], "rs2v", sscr[0:1, 4:5], "s2v",
                         sscr[0:1, 6:7], "ntmp2")
        blk.vector(_)

        def _(scalar):
            P.op(scalar, "act", ["s2v", "rs2v"], ["scd23"],
                 lambda a: a.dma_start(out=scd[6:8], in_=sscr[0:1, 4:6]), sem="vr", inc=16)
        blk.scalar(_)

        def _(sync):
            P.op(sync, "sync", ["scd23"], ["s2b"],
                 lambda q: q.dma_start(out=s2b[:, :], in_=scd[None, 6:7].broadcast_to([128, 1])),
                 sem="qb", inc=16)
            P.op(sync, "sync", ["scd23"], ["rs2b"],
                 lambda q: q.dma_start(out=rs2b[:, :], in_=scd[None, 7:8].broadcast_to([128, 1])),
                 sem="qb", inc=16)
            sync.wait_ge(P.h["qb"], P.cnt["qb"])
            sync.wait_ge(P.h["gw_h"], P.cnt["gw_h"])
        blk.sync(_)
    P.reset_sat()

    WN = 512
    NW = T // WN
    NOWq = OWNT // WN
    with contextlib.ExitStack() as es, nc.Block() as blk:
        qstk = [es.enter_context(nc.sbuf_tensor(f"qstk{h}", [128, OWNT], BF16)) for h in range(2)]
        qswp = [es.enter_context(nc.sbuf_tensor(f"qswp{h}", [128, OWNT], BF16)) for h in range(2)]
        kstk = [es.enter_context(nc.sbuf_tensor(f"kstk{h}", [128, T], BF16)) for h in range(2)]
        vstk = [es.enter_context(nc.sbuf_tensor(f"vstk{h}", [128, T], BF16)) for h in range(2)]
        wsl = [es.enter_context(nc.sbuf_tensor(f"wsl{i}", [128, D], F32)) for i in range(2)]
        whi = es.enter_context(nc.sbuf_tensor("whi", [128, D], BF16))
        wlo = es.enter_context(nc.sbuf_tensor("wlo", [128, D], BF16))
        wThi = [es.enter_context(nc.sbuf_tensor(f"wThi{o}", [128, D], BF16)) for o in range(3)]
        wTlo = [es.enter_context(nc.sbuf_tensor(f"wTlo{o}", [128, D], BF16)) for o in range(3)]
        NJH = max(1, NDC // 2)
        xwh = [es.enter_context(nc.sbuf_tensor(f"xwh{i}", [128, NJH * WN], BF16)) for i in range(2)]
        xwl = [es.enter_context(nc.sbuf_tensor(f"xwl{i}", [128, NJH * WN], BF16)) for i in range(2)]
        vf32 = es.enter_context(nc.sbuf_tensor("vf32", [128, WN], F32))
        em32 = [es.enter_context(nc.sbuf_tensor(f"em32_{i}", [128, QW], F32)) for i in range(8)]
        ebuf = [es.enter_context(nc.sbuf_tensor(f"ebuf{i}", [128, 2 * QW], BF16)) for i in range(8)]
        maskbuf = es.enter_context(nc.sbuf_tensor("maskbuf", [128, max(1, c["NMASK"]) * QW], BF16))
        onesk = es.enter_context(nc.sbuf_tensor("onesk", [128, 1], BF16))
        rinv = es.enter_context(nc.sbuf_tensor("rinv", [1, QW], F32))
        rrep = es.enter_context(nc.sbuf_tensor("rrep", [64, QW], F32))
        ytmp = es.enter_context(nc.sbuf_tensor("ytmp", [64, QW], F32))
        ylo2 = es.enter_context(nc.sbuf_tensor("ylo2", [64, QW], F32))
        ynrm = es.enter_context(nc.sbuf_tensor("ynrm", [64, QW], F32))
        yhi = es.enter_context(nc.sbuf_tensor("yhi", [64, QW], BF16))
        ylo = es.enter_context(nc.sbuf_tensor("ylo", [64, QW], BF16))
        ps_s = [es.enter_context(nc.psum_tensor(f"pss{i}", [128, QW], F32)) for i in range(2)]
        ps_y = es.enter_context(nc.psum_tensor("psy", [128, QW], F32))
        ps_q = es.enter_context(nc.psum_tensor("psq", [128, WN], F32))
        ps_k = es.enter_context(nc.psum_tensor("psk", [128, WN], F32))
        ps_v = es.enter_context(nc.psum_tensor("psv", [128, WN], F32))
        ps_x = es.enter_context(nc.psum_tensor("psx", [128, 512], F32))
        ps_xw = es.enter_context(nc.psum_tensor("psxw", [128, 512], BF16))

        def _(vector):
            P.op(vector, "dve", [], ["onesk"], lambda v: v.memset(onesk[:], 1.0))
        blk.vector(_)

        def _(sync):
            for mi in range(c["NMASK"]):
                P.op(sync, "sync", [], ["maskbuf"],
                     lambda q, mi=mi: q.dma_start(out=maskbuf[:, mi * QW:(mi + 1) * QW],
                                                  in_=maskin[mi, :, :]),
                     sem="qb", inc=16)
        blk.sync(_)

        MINEF, MOFF = c["MINEF"], c["MOFF"]
        qmsems = ["qm0", "qm1", "qm2", "qm3"]
        def emit_wprep(ph):
            for o in range(3):
                wi = o % 2
                rows = o * D + ph * 128

                def _(sync, wi=wi, rows=rows):
                    P.op(sync, "sync", [], [f"wsl{wi}"],
                         lambda q: q.dma_start(out=wsl[wi][:, :], in_=wattn[rows:rows + 128, :]),
                         sem=QWSEM[wi], inc=16)
                blk.sync(_)

                def _(vector, wi=wi):
                    P.op(vector, "dve", [f"wsl{wi}"], ["whi"],
                         lambda v: v.tensor_copy(out=whi[:, :], in_=wsl[wi][:, :]))
                    P.op(vector, "dve", [f"wsl{wi}", "whi"], ["wlo"],
                         lambda v: v.tensor_sub(out=wlo[:, :], in0=wsl[wi][:, :], in1=whi[:, :]))
                blk.vector(_)

                spr3 = min(4, NDC)
                for srcb, srcn, dstb, dstn in [(whi, "whi", wThi[o], f"wThi{o}"),
                                               (wlo, "wlo", wTlo[o], f"wTlo{o}")]:
                    for r in range((NDC + spr3 - 1) // spr3):
                        nsub = min(spr3, NDC - r * spr3)

                        def _(tensor, srcb=srcb, srcn=srcn, r=r, nsub=nsub):
                            def e(te):
                                ii = None
                                for j in range(nsub):
                                    dch = r * spr3 + j
                                    ii = te.transpose(ps_xw[:, j * 128:(j + 1) * 128],
                                                      srcb[:, dch * 128:(dch + 1) * 128],
                                                      idn16[:, :])
                                return ii
                            P.op(tensor, "pe", [srcn, "idn16r"], ["ps_xw"], e)
                        blk.tensor(_)

                        def _(scalar, dstb=dstb, dstn=dstn, r=r, nsub=nsub):
                            P.op(scalar, "act", ["ps_xw", "c_one"], [dstn],
                                 lambda a: a.activation(
                                     out=dstb[:, r * spr3 * 128:(r * spr3 + nsub) * 128],
                                     in_=ps_xw[:, 0:nsub * 128], func=AF.Copy,
                                     scale=c_one[:, :], bias=0.0))
                        blk.scalar(_)

        NHALF = NDC // NJH
        for ph in range(NPAIR):
            if ph == 0:
                emit_wprep(0)
            for tt in range(NW):
                for hf in range(NHALF):
                    xsl = (tt * NHALF + hf) % 2

                    def _(sync, tt=tt, hf=hf, xsl=xsl):
                        dv_h = xnTg_hi[hf * NJH * 128:(hf + 1) * NJH * 128,
                                       tt * WN:(tt + 1) * WN].rearrange("(j p) c -> p j c", p=128)
                        dv_l = xnTg_lo[hf * NJH * 128:(hf + 1) * NJH * 128,
                                       tt * WN:(tt + 1) * WN].rearrange("(j p) c -> p j c", p=128)
                        P.op(sync, "sync", ["xnt_dram"], [f"xwh{xsl}"],
                             lambda q: q.dma_start(
                                 out=xwh[xsl][:, :].rearrange("p (j c) -> p j c", c=WN), in_=dv_h),
                             sem=QX[xsl], inc=16)
                        P.op(sync, "sync", ["xnt_dram"], [f"xwl{xsl}"],
                             lambda q: q.dma_start(
                                 out=xwl[xsl][:, :].rearrange("p (j c) -> p j c", c=WN), in_=dv_l),
                             sem=QX[2 + xsl], inc=16)
                    blk.sync(_)

                    def _(tensor, hf=hf, xsl=xsl):
                        for jj in range(NJH):
                            j = hf * NJH + jj
                            for o, psd, psn in [(1, ps_k, "ps_k"), (2, ps_v, "ps_v")]:
                                def e(te, o=o, psd=psd, j=j, jj=jj):
                                    te.matmul(psd[:, :], wThi[o][:, j * 128:(j + 1) * 128],
                                              xwh[xsl][:, jj * WN:(jj + 1) * WN],
                                              start=(j == 0), stop=False)
                                    te.matmul(psd[:, :], wThi[o][:, j * 128:(j + 1) * 128],
                                              xwl[xsl][:, jj * WN:(jj + 1) * WN],
                                              start=False, stop=False)
                                    return te.matmul(psd[:, :], wTlo[o][:, j * 128:(j + 1) * 128],
                                                     xwh[xsl][:, jj * WN:(jj + 1) * WN],
                                                     start=False, stop=(j == NDC - 1))
                                P.op(tensor, "pe",
                                     [f"wThi{o}", f"wTlo{o}", f"xwh{xsl}", f"xwl{xsl}"], [psn], e)
                    blk.tensor(_)

                def _(scalar, tt=tt):
                    def e(a):
                        for h in range(2):
                            a.activation(out=kstk[h][0:64, tt * WN:(tt + 1) * WN],
                                         in_=ps_k[h * 64:(h + 1) * 64, :],
                                         func=AF.Copy, scale=c_one[0:64, :], bias=0.0)
                        return a.activation(out=vf32[:, :], in_=ps_v[:, :],
                                            func=AF.Copy, scale=c_one[:, :], bias=0.0)
                    P.op(scalar, "act", ["ps_k", "ps_v", "c_one"], ["kstk0", "kstk1", "vf32"], e)
                blk.scalar(_)

                def _(vector, tt=tt):
                    for h in range(2):
                        P.op(vector, "dve", ["ps_k", f"kstk{h}"], [f"kstk{h}"],
                             lambda v, h=h: v.tensor_sub(
                                 out=kstk[h][64:128, tt * WN:(tt + 1) * WN],
                                 in0=ps_k[h * 64:(h + 1) * 64, :],
                                 in1=kstk[h][0:64, tt * WN:(tt + 1) * WN]))
                blk.vector(_)

                nst = WN // 128

                def _(tensor, tt=tt, nst=nst):
                    def e(te):
                        ii = None
                        for stq in range(nst):
                            vps = ps_x if stq % 2 == 0 else ps_s[0]
                            ii = te.transpose(vps[:, (stq // 2) * 128:(stq // 2) * 128 + 128],
                                              vf32[:, stq * 128:(stq + 1) * 128], idn32[:, :])
                        return ii
                    P.op(tensor, "pe", ["vf32", "idn32r"], ["ps_x", "ps_s0"], e)
                blk.tensor(_)

                def _(scalar, tt=tt, nst=nst):
                    def e(a):
                        ii = None
                        for stq in range(nst):
                            vps = ps_x if stq % 2 == 0 else ps_s[0]
                            kt = tt * nst + stq
                            for h in range(2):
                                ii = a.activation(out=vstk[h][:, kt * 128:kt * 128 + 64],
                                                  in_=vps[:, (stq // 2) * 128 + h * 64:(stq // 2) * 128 + (h + 1) * 64],
                                                  func=AF.Copy, scale=c_one[:, :], bias=0.0)
                        return ii
                    P.op(scalar, "act", ["ps_x", "ps_s0", "c_one"], ["vstk0", "vstk1"], e)
                blk.scalar(_)

                def _(vector, tt=tt, nst=nst):
                    for stq in range(nst):
                        vps = ps_x if stq % 2 == 0 else ps_s[0]
                        vpsn = "ps_x" if stq % 2 == 0 else "ps_s0"
                        kt = tt * nst + stq
                        for h in range(2):
                            P.op(vector, "dve", [vpsn, f"vstk{h}"], [f"vstk{h}"],
                                 lambda v, h=h, vps=vps, stq=stq, kt=kt: v.tensor_sub(
                                     out=vstk[h][:, kt * 128 + 64:kt * 128 + 128],
                                     in0=vps[:, (stq // 2) * 128 + h * 64:(stq // 2) * 128 + (h + 1) * 64],
                                     in1=vstk[h][:, kt * 128:kt * 128 + 64]))
                blk.vector(_)

            for tt in range(NOWq):
                for hf in range(NHALF):
                    xsl = (tt * NHALF + hf) % 2

                    def _(sync, tt=tt, hf=hf, xsl=xsl):
                        dv_h = xnTo_hi[hf * NJH * 128:(hf + 1) * NJH * 128,
                                       tt * WN:(tt + 1) * WN].rearrange("(j p) c -> p j c", p=128)
                        dv_l = xnTo_lo[hf * NJH * 128:(hf + 1) * NJH * 128,
                                       tt * WN:(tt + 1) * WN].rearrange("(j p) c -> p j c", p=128)
                        P.op(sync, "sync", ["xnt_dram"], [f"xwh{xsl}"],
                             lambda q: q.dma_start(
                                 out=xwh[xsl][:, :].rearrange("p (j c) -> p j c", c=WN), in_=dv_h),
                             sem=QX[xsl], inc=16)
                        P.op(sync, "sync", ["xnt_dram"], [f"xwl{xsl}"],
                             lambda q: q.dma_start(
                                 out=xwl[xsl][:, :].rearrange("p (j c) -> p j c", c=WN), in_=dv_l),
                             sem=QX[2 + xsl], inc=16)
                    blk.sync(_)

                    def _(tensor, hf=hf, xsl=xsl):
                        for jj in range(NJH):
                            j = hf * NJH + jj

                            def e(te, j=j, jj=jj):
                                te.matmul(ps_q[:, :], wThi[0][:, j * 128:(j + 1) * 128],
                                          xwh[xsl][:, jj * WN:(jj + 1) * WN],
                                          start=(j == 0), stop=False)
                                te.matmul(ps_q[:, :], wThi[0][:, j * 128:(j + 1) * 128],
                                          xwl[xsl][:, jj * WN:(jj + 1) * WN],
                                          start=False, stop=False)
                                return te.matmul(ps_q[:, :], wTlo[0][:, j * 128:(j + 1) * 128],
                                                 xwh[xsl][:, jj * WN:(jj + 1) * WN],
                                                 start=False, stop=(j == NDC - 1))
                            P.op(tensor, "pe", ["wThi0", "wTlo0", f"xwh{xsl}", f"xwl{xsl}"],
                                 ["ps_q"], e)
                    blk.tensor(_)

                def _(scalar, tt=tt):
                    def e(a):
                        ii = None
                        for h in range(2):
                            ii = a.activation(out=qstk[h][0:64, tt * WN:(tt + 1) * WN],
                                              in_=ps_q[h * 64:(h + 1) * 64, :],
                                              func=AF.Copy, scale=c_one[0:64, :], bias=0.0)
                        return ii
                    P.op(scalar, "act", ["ps_q", "c_one"], ["qstk0", "qstk1"], e)
                blk.scalar(_)

                def _(vector, tt=tt):
                    for h in range(2):
                        P.op(vector, "dve", ["ps_q", f"qstk{h}"], [f"qstk{h}"],
                             lambda v, h=h: v.tensor_sub(
                                 out=qstk[h][64:128, tt * WN:(tt + 1) * WN],
                                 in0=ps_q[h * 64:(h + 1) * 64, :],
                                 in1=qstk[h][0:64, tt * WN:(tt + 1) * WN]))
                blk.vector(_)

                def _(vector, tt=tt):
                    for h in range(2):
                        P.op(vector, "dve", [f"qstk{h}"], [f"qswp{h}"],
                             lambda v, h=h: v.tensor_copy(
                                 out=qswp[h][0:64, tt * WN:(tt + 1) * WN],
                                 in_=qstk[h][64:128, tt * WN:(tt + 1) * WN]))
                        P.op(vector, "dve", [f"qstk{h}"], [f"qswp{h}"],
                             lambda v, h=h: v.tensor_copy(
                                 out=qswp[h][64:128, tt * WN:(tt + 1) * WN],
                                 in_=qstk[h][0:64, tt * WN:(tt + 1) * WN]))
                blk.vector(_)

            if ph + 1 < NPAIR:
                emit_wprep(ph + 1)

            SPS = [ps_s[0], ps_s[1], ps_q, ps_k]
            SPSN = ["ps_s0", "ps_s1", "ps_q", "ps_k"]
            for h in range(2):
                head = ph * 2 + h
                for s in range(NSLOT):
                    E = EPROF[s]
                    qc = s * QW
                    mef, mof = MINEF[s], MOFF[s]

                    def emit_av(kts, h=h, E=E):
                        def _(tensor, kts=kts):
                            for kt in kts:
                                i = kt % 8

                                def e(te, kt=kt, i=i):
                                    te.matmul(ps_y[:, :], vstk[h][:, kt * 128:(kt + 1) * 128],
                                              ebuf[i][:, 0:QW], start=(kt == 0), stop=False)
                                    te.matmul(ps_y[:, :], vstk[h][:, kt * 128:(kt + 1) * 128],
                                              ebuf[i][:, QW:2 * QW], start=False, stop=(kt == E - 1))
                                    te.matmul(ps_v[0:1, 0:QW], onesk[:, :], ebuf[i][:, 0:QW],
                                              start=(kt == 0), stop=False)
                                    return te.matmul(ps_v[0:1, 0:QW], onesk[:, :],
                                                     ebuf[i][:, QW:2 * QW],
                                                     start=False, stop=(kt == E - 1))
                                P.op(tensor, "pe", [f"vstk{h}", f"ebuf{i}", "onesk"],
                                     ["ps_y", "ps_v"], e)
                        blk.tensor(_)

                    pending = None
                    for k0 in range(0, E, 4):
                        kts = list(range(k0, min(k0 + 4, E)))

                        def _(tensor, h=h, kts=kts, qc=qc):
                            for kt in kts:
                                i = kt % 4

                                def e(te, kt=kt, i=i):
                                    te.matmul(SPS[i][:, 0:QW], kstk[h][:, kt * 128:(kt + 1) * 128],
                                              qstk[h][:, qc:qc + QW], start=True, stop=False)
                                    return te.matmul(SPS[i][:, 0:QW], kstk[h][:, kt * 128:(kt + 1) * 128],
                                                     qswp[h][:, qc:qc + QW], start=False, stop=True)
                                P.op(tensor, "pe", [f"kstk{h}", f"qstk{h}", f"qswp{h}"], [SPSN[i]], e)
                        blk.tensor(_)

                        def _(scalar, kts=kts):
                            for kt in kts:
                                i = kt % 4
                                i8 = kt % 8
                                P.op(scalar, "act", [SPSN[i], "c_es", "c_eb"], [f"em32_{i8}"],
                                     lambda a, i=i, i8=i8: a.activation(out=em32[i8][:, :],
                                                                        in_=SPS[i][:, 0:QW],
                                                                        func=AF.Exp, scale=c_es[:, :],
                                                                        bias=c_eb[:, :]))
                        blk.scalar(_)

                        if any(kt >= mef for kt in kts):
                            def _(vector, kts=kts, mef=mef, mof=mof):
                                for kt in kts:
                                    if kt >= mef:
                                        i8 = kt % 8
                                        mi = mof + kt - mef
                                        P.op(vector, "dve", ["maskbuf", f"em32_{i8}"], [f"em32_{i8}"],
                                             lambda v, i8=i8, mi=mi: v.scalar_tensor_tensor(
                                                 out=em32[i8][:, :], in0=em32[i8][:, :], scalar=1e38,
                                                 in1=maskbuf[:, mi * QW:(mi + 1) * QW],
                                                 op0=ALU.min, op1=ALU.mult))
                            blk.vector(_)

                        def _(vector, kts=kts):
                            for kt in kts:
                                i8 = kt % 8
                                P.op(vector, "dve", [f"em32_{i8}"], [f"ebuf{i8}"],
                                     lambda v, i8=i8: v.tensor_copy(out=ebuf[i8][:, 0:QW],
                                                                    in_=em32[i8][:, :]))
                                P.op(vector, "dve", [f"em32_{i8}", f"ebuf{i8}"], [f"ebuf{i8}"],
                                     lambda v, i8=i8: v.tensor_sub(out=ebuf[i8][:, QW:2 * QW],
                                                                   in0=em32[i8][:, :],
                                                                   in1=ebuf[i8][:, 0:QW]))
                        blk.vector(_)

                        if pending is not None:
                            emit_av(pending)
                        pending = kts
                    emit_av(pending)

                    def _(vector):
                        P.op(vector, "dve", ["ps_v"], ["rinv"],
                             lambda v: v.reciprocal(out=rinv[:, :], in_=ps_v[0:1, 0:QW]))
                    blk.vector(_)

                    def _(scalar, head=head, s=s):
                        P.op(scalar, "act", ["rinv"], ["rbd"],
                             lambda a: a.dma_start(out=rbd[head * NSLOT + s, :], in_=rinv[0:1, :]),
                             sem="vr", inc=16)
                        P.op(scalar, "act", ["rbd"], ["rrep"],
                             lambda q: q.dma_start(
                                 out=rrep[:, :],
                                 in_=rbd[head * NSLOT + s:head * NSLOT + s + 1, :]
                                 .broadcast_to([64, QW])),
                             sem="rr0", inc=16)
                    blk.scalar(_)

                    def _(vector):
                        P.op(vector, "dve", ["ps_y"], ["ylo2"],
                             lambda v: v.tensor_copy(out=ylo2[:, :], in_=ps_y[64:128, :]))
                        P.op(vector, "dve", ["ps_y", "ylo2"], ["ytmp"],
                             lambda v: v.tensor_add(out=ytmp[:, :], in0=ps_y[0:64, :],
                                                    in1=ylo2[:, :]))
                        P.op(vector, "dve", ["ytmp", "rrep"], ["ynrm"],
                             lambda v: v.tensor_mul(out=ynrm[:, :], in0=ytmp[:, :], in1=rrep[:, :]))
                        P.op(vector, "dve", ["ynrm"], ["yhi"],
                             lambda v: v.tensor_copy(out=yhi[:, :], in_=ynrm[:, :]))
                        P.op(vector, "dve", ["ynrm", "yhi"], ["ylo"],
                             lambda v: v.tensor_sub(out=ylo[:, :], in0=ynrm[:, :], in1=yhi[:, :]))
                    blk.vector(_)

                    def _(g, head=head, s=s):
                        P.op(g, "pool", ["yhi"], ["yscr"],
                             lambda q: q.dma_start(
                                 out=yscr_hi[head * 64:(head + 1) * 64, s * QW:(s + 1) * QW],
                                 in_=yhi[:, :]), sem="gw_y", inc=16)
                        P.op(g, "pool", ["ylo"], ["yscr"],
                             lambda q: q.dma_start(
                                 out=yscr_lo[head * 64:(head + 1) * 64, s * QW:(s + 1) * QW],
                                 in_=ylo[:, :]), sem="gw_y", inc=16)
                    blk.gpsimd(_)

        def _(sync):
            sync.wait_ge(P.h["gw_y"], P.cnt["gw_y"])
        blk.sync(_)
    P.reset_sat()

    NST = TW // 128
    with contextlib.ExitStack() as es, nc.Block() as blk:
        wpr = [es.enter_context(nc.sbuf_tensor("wpr0", [128, D], F32))] * 2
        wph = es.enter_context(nc.sbuf_tensor("wph", [128, D], BF16))
        wpl = es.enter_context(nc.sbuf_tensor("wpl", [128, D], BF16))
        wpTh = es.enter_context(nc.sbuf_tensor("wpTh", [128, NDC * D], BF16))
        wpTl = es.enter_context(nc.sbuf_tensor("wpTl", [128, NDC * D], BF16))
        w2r = [es.enter_context(nc.sbuf_tensor("w2r0", [128, F1], F32))] * 2
        w2q = [es.enter_context(nc.sbuf_tensor(f"w2q{i}", [128, F1], BF16)) for i in range(2)]
        w2stg = es.enter_context(nc.sbuf_tensor("w2stg", [128, 512], BF16))
        w2row = [es.enter_context(nc.sbuf_tensor(f"w2row{i}", [128, NF1C * 128], BF16)) for i in range(2)]
        w2st = es.enter_context(nc.sbuf_tensor("w2st", [128, 8 * NDC], F32))
        b2sb = es.enter_context(nc.sbuf_tensor("b2sb", [128, NDC], F32))
        hq = es.enter_context(nc.sbuf_tensor("hq", [128, NF1C * TW], BF16))
        hwin = [es.enter_context(nc.sbuf_tensor(f"hwin{i}", [128, 2 * TW], F32)) for i in range(2)]
        tqa = [es.enter_context(nc.sbuf_tensor(f"tqa{i}", [128, 2 * TW], F32)) for i in range(2)]
        tqb = [es.enter_context(nc.sbuf_tensor(f"tqb{i}", [128, 2 * TW], F32)) for i in range(2)]
        ybh = es.enter_context(nc.sbuf_tensor("ybh", [128, NDC * TW], BF16))
        ybl = es.enter_context(nc.sbuf_tensor("ybl", [128, NDC * TW], BF16))
        sfm = es.enter_context(nc.sbuf_tensor("sfm", [128, TW], F32))
        prj = es.enter_context(nc.sbuf_tensor("prj", [128, TW], F32))
        outsb = es.enter_context(nc.sbuf_tensor("outsb", [128, NST * D], F32))
        xres = [es.enter_context(nc.sbuf_tensor(f"xres{i}", [128, D], F32)) for i in range(2)]
        outw = [es.enter_context(nc.sbuf_tensor(f"outw{i}", [128, D], F32)) for i in range(2)]
        ps_o1p = [es.enter_context(nc.psum_tensor(f"pso1_{i}", [128, TW], F32)) for i in range(2)]
        ps_o2p = [es.enter_context(nc.psum_tensor(f"pso2_{i}", [128, TW], F32)) for i in range(2)]
        ps_x2 = es.enter_context(nc.psum_tensor("psx2", [128, 512], BF16))
        ps_tt = es.enter_context(nc.psum_tensor("pstt", [128, 512], F32))

        def _(sync):
            b2r = b2.rearrange("(c p) -> p c", p=128)
            for j in range(NDC):
                P.op(sync, "sync", [], ["b2sb"],
                     lambda q, j=j: q.dma_start(out=b2sb[:, j:j + 1], in_=b2r[:, j:j + 1]),
                     sem="qb", inc=16)
        blk.sync(_)

        for dm in range(NDC):
            sl = dm % 2

            def _(sync, sl=sl, dm=dm):
                P.op(sync, "sync", [], ["wpr0"],
                     lambda q: q.dma_start(out=wpr[sl][:, :],
                                           in_=wproj[dm * 128:(dm + 1) * 128, :]),
                     sem=QWSEM[sl], inc=16)
            blk.sync(_)

            def _(vector, sl=sl):
                P.op(vector, "dve", ["wpr0"], ["wph"],
                     lambda v: v.tensor_copy(out=wph[:, :], in_=wpr[sl][:, :]))
                P.op(vector, "dve", ["wpr0", "wph"], ["wpl"],
                     lambda v: v.tensor_sub(out=wpl[:, :], in0=wpr[sl][:, :], in1=wph[:, :]))
            blk.vector(_)

            spr4 = min(4, NDC)
            for srcb, srcn, dstb, dstn in [(wph, "wph", wpTh, "wpTh"), (wpl, "wpl", wpTl, "wpTl")]:
                for r in range((NDC + spr4 - 1) // spr4):
                    nsub = min(spr4, NDC - r * spr4)

                    def _(tensor, srcb=srcb, srcn=srcn, r=r, nsub=nsub):
                        def e(te):
                            ii = None
                            for j in range(nsub):
                                fch = r * spr4 + j
                                ii = te.transpose(ps_x2[:, j * 128:(j + 1) * 128],
                                                  srcb[:, fch * 128:(fch + 1) * 128], idn16[:, :])
                            return ii
                        P.op(tensor, "pe", [srcn, "idn16r"], ["ps_x2"], e)
                    blk.tensor(_)

                    def _(scalar, dstb=dstb, dstn=dstn, dm=dm, r=r, nsub=nsub):
                        def e(a):
                            ii = None
                            for j in range(nsub):
                                fch = r * spr4 + j
                                cd = (fch * NDC + dm) * 128
                                ii = a.activation(out=dstb[:, cd:cd + 128],
                                                  in_=ps_x2[:, j * 128:(j + 1) * 128],
                                                  func=AF.Copy, scale=c_one[:, :], bias=0.0)
                            return ii
                        P.op(scalar, "act", ["ps_x2", "c_one"], [dstn], e)
                    blk.scalar(_)

        for dm in range(NDC):
            sl = dm % 2

            def _(sync, sl=sl, dm=dm):
                P.op(sync, "sync", [], ["w2r0"],
                     lambda q: q.dma_start(out=w2r[sl][:, :], in_=w2[dm * 128:(dm + 1) * 128, :]),
                     sem=QWSEM[sl], inc=16)
            blk.sync(_)

            def _(vector, sl=sl, dm=dm):
                W = w2st
                c0 = dm * 8
                P.op(vector, "dve", ["w2r0"], [f"w2aw{dm}"],
                     lambda v: v.tensor_reduce(out=W[:, c0:c0 + 1], in_=w2r[sl][:, :], axis=AX.X,
                                               op=ALU.max, apply_absolute_value=True))
                P.op(vector, "dve", [f"w2aw{dm}"], [f"w2sw{dm}"],
                     lambda v: v.tensor_scalar(out=W[:, c0 + 1:c0 + 2], in0=W[:, c0:c0 + 1],
                                               scalar1=1e-8, scalar2=1.0 / 127.0,
                                               op0=ALU.max, op1=ALU.mult))
                newton_recip(vector, W[:, c0 + 2:c0 + 3], f"w2rw{dm}", W[:, c0 + 1:c0 + 2],
                             f"w2sw{dm}", W[:, c0 + 3:c0 + 4], f"w2tmp{dm}")
                P.op(vector, "dve", ["w2r0", f"w2rw{dm}"], ["w2r0"],
                     lambda v: v.tensor_scalar(out=w2r[sl][:, :], in0=w2r[sl][:, :],
                                               scalar1=W[:, c0 + 2:c0 + 3], scalar2=MAGIC,
                                               op0=ALU.mult, op1=ALU.add))
                P.op(vector, "dve", ["w2r0"], ["w2r0"],
                     lambda v: v.tensor_scalar(out=w2r[sl][:, :], in0=w2r[sl][:, :],
                                               scalar1=MAGIC, scalar2=-128.0,
                                               op0=ALU.subtract, op1=ALU.max))
                P.op(vector, "dve", ["w2r0"], [f"w2q{sl}"],
                     lambda v: v.tensor_single_scalar(out=w2q[sl][:, :], in_=w2r[sl][:, :],
                                                      scalar=127.0, op=ALU.min))
                P.op(vector, "dve", [f"w2sw{dm}", "s2b"], [f"w2ssw{dm}"],
                     lambda v: v.tensor_mul(out=W[:, c0 + 4:c0 + 5], in0=W[:, c0 + 1:c0 + 2],
                                            in1=s2b[:, :]))
                newton_recip(vector, W[:, c0 + 5:c0 + 6], f"w2rssw{dm}", W[:, c0 + 4:c0 + 5],
                             f"w2ssw{dm}", W[:, c0 + 3:c0 + 4], f"w2tmp{dm}")
                P.op(vector, "dve", ["b2sb", f"w2rssw{dm}"], [f"w2bq{dm}"],
                     lambda v: v.scalar_tensor_tensor(out=W[:, c0 + 6:c0 + 7],
                                                      in0=b2sb[:, dm:dm + 1], scalar=1.0,
                                                      in1=W[:, c0 + 5:c0 + 6],
                                                      op0=ALU.mult, op1=ALU.mult))
                P.op(vector, "dve", [f"w2bq{dm}"], [f"w2bq{dm}"],
                     lambda v: v.tensor_scalar_add(out=W[:, c0 + 6:c0 + 7],
                                                   in0=W[:, c0 + 6:c0 + 7], scalar1=MAGIC))
                P.op(vector, "dve", [f"w2bq{dm}"], [f"w2bq{dm}"],
                     lambda v: v.tensor_scalar_sub(out=W[:, c0 + 6:c0 + 7],
                                                   in0=W[:, c0 + 6:c0 + 7], scalar1=MAGIC))
                P.op(vector, "dve", [f"w2bq{dm}", f"w2ssw{dm}"], [f"w2bqf{dm}"],
                     lambda v: v.tensor_mul(out=W[:, c0 + 6:c0 + 7], in0=W[:, c0 + 6:c0 + 7],
                                            in1=W[:, c0 + 4:c0 + 5]))
            blk.vector(_)

            spr5 = 4
            for r in range((NF1C + spr5 - 1) // spr5):
                nsub = min(spr5, NF1C - r * spr5)

                def _(tensor, sl=sl, r=r, nsub=nsub):
                    def e(te):
                        ii = None
                        for j in range(nsub):
                            fch = r * spr5 + j
                            ii = te.transpose(ps_x2[:, j * 128:(j + 1) * 128],
                                              w2q[sl][:, fch * 128:(fch + 1) * 128], idn16[:, :])
                        return ii
                    P.op(tensor, "pe", [f"w2q{sl}", "idn16r"], ["ps_x2"], e)
                blk.tensor(_)

                def _(scalar, r=r, nsub=nsub):
                    P.op(scalar, "act", ["ps_x2", "c_one"], ["w2stg"],
                         lambda a: a.activation(out=w2stg[:, 0:nsub * 128],
                                                in_=ps_x2[:, 0:nsub * 128],
                                                func=AF.Copy, scale=c_one[:, :], bias=0.0))
                blk.scalar(_)

                def _(g_, dm=dm, r=r, nsub=nsub):
                    for j in range(nsub):
                        fch = r * spr5 + j
                        P.op(g_, "pool", ["w2stg"], ["w2qTd"],
                             lambda q, j=j, fch=fch: q.dma_start(
                                 out=w2qTd[dm, :, fch * 128:(fch + 1) * 128],
                                 in_=w2stg[:, j * 128:(j + 1) * 128]),
                             sem="gw_y", inc=16)
                blk.gpsimd(_)

        for g in range(NOW):
            for fc0 in range(0, NF1C, 2):
                sl = (fc0 // 2) % 2

                def _(sync, sl=sl, fc0=fc0, g=g):
                    dv = hscr[fc0 * 128:(fc0 + 2) * 128,
                              g * TW:(g + 1) * TW].rearrange("(j p) c -> p j c", p=128)
                    P.op(sync, "sync", ["hscr"], [f"hwin{sl}"],
                         lambda q: q.dma_start(
                             out=hwin[sl][:, :].rearrange("p (j c) -> p j c", c=TW), in_=dv),
                         sem=QX[sl], inc=16)
                blk.sync(_)

                def _(vector, sl=sl, fc0=fc0):
                    P.op(vector, "dve", [f"hwin{sl}", "rs2b"], [f"tqa{sl}"],
                         lambda v: v.tensor_scalar(out=tqa[sl][:, :], in0=hwin[sl][:, :],
                                                   scalar1=rs2b[:, :], scalar2=MAGIC,
                                                   op0=ALU.mult, op1=ALU.add))
                    round_chain(vector, tqa[sl][:, :], f"tqa{sl}", tqb[sl][:, :], f"tqb{sl}",
                                0.0, 255.0, hq[:, fc0 * TW:(fc0 + 2) * TW], "hq")
                blk.vector(_)

            def _(sync, g=g):
                for fch in range(NDC):
                    P.op(sync, "sync", ["yscr"], ["ybh"],
                         lambda q, fch=fch: q.dma_start(
                             out=ybh[:, fch * TW:(fch + 1) * TW],
                             in_=yscr_hi[fch * 128:(fch + 1) * 128, g * TW:(g + 1) * TW]),
                         sem="qy0", inc=16)
                    P.op(sync, "sync", ["yscr"], ["ybl"],
                         lambda q, fch=fch: q.dma_start(
                             out=ybl[:, fch * TW:(fch + 1) * TW],
                             in_=yscr_lo[fch * 128:(fch + 1) * 128, g * TW:(g + 1) * TW]),
                         sem="qy1", inc=16)
            blk.sync(_)

            for dm in range(NDC):
                wsl2 = dm % 2
                po1, po1n = ps_o1p[dm % 2], f"ps_o1_{dm % 2}"
                po2, po2n = ps_o2p[dm % 2], f"ps_o2_{dm % 2}"

                def _(sync, g=g, dm=dm, wsl2=wsl2):
                    P.op(sync, "sync", ["w2qTd"], [f"w2row{wsl2}"],
                         lambda q: q.dma_start(out=w2row[wsl2][:, :], in_=w2qTd[dm, :, :]),
                         sem="qy0" if wsl2 == 0 else "qy1", inc=16)
                blk.sync(_)

                def _(tensor, g=g, dm=dm, wsl2=wsl2, po1=po1, po2=po2, po1n=po1n, po2n=po2n):
                    def e(te):
                        for fch in range(NDC):
                            cd = (fch * NDC + dm) * 128
                            te.matmul(po1[:, :], wpTh[:, cd:cd + 128],
                                      ybh[:, fch * TW:(fch + 1) * TW],
                                      start=(fch == 0), stop=False)
                            te.matmul(po1[:, :], wpTh[:, cd:cd + 128],
                                      ybl[:, fch * TW:(fch + 1) * TW], start=False, stop=False)
                            te.matmul(po1[:, :], wpTl[:, cd:cd + 128],
                                      ybh[:, fch * TW:(fch + 1) * TW],
                                      start=False, stop=(fch == NDC - 1))
                        ii = None
                        for fch in range(NF1C):
                            ii = te.matmul(po2[:, :], w2row[wsl2][:, fch * 128:(fch + 1) * 128],
                                           hq[:, fch * TW:(fch + 1) * TW],
                                           start=(fch == 0), stop=(fch == NF1C - 1))
                        return ii
                    P.op(tensor, "pe", ["wpTh", "wpTl", "ybh", "ybl", f"w2row{wsl2}", "hq"],
                         [po1n, po2n], e)
                blk.tensor(_)

                def _(scalar, dm=dm, po1=po1, po1n=po1n):
                    P.op(scalar, "act", [po1n, "c_one"], ["prj"],
                         lambda a: a.activation(out=prj[:, :], in_=po1[:, :],
                                                func=AF.Copy, scale=c_one[:, :], bias=0.0))
                blk.scalar(_)

                def _(vector, dm=dm, po2=po2, po2n=po2n):
                    c0 = dm * 8
                    P.op(vector, "dve", ["prj", po2n, f"w2ssw{dm}"], ["sfm"],
                         lambda v: v.scalar_tensor_tensor(out=sfm[:, :], in0=po2[:, :],
                                                          scalar=w2st[:, c0 + 4:c0 + 5],
                                                          in1=prj[:, :],
                                                          op0=ALU.mult, op1=ALU.add))
                    P.op(vector, "dve", ["sfm", f"w2bqf{dm}"], ["sfm"],
                         lambda v: v.tensor_scalar_add(out=sfm[:, :], in0=sfm[:, :],
                                                       scalar1=w2st[:, c0 + 6:c0 + 7]))
                blk.vector(_)

                def _(tensor, dm=dm):
                    def e(te):
                        ii = None
                        for st in range(NST):
                            ii = te.transpose(ps_tt[:, st * 128:(st + 1) * 128],
                                              sfm[:, st * 128:(st + 1) * 128], idn32[:, :])
                        return ii
                    P.op(tensor, "pe", ["sfm", "idn32r"], ["ps_tt"], e)
                blk.tensor(_)

                def _(scalar, dm=dm):
                    def e(a):
                        ii = None
                        for st in range(NST):
                            ii = a.activation(out=outsb[:, st * D + dm * 128: st * D + dm * 128 + 128],
                                              in_=ps_tt[:, st * 128:(st + 1) * 128],
                                              func=AF.Copy, scale=c_one[:, :], bias=0.0)
                        return ii
                    P.op(scalar, "act", ["ps_tt", "c_one"], ["outsb"], e)
                blk.scalar(_)

            for st in range(NST):
                rsl = (g * NST + st) % 2

                def _(sync, g=g, st=st, rsl=rsl):
                    row = g * TW + st * 128
                    P.op(sync, "sync", [], [f"xres{rsl}"],
                         lambda q: q.dma_start(out=xres[rsl][:, :], in_=xbo[row:row + 128, :]),
                         sem=QWSEM[rsl], inc=16)
                blk.sync(_)

                def _(vector, st=st, rsl=rsl):
                    P.op(vector, "dve", ["outsb", f"xres{rsl}"], [f"outw{rsl}"],
                         lambda v: v.tensor_add(out=outw[rsl][:, :],
                                                in0=outsb[:, st * D:(st + 1) * D],
                                                in1=xres[rsl][:, :]))
                blk.vector(_)

                def _(scalar, g=g, st=st, rsl=rsl):
                    row = g * TW + st * 128
                    P.op(scalar, "act", [f"outw{rsl}"], ["outd"],
                         lambda q: q.dma_start(out=out[row:row + 128, :], in_=outw[rsl][:, :]),
                         sem="oc", inc=16)
                blk.scalar(_)

        def _(sync):
            sync.wait_ge(P.h["oc"], P.cnt["oc"])
        blk.sync(_)

    return nc, c


_NC_CACHE = {}


def kernel(L0_attn_x, L0_attn_w_attn, L0_attn_w_proj, L0_ffn_w1, L0_ffn_b1,
           L0_ffn_w2, L0_ffn_b2):
    x = np.ascontiguousarray(L0_attn_x, dtype=np.float32)
    B, T, D = x.shape
    H = 16 if D == 1024 else D // 64
    F1 = L0_ffn_w1.shape[0]
    key = (T, D, H, F1)
    if key not in _NC_CACHE:
        _NC_CACHE[key] = build_nc(T, D, H, F1)
    nc, c = _NC_CACHE[key]
    CH, OWNT = c["CH"], c["OWNT"]

    wattn = np.ascontiguousarray(L0_attn_w_attn, dtype=np.float32)
    wproj = np.ascontiguousarray(L0_attn_w_proj, dtype=np.float32)
    w1 = np.ascontiguousarray(L0_ffn_w1, dtype=np.float32)
    b1 = np.ascontiguousarray(L0_ffn_b1, dtype=np.float32)
    w2 = np.ascontiguousarray(L0_ffn_w2, dtype=np.float32)
    b2 = np.ascontiguousarray(L0_ffn_b2, dtype=np.float32)
    masks = {p: _make_masks(c, p).astype(ml_dtypes.bfloat16) for p in (0, 1)}

    in_maps = []
    for core in range(8):
        b, p = core // 2, core % 2
        xbc = np.ascontiguousarray(x[b])
        own_rows = np.concatenate([np.arange(ch * CH, (ch + 1) * CH) for ch in c["chunks"][p]])
        xbo = np.ascontiguousarray(xbc[own_rows])
        in_maps.append({
            "xb": xbc, "xbo": xbo, "wattn": wattn, "wproj": wproj,
            "w1": w1, "b1": b1, "w2": w2, "b2": b2,
            "maskin": masks[p],
        })

    res = run_bass_kernel_spmd(nc, in_maps, core_ids=list(range(8)))
    outf = np.empty((B, T, D), dtype=np.float32)
    for core in range(8):
        b, p = core // 2, core % 2
        o = np.asarray(res.results[core]["out"])
        for k, ch in enumerate(c["chunks"][p]):
            outf[b, ch * CH:(ch + 1) * CH] = o[k * CH:(k + 1) * CH]
    return outf
